# version 02
# baseline (speedup 1.0000x reference)
import contextlib
import numpy as np
import ml_dtypes
import concourse.bass as bass
import concourse.mybir as mybir
from concourse.bass_utils import run_bass_kernel_spmd

F32 = mybir.dt.float32
BF16 = mybir.dt.bfloat16
FP16 = mybir.dt.float16
U8 = mybir.dt.uint8
AF = mybir.ActivationFunctionType
ALU = mybir.AluOpType

NCORES = 8
T = 8192
TC = T // NCORES
D = 2048
KC = D // 128
BFD = np.dtype(ml_dtypes.bfloat16)

CV_EPS_LN = 0
CV_NHALF = 1
CV_INV_D = 2
CV_KVSCALE = 3
CV_EPS_RMS = 4
CV_INV_QR = 5
CV_INV_KVR = 6
CV_ATTNSCALE = 7
NCONST = 8
CV_VALS = np.array([[1e-5, -0.5, 1.0 / 2048.0, 1.0 / np.sqrt(128.0),
                     float(np.finfo(np.float32).eps), 1.0 / 768.0,
                     1.0 / 512.0, 128.0 ** -0.5]], dtype=np.float32)


class Sem:
    __slots__ = ("h", "n", "name")

    def __init__(self, h, name):
        self.h = h
        self.n = 0
        self.name = name


class Plan:

    ENGINES = ("sync", "tensor", "vector", "scalar", "gpsimd")

    def __init__(self, nc, es):
        self.nc = nc
        self.es = es
        self.ops = {e: [] for e in self.ENGINES}

    def sem(self, name):
        return Sem(self.es.enter_context(self.nc.semaphore(name)), name)

    def emit(self, eng, fn):
        self.ops[eng].append(fn)

    def wait(self, eng, sem, val=None):
        v = sem.n if val is None else val
        if v > 0:
            self.emit(eng, lambda e, s=sem.h, v=v: e.wait_ge(s, v))

    def dma(self, eng, out, in_, sem=None, **kw):
        tgt = None
        if sem is not None:
            sem.n += 16
            tgt = sem.n
        def fn(e, out=out, in_=in_, sem=sem, kw=kw):
            ins = e.dma_start(out=out, in_=in_, **kw)
            if sem is not None:
                ins.then_inc(sem.h, 16)
        self.emit(eng, fn)
        return tgt

    def op(self, eng, fn, sem=None, inc=1):
        tgt = None
        if sem is not None:
            sem.n += inc
            tgt = sem.n
        def wrapped(e, fn=fn, sem=sem, inc=inc):
            ins = fn(e)
            if sem is not None:
                ins.then_inc(sem.h, inc)
        self.emit(eng, wrapped)
        return tgt

    def build(self):
        with self.nc.Block() as block:
            for eng_name in self.ENGINES:
                ops = self.ops[eng_name]
                if not ops:
                    continue
                def body(e, ops=ops):
                    for fn in ops:
                        fn(e)
                getattr(block, eng_name)(body)



def bank(B, i, parts=128, width=512):
    return B.pool[0:parts, i * 512:i * 512 + width]


def chunked(dram_ap):
    return dram_ap.rearrange("(k p) t -> p k t", p=128)


def sb3(sbuf_ap, n):
    return sbuf_ap.rearrange("p (k t) -> p k t", k=n)


def emit_ln_stats(P, B, src, th, n_chunks, inv_n, out_m, out_r,
                  eps_col, cast_first):
    nc = P.nc
    sA, sT, sV = B.s_act, B.s_mm, B.s_dve
    want_m = out_m is not None
    for k in range(n_chunks):
        slot = k % 2
        a = B.scr_a[:, slot * 512:(slot + 1) * 512]
        b = B.scr_b[:, slot * 512:(slot + 1) * 512]
        guard = B.g.get(("lnmm", slot), 0)
        if cast_first:
            P.wait("vector", sT, guard)
            P.op("vector", lambda e, a=a, s=src(k): e.tensor_copy(a, s), sV)
            cast_tgt = sV.n
            xslot = B.g.pop("_cur_xs_slot", None)
            if xslot is not None:
                B.g[("xs", xslot)] = cast_tgt
            P.wait("scalar", sV, cast_tgt)
            stat_in = a
        else:
            stat_in = src(k)
            P.wait("scalar", sT, guard)
        P.op("scalar", lambda e, b=b, s=stat_in: e.activation(b, s, AF.Square), sA)
        sq_tgt = sA.n
        if cast_first:
            P.wait("tensor", sV, cast_tgt)
        st = (k == 0)
        sp = (k == n_chunks - 1)
        if st:
            P.wait("tensor", sV, B.g.get("ln_stats_free", 0))
        if want_m:
            P.op("tensor", lambda e, s=stat_in, st=st, sp=sp:
                 e.matmul(bank(B, 0, parts=1), B.ones_col[:, :], s, start=st, stop=sp))
        P.wait("tensor", sA, sq_tgt)
        P.op("tensor", lambda e, b=b, st=st, sp=sp:
             e.matmul(bank(B, 1, parts=1), B.ones_col[:, :], b, start=st, stop=sp), sT)
        B.g[("lnmm", slot)] = sT.n
    mm_done = sT.n
    ts = th * 512
    m_row = B.m_row[0:1, ts:ts + 512]
    v_row = B.v_row[0:1, ts:ts + 512]
    r_row = B.r_row[0:1, ts:ts + 512]
    m_bf = B.m_bf[0:1, ts:ts + 512]
    r_bf = B.r_bf[0:1, ts:ts + 512]
    P.wait("vector", sT, mm_done)
    P.wait("vector", sT, B.g.get(("row_free", th), 0))
    if want_m:
        P.op("vector", lambda e, m=m_row: e.tensor_scalar(
            m, bank(B, 0, parts=1), inv_n, None, ALU.mult), sV)
        P.op("vector", lambda e, m=m_row, v=v_row: e.tensor_mul(v, m, m), sV)
        P.op("vector", lambda e, v=v_row: e.scalar_tensor_tensor(
            v, bank(B, 1, parts=1), inv_n, v, ALU.mult, ALU.subtract), sV)
    else:
        P.op("vector", lambda e, v=v_row: e.tensor_scalar(
            v, bank(B, 1, parts=1), inv_n, None, ALU.mult), sV)
    B.g["ln_stats_free"] = sV.n
    P.wait("scalar", sV, sV.n)
    P.op("scalar", lambda e, r=r_row, v=v_row, b=eps_col:
         e.activation(r, v, AF.Ln, bias=b), sA)
    P.op("scalar", lambda e, r=r_row: e.activation(
        r, r, AF.Exp, scale=B.cv[0:1, CV_NHALF:CV_NHALF + 1]), sA)
    P.wait("vector", sA, sA.n)
    if want_m:
        P.op("vector", lambda e, m=m_row, mb=m_bf: e.tensor_copy(mb, m), sV)
    P.op("vector", lambda e, r=r_row, rb=r_bf: e.tensor_copy(rb, r), sV)
    rows_done = sV.n
    P.wait("tensor", sV, rows_done)
    P.wait("tensor", sV, B.g.get("ln_bc_free", 0))
    if want_m:
        P.op("tensor", lambda e, mb=m_bf: e.matmul(
            bank(B, 2), B.ones_row[0:1, 0:128], mb, start=True, stop=True))
    P.op("tensor", lambda e, rb=r_bf: e.matmul(
        bank(B, 3), B.ones_row[0:1, 0:128], rb, start=True, stop=True), sT)
    bc_done = sT.n
    B.g[("row_free", th)] = bc_done
    P.wait("vector", sT, bc_done)
    if want_m:
        P.op("vector", lambda e, o=out_m: e.tensor_copy(o, bank(B, 2)), sV)
    P.op("vector", lambda e, o=out_r: e.tensor_copy(o, bank(B, 3)), sV)
    B.g["ln_bc_free"] = sV.n
    return sV.n


def emit_proj(P, B, n_og, wget, rhs_get, out_put, n_k, kstride, bias_col=None,
              scale_col=None, act=None, corr=None, og_cols=512, rhs_parts=128):
    act = act or AF.Identity
    n_oc = og_cols // 128
    P.wait("gpsimd", B.s_mm, B.s_mm.n)
    tgts = {0: wget(0, 0)}
    if n_og > 1:
        tgts[1] = wget(1, 1)
    for og in range(n_og):
        slot = og % 2
        P.wait("tensor", B.s_w[og % 2], tgts[og])
        for oc in range(n_oc):
            o = og * n_oc + oc
            for th in range(2):
                bk = (oc * 2 + th) % 2
                pm = bank(B, 4 + bk)
                prev = B.g.get(("proj_evac", bk), 0)
                P.wait("tensor", B.s_act, prev)
                mms = []
                ss = B.slab_stride
                for k in range(n_k):
                    lhsT = B.wslab[:, slot * ss + k * kstride + oc * 128:
                                   slot * ss + k * kstride + oc * 128 + 128]
                    mms.append((lhsT, rhs_get(k, th)))
                if corr is not None:
                    mms.extend(corr(o, th))
                for i, (l, r) in enumerate(mms):
                    last = (i == len(mms) - 1)
                    P.op("tensor", lambda e, p=pm, l=l, r=r, st=(i == 0), sp=last:
                         e.matmul(p, l, r, start=st, stop=sp),
                         B.s_mm if last else None)
                mm_tgt = B.s_mm.n
                P.wait("scalar", B.s_mm, mm_tgt)
                kw = {}
                if bias_col is not None:
                    kw["bias"] = bias_col(o)
                if scale_col is not None:
                    kw["scale"] = scale_col(o)
                P.op("scalar", lambda e, d=out_put(o, th), p=pm, kw=kw:
                     e.activation(d, p, act, **kw), B.s_act)
                B.g[("proj_evac", bk)] = B.s_act.n
        if og + 2 < n_og:
            P.wait("gpsimd", B.s_mm, B.s_mm.n)
            tgts[og + 2] = wget(og + 2, slot)
    return B.s_act.n


def emit_res_ln(P, B, ybuf_get, x_get, n_chunks, eps_col, inv_n):
    for th in range(2):
        emit_ln_stats(P, B, lambda k, th=th: ybuf_get(k, th), th, n_chunks,
                      inv_n, B.m_b[:, th * 512:(th + 1) * 512],
                      B.r_b[:, th * 512:(th + 1) * 512], eps_col, cast_first=False)
        for k in range(n_chunks):
            y_ap = ybuf_get(k, th)
            x_ap = x_get(k, th)
            m_ap = B.m_b[:, th * 512:(th + 1) * 512]
            r_ap = B.r_b[:, th * 512:(th + 1) * 512]
            P.op("vector", lambda e, y=y_ap, m=m_ap: e.tensor_sub(B.t1[:, :], y, m))
            P.op("vector", lambda e, r=r_ap: e.tensor_mul(B.t1[:, :], B.t1[:, :], r))
            P.op("vector", lambda e, x=x_ap: e.tensor_add(x, x, B.t1[:, :]), B.s_dve)
    return B.s_dve.n


def emit_inner_ln(P, B, x_get, xn_put, n_chunks, eps_col, inv_n):
    done = []
    for th in range(2):
        emit_ln_stats(P, B, lambda k, th=th: x_get(k, th), th, n_chunks,
                      inv_n, B.m_b[:, th * 512:(th + 1) * 512],
                      B.r_b[:, th * 512:(th + 1) * 512], eps_col, cast_first=True)
        for k in range(n_chunks):
            x_ap = x_get(k, th)
            xn_ap = xn_put(k, th)
            m_ap = B.m_b[:, th * 512:(th + 1) * 512]
            r_ap = B.r_b[:, th * 512:(th + 1) * 512]
            P.op("vector", lambda e, x=x_ap, m=m_ap: e.tensor_sub(B.t1[:, :], x, m))
            P.op("vector", lambda e, o=xn_ap, r=r_ap:
                 e.tensor_mul(o, B.t1[:, :], r), B.s_dve)
        done.append(B.s_dve.n)
    return done


def make_common(P, nc, es, B):
    sb = lambda name, shape, dt: es.enter_context(nc.sbuf_tensor(name, shape, dt))
    B.g = {}
    B.pool = es.enter_context(nc.psum_tensor("pool", [128, 4096], F32))
    B.scr_a = sb("scr_a", [128, 1024], BF16)
    B.scr_b = sb("scr_b", [128, 1024], BF16)
    B.t1 = sb("t1", [128, 512], F32)
    B.m_b = sb("m_b", [128, 1024], F32)
    B.r_b = sb("r_b", [128, 1024], F32)
    B.m_row = sb("m_row", [1, 1024], F32)
    B.v_row = sb("v_row", [1, 1024], F32)
    B.r_row = sb("r_row", [1, 1024], F32)
    B.m_bf = sb("m_bf", [1, 1024], BF16)
    B.r_bf = sb("r_bf", [1, 1024], BF16)
    B.ones_col = sb("ones_col", [128, 1], BF16)
    B.ones_row = sb("ones_row", [1, 512], BF16)
    B.cv = sb("cv_sb", [1, NCONST], F32)
    B.s_x = P.sem("s_x")
    B.s_w = [P.sem("s_w0"), P.sem("s_w1")]
    B.s_dve = P.sem("s_dve")
    B.s_act = P.sem("s_act")
    B.s_mm = P.sem("s_mm")
    B.s_cc = P.sem("s_cc")
    B.s_st = P.sem("s_st")
    P.op("vector", lambda e: e.memset(B.ones_col[:, :], 1.0))
    P.op("vector", lambda e: e.memset(B.ones_row[:, :], 1.0), B.s_dve)
    B.init_done = B.s_dve.n
    return B.init_done



def emit_l0_attn(P, B, nc, xin, wqT, wkT, wvT, woT, bq_c, bo_c, bk_r, bv_r,
                 cv_in, xout, kv_cc_in, kv_cc_out):
    es2 = contextlib.ExitStack()
    if True:
        sb = lambda name, shape, dt: es2.enter_context(
            nc.sbuf_tensor("A_" + name, shape, dt))
        init_done = B.init_done

        B.xres = sb("xres", [128, KC * TC], F32)
        B.xn = sb("xn", [128, KC * TC], BF16)
        B.qb = sb("qb", [128, KC * TC], BF16)
        B.wslab = sb("wslab", [128, 2 * 4096], BF16)
        B.slab_stride = 4096
        B.kst = sb("kst", [128, 2 * 128], BF16)
        B.vst = sb("vst", [128, 2 * 128], BF16)
        B.kvacc = sb("kvacc", [128, D], F32)
        B.kvb = sb("kvb", [128, D], BF16)
        B.bqc = sb("bqc", [128, KC], F32)
        B.boc = sb("boc", [128, KC], F32)
        B.bkr = sb("bkr", [1, D], BF16)
        B.bvr = sb("bvr", [1, D], BF16)

        P.dma("sync", sb3(B.xres[:, :], KC), chunked(xin), B.s_x)
        P.dma("sync", B.cv[:, :], cv_in, B.s_x)
        P.dma("sync", B.bqc[:, :], bq_c, B.s_x)
        P.dma("sync", B.boc[:, :], bo_c, B.s_x)
        P.dma("gpsimd", B.bkr[:, :], bk_r, B.s_x)
        P.dma("gpsimd", B.bvr[:, :], bv_r, B.s_x)
        x_loaded = B.s_x.n

        eps_ln = B.cv[0:1, CV_EPS_LN:CV_EPS_LN + 1]
        inv_d = 1.0 / 2048.0

        def slab_load(w, og, slot, cols=256):
            src = chunked(w[:, og * cols:(og + 1) * cols])
            return P.dma("gpsimd",
                         sb3(B.wslab[:, slot * 4096: slot * 4096 + KC * cols], KC),
                         src, B.s_w[slot])

        P.wait("gpsimd", B.s_x, x_loaded)

        P.wait("vector", B.s_x, x_loaded)
        P.wait("scalar", B.s_x, x_loaded)
        P.wait("tensor", B.s_dve, init_done)
        P.wait("tensor", B.s_x, x_loaded)

        def x_get(k, th):
            return B.xres[:, k * TC + th * 512: k * TC + (th + 1) * 512]

        def xn_put(k, th):
            return B.xn[:, k * TC + th * 512: k * TC + (th + 1) * 512]

        xn_done = emit_inner_ln(P, B, x_get, xn_put, KC, eps_ln, inv_d)

        P.wait("tensor", B.s_dve, xn_done[1])

        def q_out(o, th):
            return B.qb[:, o * TC + th * 512: o * TC + (th + 1) * 512]

        emit_proj(P, B, 8, lambda og, slot: slab_load(wqT, og, slot),
                  lambda k, th: xn_put(k, th), q_out, KC, 256,
                  bias_col=lambda o: B.bqc[:, o:o + 1], og_cols=256)
        q_all = B.s_act.n

        def kv_slab_load(dg, slot):
            P.dma("gpsimd", sb3(B.wslab[:, slot * 4096: slot * 4096 + KC * 128], KC),
                  chunked(wkT[:, dg * 128:(dg + 1) * 128]), B.s_w[slot])
            return P.dma("gpsimd",
                         sb3(B.wslab[:, slot * 4096 + 2048:
                                     slot * 4096 + 2048 + KC * 128], KC),
                         chunked(wvT[:, dg * 128:(dg + 1) * 128]), B.s_w[slot])

        P.wait("gpsimd", B.s_act, q_all)
        kvt = {0: kv_slab_load(0, 0), 1: kv_slab_load(1, 1)}
        P.wait("tensor", B.s_act, q_all)
        for dg in range(16):
            slot = dg % 2
            P.wait("tensor", B.s_w[dg % 2], kvt[dg])
            P.wait("tensor", B.s_dve, B.g.get("kva_evac", 0))
            for s in range(8):
                sbk = s % 2
                for which, base, pm, dst, brow, gkey in (
                    ("k", 0, bank(B, 4 + sbk, width=128),
                     B.kst[:, sbk * 128:(sbk + 1) * 128], B.bkr, ("kprod", sbk)),
                    ("v", 2048, bank(B, 2 + sbk, width=128),
                     B.vst[:, sbk * 128:(sbk + 1) * 128], B.bvr, ("vprod", sbk)),
                ):
                    P.wait("tensor", B.s_act, B.g.get(gkey, 0))
                    for k in range(KC):
                        lhsT = B.xn[:, k * TC + s * 128: k * TC + s * 128 + 128]
                        rhs = B.wslab[:, slot * 4096 + base + k * 128:
                                      slot * 4096 + base + k * 128 + 128]
                        P.op("tensor", lambda e, p=pm, l=lhsT, r=rhs, st=(k == 0):
                             e.matmul(p, l, r, start=st, stop=False))
                    P.op("tensor", lambda e, p=pm, b=brow, dg=dg: e.matmul(
                        p, B.ones_row[0:1, 0:128], b[0:1, dg * 128:(dg + 1) * 128],
                        start=False, stop=True), B.s_mm)
                    mm_tgt = B.s_mm.n
                    P.wait("scalar", B.s_mm, mm_tgt)
                    P.op("scalar", lambda e, d=dst, p=pm:
                         e.activation(d, p, AF.Identity), B.s_act)
                    B.g[gkey] = B.s_act.n
                P.wait("tensor", B.s_act, B.g[("vprod", sbk)])
                lhsT = B.kst[:, sbk * 128:(sbk + 1) * 128]
                rhs = B.vst[:, sbk * 128:(sbk + 1) * 128]
                pm = B.pool[:, 6 * 512: 6 * 512 + 128]
                P.op("tensor", lambda e, p=pm, l=lhsT, r=rhs, st=(s == 0),
                     sp=(s == 7): e.matmul(p, l, r, start=st, stop=sp),
                     B.s_mm if s == 7 else None)
            P.wait("vector", B.s_mm, B.s_mm.n)
            P.op("vector", lambda e, dg=dg: e.tensor_copy(
                B.kvacc[:, dg * 128:(dg + 1) * 128], bank(B, 6, width=128)), B.s_dve)
            B.g["kva_evac"] = B.s_dve.n
            if dg + 2 < 16:
                P.wait("gpsimd", B.s_mm, B.s_mm.n)
                kvt[dg + 2] = kv_slab_load(dg + 2, slot)

        P.wait("sync", B.s_dve, B.g["kva_evac"])
        P.dma("sync", kv_cc_in, B.kvacc[:, :], B.s_st)
        P.wait("gpsimd", B.s_st, B.s_st.n)
        P.op("gpsimd", lambda e: e.collective_compute(
            "AllReduce", ALU.add, replica_groups=[list(range(NCORES))],
            ins=[kv_cc_in], outs=[kv_cc_out]), B.s_cc)
        P.wait("sync", B.s_cc, B.s_cc.n)
        P.dma("sync", B.kvacc[:, :], kv_cc_out, B.s_x)
        P.wait("vector", B.s_x, B.s_x.n)
        P.op("vector", lambda e: e.tensor_scalar(
            B.kvb[:, :], B.kvacc[:, :], float(128.0 ** -0.5),
            None, ALU.mult), B.s_dve)
        kvb_done = B.s_dve.n

        P.wait("tensor", B.s_dve, kvb_done)
        for h in range(16):
            for th in range(2):
                bk = (h * 2 + th) % 2
                pm = bank(B, 4 + bk)
                P.wait("tensor", B.s_act, B.g.get(("attn", bk), 0))
                lhsT = B.kvb[:, h * 128:(h + 1) * 128]
                rhs = B.qb[:, h * TC + th * 512: h * TC + (th + 1) * 512]
                P.op("tensor", lambda e, p=pm, l=lhsT, r=rhs:
                     e.matmul(p, l, r, start=True, stop=True), B.s_mm)
                P.wait("scalar", B.s_mm, B.s_mm.n)
                dst = B.qb[:, h * TC + th * 512: h * TC + (th + 1) * 512]
                P.op("scalar", lambda e, d=dst, p=pm:
                     e.activation(d, p, AF.Identity), B.s_act)
                B.g[("attn", bk)] = B.s_act.n
        attn_all = B.s_act.n

        P.wait("gpsimd", B.s_act, attn_all)
        P.wait("tensor", B.s_act, attn_all)

        def y_put(o, th):
            return B.xn[:, o * TC + th * 512: o * TC + (th + 1) * 512]

        emit_proj(P, B, 8, lambda og, slot: slab_load(woT, og, slot),
                  lambda k, th: B.qb[:, k * TC + th * 512: k * TC + (th + 1) * 512],
                  y_put, KC, 256, bias_col=lambda o: B.boc[:, o:o + 1], og_cols=256)
        y_all = B.s_act.n

        P.wait("vector", B.s_act, y_all)
        P.wait("scalar", B.s_act, y_all)
        P.wait("tensor", B.s_act, y_all)
        res_done = emit_res_ln(P, B, y_put, x_get, KC, eps_ln, inv_d)

        P.wait("sync", B.s_dve, res_done)
        P.dma("sync", chunked(xout), sb3(B.xres[:, :], KC), B.s_st)
        P.wait("sync", B.s_st, B.s_st.n)
    es2.close()



INTER = 8192
IG = 8
ICG = INTER // 128 // IG


class XStream:

    COUNT = [0]

    def __init__(self, P, B, xin, nslots=4):
        self.P, self.B, self.xin = P, B, xin
        self.nslots = nslots
        self.i = 0
        uid = XStream.COUNT[0]
        XStream.COUNT[0] += 1
        self.sems = [P.sem(f"s_xs{uid}_{j}") for j in range(nslots)]

    def get(self, k, th, waiters=("vector",)):
        P, B = self.P, self.B
        slot = self.i % self.nslots
        self.i += 1
        ap = B.xs[:, slot * 512:(slot + 1) * 512]
        P.wait("sync", B.s_dve, B.g.get(("xs", slot), 0))
        src = self.xin[k * 128:(k + 1) * 128, th * 512:(th + 1) * 512]
        tgt = P.dma("sync", ap, src, self.sems[slot])
        for w in waiters:
            P.wait(w, self.sems[slot], tgt)
        return ap, slot

    def mark(self, slot, tgt):
        self.B.g[("xs", slot)] = tgt


def emit_l0_ffn(P, B, nc, xin, cv_in, spl, xout, aggu_in, aggu_out,
                agd_in, agd_out):
    es2 = contextlib.ExitStack()
    if True:
        sb = lambda name, shape, dt: es2.enter_context(
            nc.sbuf_tensor("B_" + name, shape, dt))
        init_done = B.init_done

        B.xn = sb("xn", [128, KC * TC], BF16)
        B.inter = sb("inter", [128, ICG * TC], BF16)
        B.acc = sb("acc", [128, KC * TC], F32)
        B.wslab = sb("wslab", [128, 2 * 4096], BF16)
        B.slab_stride = 4096
        B.spst = sb("spst", [128, 2 * 4096], FP16)
        B.sp16 = sb("sp16", [128, 4 * 512], FP16)
        B.wout = sb("wout", [128, 2 * 512], BF16)
        B.xs = sb("xs", [128, 4 * 512], F32)
        B.dslab = sb("dslab", [128, 2 * 1024], BF16)
        s_wd = [P.sem("s_wd0"), P.sem("s_wd1")]

        eps_ln = B.cv[0:1, CV_EPS_LN:CV_EPS_LN + 1]
        inv_d = 1.0 / 2048.0

        s_sp = [P.sem("s_sp0"), P.sem("s_sp1")]
        s_spv = P.sem("s_spv")
        s_spp = P.sem("s_spp")
        s_agw = [P.sem("s_agw0"), P.sem("s_agw1")]
        s_cc2 = P.sem("s_cc2")

        xst = XStream(P, B, xin)

        def stats_src(k, th):
            ap, slot = xst.get(k, th)
            B.g["_cur_xs_slot"] = slot
            return ap

        def xn_put(k, th):
            return B.xn[:, k * TC + th * 512: k * TC + (th + 1) * 512]

        xn_done = []
        for th in range(2):
            emit_ln_stats(P, B, lambda k, th=th: stats_src(k, th), th, KC,
                          inv_d, B.m_b[:, th * 512:(th + 1) * 512],
                          B.r_b[:, th * 512:(th + 1) * 512], eps_ln,
                          cast_first=True)
            for k in range(KC):
                ap, slot = xst.get(k, th)
                m_ap = B.m_b[:, th * 512:(th + 1) * 512]
                r_ap = B.r_b[:, th * 512:(th + 1) * 512]
                P.op("vector", lambda e, x=ap, m=m_ap: e.tensor_sub(B.t1[:, :], x, m))
                t = P.op("vector", lambda e, o=xn_put(k, th), r=r_ap:
                         e.tensor_mul(o, B.t1[:, :], r), B.s_dve)
                xst.mark(slot, t)
            xn_done.append(B.s_dve.n)

        chunks = []
        for hc in range(KC):
            for h2 in range(2):
                cs_ = slice(h2 * 512, (h2 + 1) * 512)
                chunks.append((
                    [spl[f"p_g{k}"][hc * 128:(hc + 1) * 128, cs_] for k in range(4)]
                    + [spl[f"c_gu{k}"][hc * 128:(hc + 1) * 128, cs_] for k in range(4)],
                    aggu_in[hc * 128:(hc + 1) * 128, cs_]))
        for hc in range(KC):
            for h2 in range(2):
                cs_ = slice(h2 * 512, (h2 + 1) * 512)
                chunks.append((
                    [spl[f"p_u{k}"][hc * 128:(hc + 1) * 128, cs_] for k in range(4)]
                    + [spl[f"c_gu{k}"][hc * 128:(hc + 1) * 128, cs_] for k in range(4)],
                    aggu_in[D + hc * 128: D + (hc + 1) * 128, cs_]))
        for ic in range(TC // 128):
            for q in range(4):
                cs_ = slice(q * 512, (q + 1) * 512)
                chunks.append((
                    [spl[f"p_d{k}"][ic * 128:(ic + 1) * 128, cs_] for k in range(4)]
                    + [spl[f"c_cd{k}"][ic * 128:(ic + 1) * 128, cs_] for k in range(4)],
                    agd_in[ic * 128:(ic + 1) * 128, cs_]))

        eng_sems = {"vector": s_spv, "gpsimd": s_spp}
        for n, (srcs, dst) in enumerate(chunks):
            eng = "vector" if n % 2 == 0 else "gpsimd"
            sl = n % 2
            sE = eng_sems[eng]
            P.wait("gpsimd", sE, B.g.get(("spst", sl), 0))
            for j, sap in enumerate(srcs):
                P.dma("gpsimd", B.spst[:, sl * 4096 + j * 512:
                                       sl * 4096 + (j + 1) * 512], sap, s_sp[sl])
            lds = s_sp[sl].n
            pv = lambda i, sl=sl: B.spst[:, sl * 4096 + i * 512:
                                         sl * 4096 + (i + 1) * 512]
            ta = B.sp16[:, sl * 1024: sl * 1024 + 512]
            tb = B.sp16[:, sl * 1024 + 512: sl * 1024 + 1024]
            wo = B.wout[:, sl * 512:(sl + 1) * 512]
            P.wait(eng, s_sp[sl], lds)
            P.op(eng, lambda e, t=ta, x=pv(0), y=pv(4): e.tensor_mul(t, x, y))
            P.op(eng, lambda e, t=tb, x=pv(1), y=pv(5): e.tensor_mul(t, x, y))
            P.op(eng, lambda e, a=ta, b=tb: e.tensor_add(a, a, b))
            P.op(eng, lambda e, t=tb, x=pv(2), y=pv(6): e.tensor_mul(t, x, y))
            P.op(eng, lambda e, a=ta, b=tb: e.tensor_add(a, a, b))
            P.op(eng, lambda e, t=tb, x=pv(3), y=pv(7): e.tensor_mul(t, x, y), sE)
            B.g[("spst", sl)] = sE.n
            P.wait(eng, s_agw[sl], B.g.get(("wout", sl), 0))
            P.op(eng, lambda e, w=wo, a=ta, b=tb: e.tensor_add(w, a, b), sE)
            wtgt = sE.n
            P.wait("sync", sE, wtgt)
            B.g[("wout", sl)] = P.dma("sync", dst, wo, s_agw[sl])

        P.wait("gpsimd", s_agw[0], s_agw[0].n)
        P.wait("gpsimd", s_agw[1], s_agw[1].n)
        P.op("gpsimd", lambda e: e.collective_compute(
            "AllGather", ALU.bypass, replica_groups=[list(range(NCORES))],
            ins=[aggu_in], outs=[aggu_out]), B.s_cc)
        P.op("gpsimd", lambda e: e.collective_compute(
            "AllGather", ALU.bypass, replica_groups=[list(range(NCORES))],
            ins=[agd_in], outs=[agd_out]), s_cc2)
        P.wait("sync", B.s_cc, B.s_cc.n)
        P.wait("sync", s_cc2, s_cc2.n)

        P.wait("tensor", B.s_dve, xn_done[1])

        def gu_slab_load(g, ic, slot):
            i0 = ic * 128
            gsrc = aggu_out[g * 2 * D: g * 2 * D + D, i0:i0 + 128].rearrange(
                "(k p) j -> p k j", p=128)
            usrc = aggu_out[g * 2 * D + D: (g + 1) * 2 * D, i0:i0 + 128].rearrange(
                "(k p) j -> p k j", p=128)
            P.dma("sync", sb3(B.wslab[:, slot * 4096: slot * 4096 + 2048], KC),
                  gsrc, B.s_w[slot])
            return P.dma("sync", sb3(B.wslab[:, slot * 4096 + 2048:
                                             slot * 4096 + 4096], KC), usrc, B.s_w[slot])

        inter_free = 0
        for g in range(IG):
            for ic in range(ICG):
                slot = ic % 2
                P.wait("sync", B.s_mm, B.g.get(("gu_slab", slot), 0))
                wt = gu_slab_load(g, ic, slot)
                P.wait("tensor", B.s_w[slot], wt)
                if ic == 0:
                    P.wait("tensor", B.s_mm, inter_free)
                for th in range(2):
                    gbk = (ic * 2 + th) % 2
                    pg = bank(B, 4 + gbk)
                    pu = bank(B, 2 + gbk)
                    P.wait("tensor", B.s_act, B.g.get(("gact", gbk), 0))
                    P.wait("tensor", B.s_dve, B.g.get(("uact", gbk), 0))
                    for k in range(KC):
                        lg = B.wslab[:, slot * 4096 + k * 128: slot * 4096 + (k + 1) * 128]
                        P.op("tensor", lambda e, p=pg, l=lg, r=xn_put(k, th),
                             st=(k == 0), sp=(k == KC - 1):
                             e.matmul(p, l, r, start=st, stop=sp),
                             B.s_mm if k == KC - 1 else None)
                    gmm = B.s_mm.n
                    for k in range(KC):
                        lu = B.wslab[:, slot * 4096 + 2048 + k * 128:
                                     slot * 4096 + 2048 + (k + 1) * 128]
                        P.op("tensor", lambda e, p=pu, l=lu, r=xn_put(k, th),
                             st=(k == 0), sp=(k == KC - 1):
                             e.matmul(p, l, r, start=st, stop=sp),
                             B.s_mm if k == KC - 1 else None)
                    umm = B.s_mm.n
                    sg = B.scr_a[:, th * 512:(th + 1) * 512]
                    P.wait("scalar", B.s_mm, gmm)
                    P.wait("scalar", B.s_dve, B.g.get(("sg", th), 0))
                    P.op("scalar", lambda e, d=sg, p=pg:
                         e.activation(d, p, AF.Silu), B.s_act)
                    B.g[("gact", gbk)] = B.s_act.n
                    it = B.inter[:, ic * TC + th * 512: ic * TC + (th + 1) * 512]
                    P.wait("vector", B.s_mm, umm)
                    P.wait("vector", B.s_act, B.g[("gact", gbk)])
                    P.op("vector", lambda e, d=it, s=sg, p=pu:
                         e.tensor_mul(d, s, p), B.s_dve)
                    B.g[("uact", gbk)] = B.s_dve.n
                    B.g[("sg", th)] = B.s_dve.n
                B.g[("gu_slab", slot)] = B.s_mm.n
            inter_done = B.s_dve.n
            P.wait("tensor", B.s_dve, inter_done)
            for o in range(KC):
                dslot = o % 2
                P.wait("sync", B.s_mm, B.g.get(("d_slab", dslot), 0))
                dsrc = agd_out[g * 1024: (g + 1) * 1024, o * 128:(o + 1) * 128]
                dt_ = P.dma("sync",
                            sb3(B.dslab[:, dslot * 1024:(dslot + 1) * 1024], ICG),
                            dsrc.rearrange("(k p) j -> p k j", p=128), s_wd[dslot])
                P.wait("tensor", s_wd[dslot], dt_)
                for th in range(2):
                    dbk = (o * 2 + th) % 2
                    pd = bank(B, 6 + dbk)
                    P.wait("tensor", B.s_dve, B.g.get(("dacc", dbk), 0))
                    for ic in range(ICG):
                        ld = B.dslab[:, dslot * 1024 + ic * 128:
                                     dslot * 1024 + (ic + 1) * 128]
                        rh = B.inter[:, ic * TC + th * 512: ic * TC + (th + 1) * 512]
                        P.op("tensor", lambda e, p=pd, l=ld, r=rh, st=(ic == 0),
                             sp=(ic == ICG - 1): e.matmul(p, l, r, start=st, stop=sp),
                             B.s_mm if ic == ICG - 1 else None)
                    dmm = B.s_mm.n
                    a_ap = B.acc[:, o * TC + th * 512: o * TC + (th + 1) * 512]
                    P.wait("vector", B.s_mm, dmm)
                    if g == 0:
                        P.op("vector", lambda e, a=a_ap, p=pd: e.tensor_copy(a, p),
                             B.s_dve)
                    else:
                        P.op("vector", lambda e, a=a_ap, p=pd: e.tensor_add(a, a, p),
                             B.s_dve)
                    B.g[("dacc", dbk)] = B.s_dve.n
                B.g[("d_slab", dslot)] = B.s_mm.n
            inter_free = B.s_mm.n

        acc_done = B.s_dve.n
        P.wait("vector", B.s_dve, acc_done)
        P.wait("scalar", B.s_dve, acc_done)
        P.wait("tensor", B.s_dve, acc_done)

        def acc_get(k, th):
            return B.acc[:, k * TC + th * 512: k * TC + (th + 1) * 512]

        res_done = 0
        for th in range(2):
            emit_ln_stats(P, B, lambda k, th=th: acc_get(k, th), th, KC,
                          inv_d, B.m_b[:, th * 512:(th + 1) * 512],
                          B.r_b[:, th * 512:(th + 1) * 512], eps_ln,
                          cast_first=True)
            for k in range(KC):
                a_ap = acc_get(k, th)
                m_ap = B.m_b[:, th * 512:(th + 1) * 512]
                r_ap = B.r_b[:, th * 512:(th + 1) * 512]
                ap, slot = xst.get(k, th)
                P.op("vector", lambda e, a=a_ap, m=m_ap: e.tensor_sub(B.t1[:, :], a, m))
                P.op("vector", lambda e, r=r_ap: e.tensor_mul(B.t1[:, :], B.t1[:, :], r))
                t = P.op("vector", lambda e, a=a_ap, x=ap:
                         e.tensor_add(a, x, B.t1[:, :]), B.s_dve)
                xst.mark(slot, t)
            res_done = B.s_dve.n

        P.wait("sync", B.s_dve, res_done)
        P.dma("sync", chunked(xout), sb3(B.acc[:, :], KC), B.s_st)
        P.wait("sync", B.s_st, B.s_st.n)
    es2.close()



NH = 16
QR = 768
KVR = 512
ROPE = 64
VD = 128


def emit_l1_attn(P, B, nc, xin, wq_a, qnw, wq_b, wkv_a, kvnw, wkvb_n, wkvb_v,
                 wo, cos_in, sin_in, cv_in, cv128_in, xout,
                 agk_in, agk_out, agv_in, agv_out):
    es2 = contextlib.ExitStack()
    if True:
        sb = lambda name, shape, dt: es2.enter_context(
            nc.sbuf_tensor("C_" + name, shape, dt))
        init_done = B.init_done

        B.xn = sb("xn", [128, KC * TC], BF16)
        B.qs = sb("qs", [128, KC * TC], BF16)
        B.qa = sb("qa", [128, 6 * TC], BF16)
        B.ckv = sb("ckv", [128, 4 * TC], BF16)
        B.kpe_b = sb("kpe_b", [128, TC], BF16)
        B.knope = sb("knope", [128, 8 * TC], BF16)
        B.vt = sb("vt", [128, 2 * 512], BF16)
        B.cosb = sb("cosb", [32, TC], BF16)
        B.sinb = sb("sinb", [32, TC], BF16)
        B.rp = sb("rp", [32, 3 * TC], BF16)
        B.xw = sb("xw", [32, 2 * TC], BF16)
        B.wslab = sb("wslab", [128, 2 * 4096], BF16)
        B.slab_stride = 4096
        B.kvsl = sb("kvsl", [128, 2 * 4096], BF16)
        B.psb = sb("psb", [128, 2 * 2048], BF16)
        B.xs = sb("xs", [128, 4 * 512], F32)
        B.oscr = sb("oscr", [128, 2 * 512], F32)
        B.qnw_sb = sb("qnw_sb", [128, QR // 128], F32)
        B.kvnw_sb = sb("kvnw_sb", [128, KVR // 128], F32)
        B.cv128 = sb("cv128_sb", [128, NCONST], F32)
        B.den_row = sb("den_row", [1, 1024], F32)
        B.rec_bf = sb("rec_bf", [1, 1024], BF16)

        eps_ln = B.cv[0:1, CV_EPS_LN:CV_EPS_LN + 1]
        eps_rms = B.cv[0:1, CV_EPS_RMS:CV_EPS_RMS + 1]
        inv_d = 1.0 / 2048.0

        P.dma("sync", B.cv[:, :], cv_in, B.s_x)
        P.dma("sync", B.cv128[:, :], cv128_in, B.s_x)
        P.dma("sync", B.qnw_sb[:, :], qnw, B.s_x)
        P.dma("sync", B.kvnw_sb[:, :], kvnw, B.s_x)
        P.dma("gpsimd", B.cosb[:, :], cos_in, B.s_x)
        P.dma("gpsimd", B.sinb[:, :], sin_in, B.s_x)
        consts_tgt = B.s_x.n
        P.wait("scalar", B.s_x, consts_tgt)
        P.wait("vector", B.s_x, consts_tgt)

        xst = XStream(P, B, xin)

        def stats_src(k, th):
            ap, slot = xst.get(k, th)
            B.g["_cur_xs_slot"] = slot
            return ap

        def xn_put(k, th):
            return B.xn[:, k * TC + th * 512: k * TC + (th + 1) * 512]

        xn_done = []
        for th in range(2):
            emit_ln_stats(P, B, lambda k, th=th: stats_src(k, th), th, KC,
                          inv_d, B.m_b[:, th * 512:(th + 1) * 512],
                          B.r_b[:, th * 512:(th + 1) * 512], eps_ln,
                          cast_first=True)
            for k in range(KC):
                ap, slot = xst.get(k, th)
                m_ap = B.m_b[:, th * 512:(th + 1) * 512]
                r_ap = B.r_b[:, th * 512:(th + 1) * 512]
                P.op("vector", lambda e, x=ap, m=m_ap: e.tensor_sub(B.t1[:, :], x, m))
                t = P.op("vector", lambda e, o=xn_put(k, th), r=r_ap:
                         e.tensor_mul(o, B.t1[:, :], r), B.s_dve)
                xst.mark(slot, t)
            xn_done.append(B.s_dve.n)
        P.wait("tensor", B.s_dve, xn_done[1])

        def slab_load(w, og, slot, cols=256, n_k=KC):
            src = w[:, og * cols:(og + 1) * cols].rearrange("(k p) j -> p k j", p=128)
            return P.dma("gpsimd",
                         sb3(B.wslab[:, slot * 4096: slot * 4096 + n_k * cols], n_k),
                         src, B.s_w[slot])

        emit_proj(P, B, 3, lambda og, slot: slab_load(wq_a, og, slot),
                  xn_put, lambda o, th: B.qa[:, o * TC + th * 512: o * TC + (th + 1) * 512],
                  KC, 256, og_cols=256)

        qa_get = lambda k, th: B.qa[:, k * TC + th * 512: k * TC + (th + 1) * 512]
        for th in range(2):
            emit_ln_stats(P, B, lambda k, th=th: qa_get(k, th), th, 6,
                          1.0 / QR, None, B.r_b[:, th * 512:(th + 1) * 512],
                          eps_rms, cast_first=False)
            for k in range(6):
                r_ap = B.r_b[:, th * 512:(th + 1) * 512]
                P.op("vector", lambda e, q=qa_get(k, th), r=r_ap:
                     e.tensor_mul(q, q, r))
                P.op("vector", lambda e, q=qa_get(k, th), k_=k: e.tensor_scalar(
                    q, q, B.qnw_sb[:, k_:k_ + 1], None, ALU.mult), B.s_dve)
        qan_done = B.s_dve.n
        P.wait("tensor", B.s_dve, qan_done)

        emit_proj(P, B, 8, lambda og, slot: slab_load(wq_b, og, slot, n_k=6),
                  qa_get, lambda o, th: B.qs[:, o * TC + th * 512: o * TC + (th + 1) * 512],
                  6, 256, og_cols=256)
        q_evac_done = B.s_act.n

        emit_proj(P, B, 2, lambda og, slot: slab_load(wkv_a, og, slot),
                  xn_put, lambda o, th: B.ckv[:, o * TC + th * 512: o * TC + (th + 1) * 512],
                  KC, 256, og_cols=256)
        P.wait("gpsimd", B.s_mm, B.s_mm.n)
        kpe_slab = P.dma("gpsimd",
                         sb3(B.wslab[:, 0:KC * 64], KC),
                         wkv_a[:, KVR:KVR + ROPE].rearrange("(k p) j -> p k j", p=128),
                         B.s_w[0])
        P.wait("tensor", B.s_w[0], kpe_slab)
        P.wait("tensor", B.s_act, B.s_act.n)
        for th in range(2):
            pm = bank(B, 4 + th, parts=64)
            for k in range(KC):
                lhsT = B.wslab[:, k * 64:(k + 1) * 64]
                P.op("tensor", lambda e, p=pm, l=lhsT, r=xn_put(k, th),
                     st=(k == 0), sp=(k == KC - 1):
                     e.matmul(p, l, r, start=st, stop=sp),
                     B.s_mm if k == KC - 1 else None)
            P.wait("scalar", B.s_mm, B.s_mm.n)
            P.op("scalar", lambda e, th=th, p=pm: e.activation(
                B.kpe_b[0:64, th * 512:(th + 1) * 512], p, AF.Identity), B.s_act)
        kpe_done = B.s_act.n
        B.g[("proj_evac", 0)] = kpe_done
        B.g[("proj_evac", 1)] = kpe_done

        c_get = lambda k, th: B.ckv[:, k * TC + th * 512: k * TC + (th + 1) * 512]
        P.wait("vector", B.s_act, B.s_act.n)
        for th in range(2):
            emit_ln_stats(P, B, lambda k, th=th: c_get(k, th), th, 4,
                          1.0 / KVR, None, B.r_b[:, th * 512:(th + 1) * 512],
                          eps_rms, cast_first=False)
            for k in range(4):
                r_ap = B.r_b[:, th * 512:(th + 1) * 512]
                P.op("vector", lambda e, q=c_get(k, th), r=r_ap:
                     e.tensor_mul(q, q, r))
                P.op("vector", lambda e, q=c_get(k, th), k_=k: e.tensor_scalar(
                    q, q, B.kvnw_sb[:, k_:k_ + 1], None, ALU.mult), B.s_dve)
        cn_done = B.s_dve.n
        P.wait("tensor", B.s_dve, cn_done)

        s_rp = [P.sem("s_rp0"), P.sem("s_rp1")]

        def emit_rope(x1, x2_src, o1, o2_dst, slot):
            xi = B.xw[0:32, slot * TC: (slot + 1) * TC]
            xo = xi
            P.wait("sync", s_rp[slot], B.g.get(("xw_dma", slot), 0))
            P.wait("sync", B.s_dve, B.g.get(("xw", slot), 0))
            it = P.dma("sync", xi, x2_src, s_rp[slot])
            a = B.rp[0:32, 0:TC]
            b = B.rp[0:32, TC:2 * TC]
            c = B.rp[0:32, 2 * TC:3 * TC]
            cos = B.cosb[0:32, :]
            sin = B.sinb[0:32, :]
            P.wait("vector", s_rp[slot], it)
            P.op("vector", lambda e: e.tensor_mul(a, x1, cos))
            P.op("vector", lambda e: e.tensor_mul(b, xi, sin))
            P.op("vector", lambda e: e.tensor_mul(c, x1, sin))
            P.op("vector", lambda e: e.tensor_mul(xi, xi, cos))
            P.op("vector", lambda e: e.tensor_sub(o1, a, b))
            t = P.op("vector", lambda e: e.tensor_add(xo, c, xi), B.s_dve)
            B.g[("xw", slot)] = t
            P.wait("sync", B.s_dve, t)
            B.g[("xw_dma", slot)] = P.dma("sync", o2_dst, xo, s_rp[slot])
            return B.g[("xw_dma", slot)]

        P.wait("vector", B.s_act, q_evac_done)
        P.wait("sync", B.s_act, q_evac_done)
        for h in range(NH):
            base = h * TC
            emit_rope(B.qs[0:32, base:base + TC], B.qs[32:64, base:base + TC],
                      B.qs[0:32, base:base + TC], B.qs[32:64, base:base + TC],
                      h % 2)
        P.wait("vector", B.s_act, kpe_done)
        P.wait("sync", B.s_act, kpe_done)
        emit_rope(B.kpe_b[0:32, :], B.kpe_b[32:64, :],
                  B.kpe_b[0:32, :], B.kpe_b[32:64, :], 0)
        rope_done = B.s_dve.n
        rope_dma = [s_rp[0].n, s_rp[1].n]

        emit_proj(P, B, 4, lambda og, slot: slab_load(wkvb_n, og, slot, n_k=4),
                  c_get, lambda o, th: B.knope[:, o * TC + th * 512: o * TC + (th + 1) * 512],
                  4, 256, og_cols=256)
        knope_done = B.s_act.n

        s_vt = [P.sem("s_vt0"), P.sem("s_vt1")]
        P.wait("tensor", B.s_act, knope_done)
        P.wait("gpsimd", B.s_mm, B.s_mm.n)
        for vg in range(4):
            slot = vg % 2
            P.wait("gpsimd", B.s_mm, B.g.get(("vslab", slot), 0))
            wt = P.dma("gpsimd",
                       sb3(B.wslab[:, slot * 4096: slot * 4096 + 4 * 512], 4),
                       wkvb_v[:, vg * 512:(vg + 1) * 512].rearrange(
                           "(k p) j -> p k j", p=128), B.s_w[slot])
            P.wait("tensor", B.s_w[slot], wt)
            for s in range(8):
                vbk = s % 2
                pm = bank(B, 4 + vbk)
                P.wait("tensor", B.s_act, B.g.get(("vprod", vbk), 0))
                for k in range(4):
                    lhsT = B.ckv[:, k * TC + s * 128: k * TC + s * 128 + 128]
                    rhs = B.wslab[:, slot * 4096 + k * 512:(slot * 4096) + (k + 1) * 512]
                    P.op("tensor", lambda e, p=pm, l=lhsT, r=rhs, st=(k == 0),
                         sp=(k == 3): e.matmul(p, l, r, start=st, stop=sp),
                         B.s_mm if k == 3 else None)
                mm_tgt = B.s_mm.n
                vsl = vbk
                P.wait("scalar", B.s_mm, mm_tgt)
                P.wait("scalar", s_vt[vsl], B.g.get(("vt", vsl), 0))
                dst = B.vt[:, vsl * 512:(vsl + 1) * 512]
                P.op("scalar", lambda e, d=dst, p=pm:
                     e.activation(d, p, AF.Identity), B.s_act)
                B.g[("vprod", vbk)] = B.s_act.n
                P.wait("sync", B.s_act, B.s_act.n)
                B.g[("vt", vsl)] = P.dma(
                    "sync", agv_in[s * 128:(s + 1) * 128, vg * 512:(vg + 1) * 512],
                    dst, s_vt[vsl])
            B.g[("vslab", slot)] = B.s_mm.n

        P.wait("sync", B.s_dve, rope_done)
        P.wait("sync", s_rp[0], rope_dma[0])
        P.wait("sync", s_rp[1], rope_dma[1])
        for h in range(NH):
            P.dma("sync", agk_in[h * 128: h * 128 + 64, :], B.kpe_b[0:64, :], B.s_st)
        P.wait("sync", B.s_act, knope_done)
        for j in range(8):
            P.dma("sync", agk_in[j * 256 + 64: j * 256 + 128, :],
                  B.knope[0:64, j * TC:(j + 1) * TC], B.s_st)
            P.dma("sync", agk_in[j * 256 + 192: j * 256 + 256, :],
                  B.knope[64:128, j * TC:(j + 1) * TC], B.s_st)
        for sl in range(2):
            P.wait("sync", s_vt[sl], s_vt[sl].n)
        P.op("sync", lambda e: e.nop(), B.s_st, inc=1)
        stage_done = B.s_st.n

        s_cck = B.s_cc
        s_ccv = P.sem("s_ccv")
        P.wait("gpsimd", B.s_st, stage_done)
        P.op("gpsimd", lambda e: e.collective_compute(
            "AllGather", ALU.bypass, replica_groups=[list(range(NCORES))],
            ins=[agk_in], outs=[agk_out]), s_cck)
        P.op("gpsimd", lambda e: e.collective_compute(
            "AllGather", ALU.bypass, replica_groups=[list(range(NCORES))],
            ins=[agv_in], outs=[agv_out]), s_ccv)
        P.wait("sync", s_cck, s_cck.n)
        P.wait("sync", s_ccv, s_ccv.n)

        s_kv = [P.sem("s_kv0"), P.sem("s_kv1")]
        P.wait("tensor", B.s_dve, B.s_dve.n)
        P.wait("tensor", B.s_act, B.s_act.n)
        P.wait("tensor", s_rp[0], rope_dma[0])
        P.wait("tensor", s_rp[1], rope_dma[1])
        attnscale = B.cv128[:, CV_ATTNSCALE:CV_ATTNSCALE + 1]
        for qh in range(2):
            for hp in range(8):
                for r in range(NCORES):
                    slot = (hp * 8 + r) % 2
                    P.wait("sync", B.s_mm, B.g.get(("kvsl", slot), 0))
                    for h01 in range(2):
                        h = hp * 2 + h01
                        P.dma("sync", B.kvsl[:, slot * 4096 + h01 * 1024:
                                             slot * 4096 + (h01 + 1) * 1024],
                              agk_out[r * 2048 + h * 128: r * 2048 + (h + 1) * 128, :],
                              s_kv[slot])
                    kvt = P.dma("sync",
                                sb3(B.kvsl[:, slot * 4096 + 2048: (slot + 1) * 4096], 8),
                                agv_out[r * 1024:(r + 1) * 1024,
                                        hp * 256:(hp + 1) * 256].rearrange(
                                            "(kc p) j -> p kc j", p=128),
                                s_kv[slot])
                    P.wait("tensor", s_kv[slot], kvt)
                    first = (r == 0)
                    last = (r == NCORES - 1)
                    for h01 in range(2):
                        h = hp * 2 + h01
                        if first:
                            P.wait("tensor", B.s_dve, B.g.get(("den", h01), 0))
                            P.wait("tensor", B.s_dve, B.g.get(("pv", h01), 0))
                        for kcg in range(2):
                            psl = (h01 * 2 + kcg) % 2
                            P.wait("tensor", B.s_act, B.g.get("exp_done", 0))
                            for kc4 in range(4):
                                kc = kcg * 4 + kc4
                                lhsT = B.kvsl[:, slot * 4096 + h01 * 1024 + kc * 128:
                                              slot * 4096 + h01 * 1024 + (kc + 1) * 128]
                                rhs = B.qs[:, h * TC + qh * 512: h * TC + (qh + 1) * 512]
                                P.op("tensor", lambda e, p=bank(B, 2 + kc4), l=lhsT, r=rhs:
                                     e.matmul(p, l, r, start=True, stop=True),
                                     B.s_mm if kc4 == 3 else None)
                            sc_tgt = B.s_mm.n
                            P.wait("scalar", B.s_mm, sc_tgt)
                            P.wait("scalar", B.s_mm, B.g.get(("psb", psl), 0))
                            pdst = B.psb[:, psl * 2048:(psl + 1) * 2048]
                            P.op("scalar", lambda e, d=pdst: e.activation(
                                d, B.pool[:, 1024:3072], AF.Exp, scale=attnscale),
                                B.s_act)
                            B.g["exp_done"] = B.s_act.n
                            exp_tgt = B.s_act.n
                            P.wait("tensor", B.s_act, exp_tgt)
                            st = first and kcg == 0
                            for kc4 in range(4):
                                kc = kcg * 4 + kc4
                                sp = last and kcg == 1 and kc4 == 3
                                vtile = B.kvsl[:, slot * 4096 + 2048 + kc * 256 +
                                               h01 * 128: slot * 4096 + 2048 +
                                               kc * 256 + h01 * 128 + 128]
                                prh = pdst[:, kc4 * 512:(kc4 + 1) * 512]
                                P.op("tensor", lambda e, p=bank(B, 6 + h01), l=vtile,
                                     r=prh, st=(st and kc4 == 0), sp=sp:
                                     e.matmul(p, l, r, start=st, stop=sp),
                                     B.s_mm if sp else None)
                                P.op("tensor", lambda e, p=bank(B, h01, parts=1),
                                     r=prh, st=(st and kc4 == 0), sp=sp:
                                     e.matmul(p, B.ones_col[:, :], r, start=st, stop=sp),
                                     B.s_mm if (sp or kc4 == 3) else None)
                            B.g[("psb", psl)] = B.s_mm.n
                    B.g[("kvsl", slot)] = B.s_mm.n
                pv_tgt = B.s_mm.n
                P.wait("vector", B.s_mm, pv_tgt)
                P.op("vector", lambda e: e.tensor_copy(B.den_row[0:1, 0:1024],
                                                       B.pool[0:1, 0:1024]), B.s_dve)
                den_copied = B.s_dve.n
                for h01 in range(2):
                    B.g[("den", h01)] = den_copied
                P.op("vector", lambda e: e.reciprocal(B.den_row[0:1, 0:1024],
                                                      B.den_row[0:1, 0:1024]))
                P.op("vector", lambda e: e.tensor_copy(B.rec_bf[0:1, 0:1024],
                                                       B.den_row[0:1, 0:1024]), B.s_dve)
                P.wait("tensor", B.s_dve, B.s_dve.n)
                P.wait("tensor", B.s_act, B.g.get("exp_done", 0))
                for h01 in range(2):
                    P.op("tensor", lambda e, h01=h01: e.matmul(
                        bank(B, 2 + h01), B.ones_row[0:1, 0:128],
                        B.rec_bf[0:1, h01 * 512:(h01 + 1) * 512],
                        start=True, stop=True), B.s_mm)
                rb_tgt = B.s_mm.n
                P.wait("vector", B.s_mm, rb_tgt)
                for h01 in range(2):
                    h = hp * 2 + h01
                    P.op("vector", lambda e, h01=h01: e.tensor_copy(
                        B.m_b[:, h01 * 512:(h01 + 1) * 512], bank(B, 2 + h01)), B.s_dve)
                    dst = B.xn[:, h * TC + qh * 512: h * TC + (qh + 1) * 512]
                    P.op("vector", lambda e, d=dst, h01=h01: e.tensor_mul(
                        d, bank(B, 6 + h01), B.m_b[:, h01 * 512:(h01 + 1) * 512]),
                        B.s_dve)
                    B.g[("pv", h01)] = B.s_dve.n
                B.g["exp_done"] = max(B.g.get("exp_done", 0), 0)
                B.g["exp_done"] = B.s_act.n
                B.g[("rb_read")] = B.s_dve.n
                P.wait("tensor", B.s_dve, B.g[("rb_read")])
        attn_done = B.s_dve.n

        P.wait("tensor", B.s_dve, attn_done)
        P.wait("gpsimd", B.s_dve, attn_done)
        emit_proj(P, B, 8, lambda og, slot: slab_load(wo, og, slot),
                  lambda k, th: B.xn[:, k * TC + th * 512: k * TC + (th + 1) * 512],
                  lambda o, th: B.qs[:, o * TC + th * 512: o * TC + (th + 1) * 512],
                  KC, 256, og_cols=256)
        y_all = B.s_act.n

        P.wait("vector", B.s_act, y_all)
        P.wait("scalar", B.s_act, y_all)
        P.wait("tensor", B.s_act, y_all)
        y_get = lambda k, th: B.qs[:, k * TC + th * 512: k * TC + (th + 1) * 512]
        s_out = [P.sem("s_out0"), P.sem("s_out1")]
        for th in range(2):
            emit_ln_stats(P, B, lambda k, th=th: y_get(k, th), th, KC,
                          inv_d, B.m_b[:, th * 512:(th + 1) * 512],
                          B.r_b[:, th * 512:(th + 1) * 512], eps_ln,
                          cast_first=False)
            for k in range(KC):
                m_ap = B.m_b[:, th * 512:(th + 1) * 512]
                r_ap = B.r_b[:, th * 512:(th + 1) * 512]
                osl = k % 2
                o_ap = B.oscr[:, osl * 512:(osl + 1) * 512]
                ap, slot = xst.get(k, th)
                P.op("vector", lambda e, y=y_get(k, th), m=m_ap:
                     e.tensor_sub(B.t1[:, :], y, m))
                P.op("vector", lambda e, r=r_ap: e.tensor_mul(B.t1[:, :], B.t1[:, :], r))
                P.wait("vector", s_out[osl], B.g.get(("oscr", osl), 0))
                t = P.op("vector", lambda e, o=o_ap, x=ap:
                         e.tensor_add(o, x, B.t1[:, :]), B.s_dve)
                xst.mark(slot, t)
                P.wait("sync", B.s_dve, t)
                B.g[("oscr", osl)] = P.dma(
                    "sync", xout[k * 128:(k + 1) * 128, th * 512:(th + 1) * 512],
                    o_ap, s_out[osl])
        P.wait("sync", s_out[0], s_out[0].n)
        P.wait("sync", s_out[1], s_out[1].n)
    es2.close()


def prep_l1_attn(inp):
    wq_b = inp["L1_attn_wq_b"]
    perm = np.zeros(D, dtype=np.int64)
    for h in range(NH):
        perm[h * 128: h * 128 + 64] = h * 128 + 64 + np.arange(64)
        perm[h * 128 + 64: h * 128 + 128] = h * 128 + np.arange(64)
    wq_b_p = np.ascontiguousarray(wq_b[:, perm])
    wkv_b = inp["L1_attn_wkv_b"]
    cols_n = np.concatenate([h * 192 + np.arange(64) for h in range(NH)])
    cols_v = np.concatenate([h * 192 + 64 + np.arange(128) for h in range(NH)])
    wkvb_n = np.ascontiguousarray(wkv_b[:, cols_n])
    wkvb_v = np.ascontiguousarray(wkv_b[:, cols_v])
    qnw = np.ascontiguousarray(inp["L1_attn_q_a_norm_w"].reshape(6, 128).T)
    kvnw = np.ascontiguousarray(inp["L1_attn_kv_a_norm_w"].reshape(4, 128).T)
    cv128 = np.repeat(CV_VALS, 128, axis=0)
    inv_freq = (1.0 / 10000.0 ** (np.arange(0, ROPE, 2, dtype=np.float64) / ROPE))
    maps = []
    for c in range(NCORES):
        tpos = np.arange(c * TC, (c + 1) * TC, dtype=np.float64)
        freqs = np.outer(inv_freq, tpos)
        maps.append({
            "wq_a": inp["L1_attn_wq_a"], "qnw": qnw, "wq_b": wq_b_p,
            "wkv_a": inp["L1_attn_wkv_a"], "kvnw": kvnw,
            "wkvb_n": wkvb_n, "wkvb_v": wkvb_v, "wo": inp["L1_attn_wo"],
            "cos_in": np.cos(freqs).astype(np.float32),
            "sin_in": np.sin(freqs).astype(np.float32),
            "cv": CV_VALS, "cv128": cv128,
        })
    return maps



HID = 5632
HC = HID // 128
DG = 4
HCG = HC // DG
D_TRUNC = 4


def emit_l1_ffn(P, B, nc, xin, wqgT, wquT, wqdT, sg_c, su_c, sd_c, cv_in, xout):
    es2 = contextlib.ExitStack()
    if True:
        sb = lambda name, shape, dt: es2.enter_context(
            nc.sbuf_tensor("D_" + name, shape, dt))
        init_done = B.init_done

        B.xn = sb("xn", [128, KC * TC], BF16)
        B.inter = sb("inter", [128, HCG * TC], BF16)
        B.acc = sb("acc", [128, KC * TC], F32)
        B.wslab = sb("wslab", [128, 2 * 4096], BF16)
        B.slab_stride = 4096
        B.dslab = sb("dslab", [128, 2 * HCG * 128], BF16)
        B.xs = sb("xs", [128, 4 * 512], F32)
        B.sgc = sb("sgc", [128, HC], F32)
        B.suc = sb("suc", [128, HC], F32)
        B.sdc = sb("sdc", [128, KC], F32)
        B.m128 = sb("m128", [1, 128], BF16)
        B.sx_bf = sb("sx_bf", [1, TC], BF16)
        B.si_bf = sb("si_bf", [1, TC], BF16)

        eps_ln = B.cv[0:1, CV_EPS_LN:CV_EPS_LN + 1]
        inv_d = 1.0 / 2048.0
        P.dma("sync", B.sgc[:, :], sg_c, B.s_x)
        P.dma("sync", B.suc[:, :], su_c, B.s_x)
        P.dma("sync", B.sdc[:, :], sd_c, B.s_x)
        P.wait("scalar", B.s_x, B.s_x.n)
        P.wait("vector", B.s_x, B.s_x.n)
        P.op("vector", lambda e: e.memset(B.m128[:, :], -128.0), B.s_dve)

        xst = XStream(P, B, xin)

        def stats_src(k, th):
            ap, slot = xst.get(k, th)
            B.g["_cur_xs_slot"] = slot
            return ap

        def xn_put(k, th):
            return B.xn[:, k * TC + th * 512: k * TC + (th + 1) * 512]

        xn_done = []
        for th in range(2):
            emit_ln_stats(P, B, lambda k, th=th: stats_src(k, th), th, KC,
                          inv_d, B.m_b[:, th * 512:(th + 1) * 512],
                          B.r_b[:, th * 512:(th + 1) * 512], eps_ln,
                          cast_first=True)
            for k in range(KC):
                ap, slot = xst.get(k, th)
                m_ap = B.m_b[:, th * 512:(th + 1) * 512]
                r_ap = B.r_b[:, th * 512:(th + 1) * 512]
                P.op("vector", lambda e, x=ap, m=m_ap: e.tensor_sub(B.t1[:, :], x, m))
                t = P.op("vector", lambda e, o=xn_put(k, th), r=r_ap:
                         e.tensor_mul(o, B.t1[:, :], r), B.s_dve)
                xst.mark(slot, t)
            xn_done.append(B.s_dve.n)
        P.wait("tensor", B.s_dve, xn_done[1])

        for th in range(2 if D_TRUNC >= 1 else 0):
            P.wait("tensor", B.s_dve, B.s_dve.n)
            for k in range(KC):
                P.op("tensor", lambda e, r=xn_put(k, th), st=(k == 0), sp=(k == KC - 1):
                     e.matmul(bank(B, 0, parts=1), B.ones_col[:, :], r,
                              start=st, stop=sp),
                     B.s_mm if k == KC - 1 else None)
            P.wait("vector", B.s_mm, B.s_mm.n)
            P.op("vector", lambda e, th=th: e.tensor_copy(
                B.sx_bf[0:1, th * 512:(th + 1) * 512], bank(B, 0, parts=1)), B.s_dve)
        sx_done = B.s_dve.n
        P.wait("tensor", B.s_dve, sx_done)
        if D_TRUNC < 1:
            sx_done = B.s_dve.n

        s_wd = [P.sem("s_wdq0"), P.sem("s_wdq1")]
        inter_free = 0
        for g in range(DG if D_TRUNC >= 4 else (1 if D_TRUNC >= 2 else 0)):
            for ic in range(HCG):
                hc = g * HCG + ic
                slot = ic % 2
                P.wait("gpsimd", B.s_mm, B.g.get(("gu_slab", slot), 0))
                P.dma("gpsimd", sb3(B.wslab[:, slot * 4096: slot * 4096 + 2048], KC),
                      chunked(wqgT[:, hc * 128:(hc + 1) * 128]), B.s_w[slot])
                wt = P.dma("gpsimd",
                           sb3(B.wslab[:, slot * 4096 + 2048: slot * 4096 + 4096], KC),
                           chunked(wquT[:, hc * 128:(hc + 1) * 128]), B.s_w[slot])
                P.wait("tensor", B.s_w[slot], wt)
                if ic == 0:
                    P.wait("tensor", B.s_mm, inter_free)
                for th in range(2):
                    gbk = (ic * 2 + th) % 2
                    pg = bank(B, 4 + gbk)
                    pu = bank(B, 2 + gbk)
                    P.wait("tensor", B.s_act, B.g.get(("gact", gbk), 0))
                    P.wait("tensor", B.s_dve, B.g.get(("uact", gbk), 0))
                    for which, base, pm in (("g", 0, pg), ("u", 2048, pu)):
                        for k in range(KC):
                            lw = B.wslab[:, slot * 4096 + base + k * 128:
                                         slot * 4096 + base + (k + 1) * 128]
                            P.op("tensor", lambda e, p=pm, l=lw, r=xn_put(k, th),
                                 st=(k == 0): e.matmul(p, l, r, start=st, stop=False))
                        P.op("tensor", lambda e, p=pm, th=th: e.matmul(
                            p, B.m128[:, :], B.sx_bf[0:1, th * 512:(th + 1) * 512],
                            start=False, stop=True),
                            B.s_mm)
                    gu_mm = B.s_mm.n
                    sg = B.scr_a[:, th * 512:(th + 1) * 512]
                    P.wait("scalar", B.s_mm, gu_mm)
                    P.wait("scalar", B.s_dve, B.g.get(("sg", th), 0))
                    P.op("scalar", lambda e, d=sg, p=pg, hc=hc: e.activation(
                        d, p, AF.Silu, scale=B.sgc[:, hc:hc + 1]), B.s_act)
                    B.g[("gact", gbk)] = B.s_act.n
                    it = B.inter[:, ic * TC + th * 512: ic * TC + (th + 1) * 512]
                    P.wait("vector", B.s_mm, gu_mm)
                    P.wait("vector", B.s_act, B.g[("gact", gbk)])
                    P.op("vector", lambda e, d=it, s=sg, p=pu, hc=hc:
                         e.scalar_tensor_tensor(d, p, B.suc[:, hc:hc + 1], s,
                                                ALU.mult, ALU.mult), B.s_dve)
                    B.g[("uact", gbk)] = B.s_dve.n
                    B.g[("sg", th)] = B.s_dve.n
                B.g[("gu_slab", slot)] = B.s_mm.n
            inter_done = B.s_dve.n
            P.wait("tensor", B.s_dve, inter_done)
            for th in range(2):
                P.wait("tensor", B.s_dve, B.s_dve.n)
                for ic in range(HCG):
                    it = B.inter[:, ic * TC + th * 512: ic * TC + (th + 1) * 512]
                    P.op("tensor", lambda e, r=it, st=(ic == 0), sp=(ic == HCG - 1):
                         e.matmul(bank(B, 1, parts=1), B.ones_col[:, :], r,
                                  start=st, stop=sp),
                         B.s_mm if ic == HCG - 1 else None)
                P.wait("vector", B.s_mm, B.s_mm.n)
                P.op("vector", lambda e, th=th: e.tensor_copy(
                    B.si_bf[0:1, th * 512:(th + 1) * 512], bank(B, 1, parts=1)),
                    B.s_dve)
            si_done = B.s_dve.n
            P.wait("tensor", B.s_dve, si_done)
            for o in range(KC if D_TRUNC >= 3 else 0):
                dslot = o % 2
                P.wait("gpsimd", B.s_mm, B.g.get(("d_slab", dslot), 0))
                dt_ = P.dma("gpsimd",
                            sb3(B.dslab[:, dslot * HCG * 128:(dslot + 1) * HCG * 128],
                                HCG),
                            chunked(wqdT[g * HCG * 128:(g + 1) * HCG * 128,
                                         o * 128:(o + 1) * 128]), s_wd[dslot])
                P.wait("tensor", s_wd[dslot], dt_)
                for th in range(2):
                    dbk = (o * 2 + th) % 2
                    pd = bank(B, 6 + dbk)
                    P.wait("tensor", B.s_dve, B.g.get(("dacc", dbk), 0))
                    for ic in range(HCG):
                        ld = B.dslab[:, dslot * HCG * 128 + ic * 128:
                                     dslot * HCG * 128 + (ic + 1) * 128]
                        rh = B.inter[:, ic * TC + th * 512: ic * TC + (th + 1) * 512]
                        P.op("tensor", lambda e, p=pd, l=ld, r=rh, st=(ic == 0):
                             e.matmul(p, l, r, start=st, stop=False))
                    P.op("tensor", lambda e, p=pd, th=th: e.matmul(
                        p, B.m128[:, :], B.si_bf[0:1, th * 512:(th + 1) * 512],
                        start=False, stop=True), B.s_mm)
                    dmm = B.s_mm.n
                    a_ap = B.acc[:, o * TC + th * 512: o * TC + (th + 1) * 512]
                    P.wait("vector", B.s_mm, dmm)
                    if g == 0:
                        P.op("vector", lambda e, a=a_ap, p=pd, o=o: e.tensor_scalar(
                            a, p, B.sdc[:, o:o + 1], None, ALU.mult), B.s_dve)
                    else:
                        P.op("vector", lambda e, a=a_ap, p=pd, o=o:
                             e.scalar_tensor_tensor(a, p, B.sdc[:, o:o + 1], a,
                                                    ALU.mult, ALU.add), B.s_dve)
                    B.g[("dacc", dbk)] = B.s_dve.n
                B.g[("d_slab", dslot)] = B.s_mm.n
            inter_free = B.s_mm.n

        acc_done = B.s_dve.n
        P.wait("vector", B.s_dve, acc_done)
        P.wait("scalar", B.s_dve, acc_done)
        P.wait("tensor", B.s_dve, acc_done)

        def acc_get(k, th):
            return B.acc[:, k * TC + th * 512: k * TC + (th + 1) * 512]

        res_done = 0
        for th in range(2):
            emit_ln_stats(P, B, lambda k, th=th: acc_get(k, th), th, KC,
                          inv_d, B.m_b[:, th * 512:(th + 1) * 512],
                          B.r_b[:, th * 512:(th + 1) * 512], eps_ln,
                          cast_first=True)
            for k in range(KC):
                a_ap = acc_get(k, th)
                m_ap = B.m_b[:, th * 512:(th + 1) * 512]
                r_ap = B.r_b[:, th * 512:(th + 1) * 512]
                ap, slot = xst.get(k, th)
                P.op("vector", lambda e, a=a_ap, m=m_ap: e.tensor_sub(B.t1[:, :], a, m))
                P.op("vector", lambda e, r=r_ap: e.tensor_mul(B.t1[:, :], B.t1[:, :], r))
                t = P.op("vector", lambda e, a=a_ap, x=ap:
                         e.tensor_add(a, x, B.t1[:, :]), B.s_dve)
                xst.mark(slot, t)
            res_done = B.s_dve.n

        P.wait("sync", B.s_dve, res_done)
        P.dma("sync", chunked(xout), sb3(B.acc[:, :], KC), B.s_st)
        P.wait("sync", B.s_st, B.s_st.n)
    es2.close()


def _u8(a):
    return np.ascontiguousarray(a.view(np.uint8).reshape(*a.shape, 4)[..., 0])


def prep_l1_ffn(inp):
    wqgT = np.ascontiguousarray(_u8(inp["L1_ffn_wq_gate"]).T)
    wquT = np.ascontiguousarray(_u8(inp["L1_ffn_wq_up"]).T)
    wqdT = np.ascontiguousarray(_u8(inp["L1_ffn_wq_down"]).T)
    sg_c = np.ascontiguousarray(inp["L1_ffn_s_gate"].reshape(HC, 128).T)
    su_c = np.ascontiguousarray(inp["L1_ffn_s_up"].reshape(HC, 128).T)
    sd_c = np.ascontiguousarray(inp["L1_ffn_s_down"].reshape(KC, 128).T)
    shared = {"wqgT": wqgT, "wquT": wquT, "wqdT": wqdT,
              "sg_c": sg_c, "su_c": su_c, "sd_c": sd_c, "cv": CV_VALS}
    return [dict(shared) for _ in range(NCORES)]



def _shard_x(x):
    xs = x.reshape(NCORES, TC, D)
    return [np.ascontiguousarray(xs[c].T) for c in range(NCORES)]


N_CTRL = 130156


def _spline_grid():
    n = INTER * D
    t = np.linspace(0.0, N_CTRL - 1.0, n, dtype=np.float64)
    i = np.clip(np.floor(t).astype(np.int64), 0, N_CTRL - 2)
    f = (t - i).astype(np.float32)
    idx = [np.clip(i + k, 0, N_CTRL - 1).astype(np.int32) for k in (-1, 0, 1, 2)]
    f64 = f.astype(np.float64)
    f2, f3 = f64 * f64, f64 * f64 * f64
    c0 = (0.5 * (-f64 + 2.0 * f2 - f3)).astype(np.float32)
    c1 = (0.5 * (2.0 - 5.0 * f2 + 3.0 * f3)).astype(np.float32)
    c2 = (0.5 * (f64 + 4.0 * f2 - 3.0 * f3)).astype(np.float32)
    c3 = (0.5 * (f3 - f2)).astype(np.float32)
    return idx, (c0, c1, c2, c3)


def _spline_shards(inp):
    idx, cs = _spline_grid()
    cp_g = inp["L0_ffn_gate_cp"]
    cp_u = inp["L0_ffn_up_cp"]
    cp_d = inp["L0_ffn_down_cp"]
    shards = [dict() for _ in range(NCORES)]
    o_idx = np.arange(D, dtype=np.int64)[:, None] * INTER
    for c in range(NCORES):
        sl = slice(c * TC * D, (c + 1) * TC * D)
        nmat = (o_idx + (c * TC + np.arange(TC, dtype=np.int64))[None, :]).T
        for k in range(4):
            ik = idx[k]
            shards[c][f"p_g{k}"] = np.ascontiguousarray(
                cp_g[ik[sl]].reshape(TC, D).T)
            shards[c][f"p_u{k}"] = np.ascontiguousarray(
                cp_u[ik[sl]].reshape(TC, D).T)
            shards[c][f"c_gu{k}"] = np.ascontiguousarray(
                cs[k][sl].reshape(TC, D).T)
            ikm = ik[nmat]
            shards[c][f"p_d{k}"] = cp_d[ikm]
            shards[c][f"c_cd{k}"] = cs[k][nmat]
    return shards


def prep_l0_ffn(inp):
    shards = _spline_shards(inp)
    for m in shards:
        m["cv"] = CV_VALS
    return shards


def prep_l0_attn(inp):
    shared = {
        "wqT": np.ascontiguousarray(inp["L0_attn_Wq"].T),
        "wkT": np.ascontiguousarray(inp["L0_attn_Wk"].T),
        "wvT": np.ascontiguousarray(inp["L0_attn_Wv"].T),
        "woT": np.ascontiguousarray(inp["L0_attn_Wo"].T),
        "bq_c": np.ascontiguousarray(inp["L0_attn_bq"].reshape(KC, 128).T),
        "bo_c": np.ascontiguousarray(inp["L0_attn_bo"].reshape(KC, 128).T),
        "bk_r": inp["L0_attn_bk"].reshape(1, D),
        "bv_r": inp["L0_attn_bv"].reshape(1, D),
        "cv": CV_VALS,
    }
    return [dict(shared) for _ in range(NCORES)]



def phase_barrier(P, B):
    s = B.s_bar
    for eng in P.ENGINES:
        P.op(eng, lambda e: e.nop(), s)
    tgt = s.n
    for eng in P.ENGINES:
        P.wait(eng, s, tgt)
    B.g = {}


def declare_drams(nc):
    dr = {}

    def inp(name, shape, dt=F32):
        dr[name] = nc.dram_tensor(name, shape, dt, kind="ExternalInput").ap()

    def internal(name, shape, dt=F32):
        dr[name] = nc.dram_tensor(name, shape, dt).ap()

    inp("xin", [D, TC])
    inp("cv", [1, NCONST])
    for n in ("wqT", "wkT", "wvT", "woT"):
        inp(n, [D, D])
    inp("bq_c", [128, KC]); inp("bo_c", [128, KC])
    inp("bk_r", [1, D]); inp("bv_r", [1, D])
    internal("kv_cc_in", [128, D]); internal("kv_cc_out", [128, D])
    for w, shape in (("g", [D, TC]), ("u", [D, TC]), ("d", [TC, D])):
        for k in range(4):
            inp(f"p_{w}{k}", shape)
    for w, shape in (("gu", [D, TC]), ("cd", [TC, D])):
        for k in range(4):
            inp(f"c_{w}{k}", shape)
    internal("aggu_in", [2 * D, TC], BF16)
    internal("aggu_out", [NCORES * 2 * D, TC], BF16)
    internal("agd_in", [TC, D], BF16)
    internal("agd_out", [T, D], BF16)
    inp("wq_a", [D, QR]); inp("qnw", [128, QR // 128]); inp("wq_b", [QR, D])
    inp("wkv_a", [D, KVR + ROPE]); inp("kvnw", [128, KVR // 128])
    inp("wkvb_n", [KVR, NH * 64]); inp("wkvb_v", [KVR, NH * VD])
    inp("wo", [D, D]); inp("cos_in", [32, TC]); inp("sin_in", [32, TC])
    inp("cv128", [128, NCONST])
    internal("agk_in", [NH * 128, TC], BF16)
    internal("agk_out", [NCORES * NH * 128, TC], BF16)
    internal("agv_in", [TC, NH * VD], BF16)
    internal("agv_out", [T, NH * VD], BF16)
    inp("wqgT", [D, HID], U8); inp("wquT", [D, HID], U8); inp("wqdT", [HID, D], U8)
    inp("sg_c", [128, HC]); inp("su_c", [128, HC]); inp("sd_c", [128, KC])
    internal("xmid1", [D, TC]); internal("xmid2", [D, TC]); internal("xmid3", [D, TC])
    dr["xout"] = nc.dram_tensor("xout", [D, TC], F32, kind="ExternalOutput").ap()
    return dr


def build_fused():
    nc = bass.Bass("TRN2", target_bir_lowering=False)
    es = contextlib.ExitStack()
    P = Plan(nc, es)
    dr = declare_drams(nc)
    with es:
        class B:
            pass
        make_common(P, nc, es, B)
        B.s_bar = P.sem("s_bar")
        emit_l0_attn(P, B, nc, dr["xin"], dr["wqT"], dr["wkT"], dr["wvT"],
                     dr["woT"], dr["bq_c"], dr["bo_c"], dr["bk_r"], dr["bv_r"],
                     dr["cv"], dr["xmid1"], dr["kv_cc_in"], dr["kv_cc_out"])
        phase_barrier(P, B)
        spl = {k: dr[k] for k in dr if k.startswith(("p_g", "p_u", "p_d", "c_gu", "c_cd"))}
        emit_l0_ffn(P, B, nc, dr["xmid1"], dr["cv"], spl, dr["xmid2"],
                    dr["aggu_in"], dr["aggu_out"], dr["agd_in"], dr["agd_out"])
        phase_barrier(P, B)
        emit_l1_attn(P, B, nc, dr["xmid2"], dr["wq_a"], dr["qnw"], dr["wq_b"],
                     dr["wkv_a"], dr["kvnw"], dr["wkvb_n"], dr["wkvb_v"],
                     dr["wo"], dr["cos_in"], dr["sin_in"], dr["cv"],
                     dr["cv128"], dr["xmid3"],
                     dr["agk_in"], dr["agk_out"], dr["agv_in"], dr["agv_out"])
        phase_barrier(P, B)
        emit_l1_ffn(P, B, nc, dr["xmid3"], dr["wqgT"], dr["wquT"], dr["wqdT"],
                    dr["sg_c"], dr["su_c"], dr["sd_c"], dr["cv"], dr["xout"])
        P.build()
    return nc


def kernel(**inputs):
    x = inputs["L0_attn_x"].astype(np.float32, copy=False)
    xs = _shard_x(x)
    pa = prep_l0_attn(inputs)
    pb = prep_l0_ffn(inputs)
    pc = prep_l1_attn(inputs)
    pd = prep_l1_ffn(inputs)
    in_maps = []
    for c in range(NCORES):
        m = {}
        m.update(pa[c]); m.update(pb[c]); m.update(pc[c]); m.update(pd[c])
        m["xin"] = xs[c]
        in_maps.append(m)
    nc = build_fused()
    res = run_bass_kernel_spmd(nc, in_maps, core_ids=list(range(NCORES)))
    out = np.concatenate([r["xout"].T[None] for r in res.results], axis=1)
    return np.ascontiguousarray(out.reshape(1, T, D))


# version 03
# speedup vs baseline: 1.0342x; 1.0342x over previous
import contextlib
import numpy as np
import ml_dtypes
import concourse.bass as bass
import concourse.mybir as mybir
from concourse.bass_utils import run_bass_kernel_spmd

F32 = mybir.dt.float32
BF16 = mybir.dt.bfloat16
FP16 = mybir.dt.float16
U8 = mybir.dt.uint8
AF = mybir.ActivationFunctionType
ALU = mybir.AluOpType

NCORES = 8
T = 8192
TC = T // NCORES
D = 2048
KC = D // 128
BFD = np.dtype(ml_dtypes.bfloat16)

CV_EPS_LN = 0
CV_NHALF = 1
CV_INV_D = 2
CV_KVSCALE = 3
CV_EPS_RMS = 4
CV_INV_QR = 5
CV_INV_KVR = 6
CV_ATTNSCALE = 7
NCONST = 8
CV_VALS = np.array([[1e-5, -0.5, 1.0 / 2048.0, 1.0 / np.sqrt(128.0),
                     float(np.finfo(np.float32).eps), 1.0 / 768.0,
                     1.0 / 512.0, 128.0 ** -0.5]], dtype=np.float32)


class Sem:
    __slots__ = ("h", "n", "name")

    def __init__(self, h, name):
        self.h = h
        self.n = 0
        self.name = name


class Plan:

    ENGINES = ("sync", "tensor", "vector", "scalar", "gpsimd")

    def __init__(self, nc, es):
        self.nc = nc
        self.es = es
        self.ops = {e: [] for e in self.ENGINES}

    def sem(self, name):
        return Sem(self.es.enter_context(self.nc.semaphore(name)), name)

    def emit(self, eng, fn):
        self.ops[eng].append(fn)

    def wait(self, eng, sem, val=None):
        v = sem.n if val is None else val
        if v > 0:
            self.emit(eng, lambda e, s=sem.h, v=v: e.wait_ge(s, v))

    def dma(self, eng, out, in_, sem=None, **kw):
        tgt = None
        if sem is not None:
            sem.n += 16
            tgt = sem.n
        def fn(e, out=out, in_=in_, sem=sem, kw=kw):
            ins = e.dma_start(out=out, in_=in_, **kw)
            if sem is not None:
                ins.then_inc(sem.h, 16)
        self.emit(eng, fn)
        return tgt

    def op(self, eng, fn, sem=None, inc=1):
        tgt = None
        if sem is not None:
            sem.n += inc
            tgt = sem.n
        def wrapped(e, fn=fn, sem=sem, inc=inc):
            ins = fn(e)
            if sem is not None:
                ins.then_inc(sem.h, inc)
        self.emit(eng, wrapped)
        return tgt

    def build(self):
        with self.nc.Block() as block:
            for eng_name in self.ENGINES:
                ops = self.ops[eng_name]
                if not ops:
                    continue
                def body(e, ops=ops):
                    for fn in ops:
                        fn(e)
                getattr(block, eng_name)(body)



def bank(B, i, parts=128, width=512):
    return B.pool[0:parts, i * 512:i * 512 + width]


def chunked(dram_ap):
    return dram_ap.rearrange("(k p) t -> p k t", p=128)


def sb3(sbuf_ap, n):
    return sbuf_ap.rearrange("p (k t) -> p k t", k=n)


def emit_ln_stats(P, B, src, th, n_chunks, inv_n, out_m, out_r,
                  eps_col, cast_first):
    nc = P.nc
    sA, sT, sV = B.s_act, B.s_mm, B.s_dve
    want_m = out_m is not None
    for k in range(n_chunks):
        slot = k % 2
        a = B.scr_a[:, slot * 512:(slot + 1) * 512]
        b = B.scr_b[:, slot * 512:(slot + 1) * 512]
        guard = B.g.get(("lnmm", slot), 0)
        if cast_first:
            P.wait("vector", sT, guard)
            P.op("vector", lambda e, a=a, s=src(k): e.tensor_copy(a, s), sV)
            cast_tgt = sV.n
            xslot = B.g.pop("_cur_xs_slot", None)
            if xslot is not None:
                B.g[("xs", xslot)] = cast_tgt
            P.wait("scalar", sV, cast_tgt)
            stat_in = a
        else:
            stat_in = src(k)
            P.wait("scalar", sT, guard)
        P.op("scalar", lambda e, b=b, s=stat_in: e.activation(b, s, AF.Square), sA)
        sq_tgt = sA.n
        if cast_first:
            P.wait("tensor", sV, cast_tgt)
        st = (k == 0)
        sp = (k == n_chunks - 1)
        if st:
            P.wait("tensor", sV, B.g.get("ln_stats_free", 0))
        if want_m:
            P.op("tensor", lambda e, s=stat_in, st=st, sp=sp:
                 e.matmul(bank(B, 0, parts=1), B.ones_col[:, :], s, start=st, stop=sp))
        P.wait("tensor", sA, sq_tgt)
        P.op("tensor", lambda e, b=b, st=st, sp=sp:
             e.matmul(bank(B, 1, parts=1), B.ones_col[:, :], b, start=st, stop=sp), sT)
        B.g[("lnmm", slot)] = sT.n
    mm_done = sT.n
    ts = th * 512
    m_row = B.m_row[0:1, ts:ts + 512]
    v_row = B.v_row[0:1, ts:ts + 512]
    r_row = B.r_row[0:1, ts:ts + 512]
    m_bf = B.m_bf[0:1, ts:ts + 512]
    r_bf = B.r_bf[0:1, ts:ts + 512]
    P.wait("vector", sT, mm_done)
    P.wait("vector", sT, B.g.get(("row_free", th), 0))
    if want_m:
        P.op("vector", lambda e, m=m_row: e.tensor_scalar(
            m, bank(B, 0, parts=1), inv_n, None, ALU.mult), sV)
        P.op("vector", lambda e, m=m_row, v=v_row: e.tensor_mul(v, m, m), sV)
        P.op("vector", lambda e, v=v_row: e.scalar_tensor_tensor(
            v, bank(B, 1, parts=1), inv_n, v, ALU.mult, ALU.subtract), sV)
    else:
        P.op("vector", lambda e, v=v_row: e.tensor_scalar(
            v, bank(B, 1, parts=1), inv_n, None, ALU.mult), sV)
    B.g["ln_stats_free"] = sV.n
    P.wait("scalar", sV, sV.n)
    P.op("scalar", lambda e, r=r_row, v=v_row, b=eps_col:
         e.activation(r, v, AF.Ln, bias=b), sA)
    P.op("scalar", lambda e, r=r_row: e.activation(
        r, r, AF.Exp, scale=B.cv[0:1, CV_NHALF:CV_NHALF + 1]), sA)
    P.wait("vector", sA, sA.n)
    if want_m:
        P.op("vector", lambda e, m=m_row, mb=m_bf: e.tensor_copy(mb, m), sV)
    P.op("vector", lambda e, r=r_row, rb=r_bf: e.tensor_copy(rb, r), sV)
    rows_done = sV.n
    P.wait("tensor", sV, rows_done)
    P.wait("tensor", sV, B.g.get("ln_bc_free", 0))
    if want_m:
        P.op("tensor", lambda e, mb=m_bf: e.matmul(
            bank(B, 2), B.ones_row[0:1, 0:128], mb, start=True, stop=True))
    P.op("tensor", lambda e, rb=r_bf: e.matmul(
        bank(B, 3), B.ones_row[0:1, 0:128], rb, start=True, stop=True), sT)
    bc_done = sT.n
    B.g[("row_free", th)] = bc_done
    P.wait("vector", sT, bc_done)
    if want_m:
        P.op("vector", lambda e, o=out_m: e.tensor_copy(o, bank(B, 2)), sV)
    P.op("vector", lambda e, o=out_r: e.tensor_copy(o, bank(B, 3)), sV)
    B.g["ln_bc_free"] = sV.n
    return sV.n


def emit_proj(P, B, n_og, wget, rhs_get, out_put, n_k, kstride, bias_col=None,
              scale_col=None, act=None, corr=None, og_cols=512, rhs_parts=128):
    act = act or AF.Identity
    n_oc = og_cols // 128
    P.wait("gpsimd", B.s_mm, B.s_mm.n)
    tgts = {0: wget(0, 0)}
    if n_og > 1:
        tgts[1] = wget(1, 1)
    for og in range(n_og):
        slot = og % 2
        P.wait("tensor", B.s_w[og % 2], tgts[og])
        for oc in range(n_oc):
            o = og * n_oc + oc
            for th in range(2):
                bk = (oc * 2 + th) % 2
                pm = bank(B, 4 + bk)
                prev = B.g.get(("proj_evac", bk), 0)
                P.wait("tensor", B.s_act, prev)
                mms = []
                ss = B.slab_stride
                for k in range(n_k):
                    lhsT = B.wslab[:, slot * ss + k * kstride + oc * 128:
                                   slot * ss + k * kstride + oc * 128 + 128]
                    mms.append((lhsT, rhs_get(k, th)))
                if corr is not None:
                    mms.extend(corr(o, th))
                for i, (l, r) in enumerate(mms):
                    last = (i == len(mms) - 1)
                    P.op("tensor", lambda e, p=pm, l=l, r=r, st=(i == 0), sp=last:
                         e.matmul(p, l, r, start=st, stop=sp),
                         B.s_mm if last else None)
                mm_tgt = B.s_mm.n
                P.wait("scalar", B.s_mm, mm_tgt)
                kw = {}
                if bias_col is not None:
                    kw["bias"] = bias_col(o)
                if scale_col is not None:
                    kw["scale"] = scale_col(o)
                P.op("scalar", lambda e, d=out_put(o, th), p=pm, kw=kw:
                     e.activation(d, p, act, **kw), B.s_act)
                B.g[("proj_evac", bk)] = B.s_act.n
        if og + 2 < n_og:
            P.wait("gpsimd", B.s_mm, B.s_mm.n)
            tgts[og + 2] = wget(og + 2, slot)
    return B.s_act.n


def emit_res_ln(P, B, ybuf_get, x_get, n_chunks, eps_col, inv_n):
    for th in range(2):
        emit_ln_stats(P, B, lambda k, th=th: ybuf_get(k, th), th, n_chunks,
                      inv_n, B.m_b[:, th * 512:(th + 1) * 512],
                      B.r_b[:, th * 512:(th + 1) * 512], eps_col, cast_first=False)
        for k in range(n_chunks):
            y_ap = ybuf_get(k, th)
            x_ap = x_get(k, th)
            m_ap = B.m_b[:, th * 512:(th + 1) * 512]
            r_ap = B.r_b[:, th * 512:(th + 1) * 512]
            P.op("vector", lambda e, y=y_ap, m=m_ap: e.tensor_sub(B.t1[:, :], y, m))
            P.op("vector", lambda e, r=r_ap: e.tensor_mul(B.t1[:, :], B.t1[:, :], r))
            P.op("vector", lambda e, x=x_ap: e.tensor_add(x, x, B.t1[:, :]), B.s_dve)
    return B.s_dve.n


def emit_inner_ln(P, B, x_get, xn_put, n_chunks, eps_col, inv_n):
    done = []
    for th in range(2):
        emit_ln_stats(P, B, lambda k, th=th: x_get(k, th), th, n_chunks,
                      inv_n, B.m_b[:, th * 512:(th + 1) * 512],
                      B.r_b[:, th * 512:(th + 1) * 512], eps_col, cast_first=True)
        for k in range(n_chunks):
            x_ap = x_get(k, th)
            xn_ap = xn_put(k, th)
            m_ap = B.m_b[:, th * 512:(th + 1) * 512]
            r_ap = B.r_b[:, th * 512:(th + 1) * 512]
            P.op("vector", lambda e, x=x_ap, m=m_ap: e.tensor_sub(B.t1[:, :], x, m))
            P.op("vector", lambda e, o=xn_ap, r=r_ap:
                 e.tensor_mul(o, B.t1[:, :], r), B.s_dve)
        done.append(B.s_dve.n)
    return done


def make_common(P, nc, es, B):
    sb = lambda name, shape, dt: es.enter_context(nc.sbuf_tensor(name, shape, dt))
    B.g = {}
    B.pool = es.enter_context(nc.psum_tensor("pool", [128, 4096], F32))
    B.scr_a = sb("scr_a", [128, 1024], BF16)
    B.scr_b = sb("scr_b", [128, 1024], BF16)
    B.t1 = sb("t1", [128, 512], F32)
    B.m_b = sb("m_b", [128, 1024], F32)
    B.r_b = sb("r_b", [128, 1024], F32)
    B.m_row = sb("m_row", [1, 1024], F32)
    B.v_row = sb("v_row", [1, 1024], F32)
    B.r_row = sb("r_row", [1, 1024], F32)
    B.m_bf = sb("m_bf", [1, 1024], BF16)
    B.r_bf = sb("r_bf", [1, 1024], BF16)
    B.ones_col = sb("ones_col", [128, 1], BF16)
    B.ones_row = sb("ones_row", [1, 512], BF16)
    B.cv = sb("cv_sb", [1, NCONST], F32)
    B.s_x = P.sem("s_x")
    B.s_w = [P.sem("s_w0"), P.sem("s_w1")]
    B.s_dve = P.sem("s_dve")
    B.s_act = P.sem("s_act")
    B.s_mm = P.sem("s_mm")
    B.s_cc = P.sem("s_cc")
    B.s_st = P.sem("s_st")
    P.op("vector", lambda e: e.memset(B.ones_col[:, :], 1.0))
    P.op("vector", lambda e: e.memset(B.ones_row[:, :], 1.0), B.s_dve)
    B.init_done = B.s_dve.n
    return B.init_done



def emit_l0_attn(P, B, nc, xin, wqT, wkT, wvT, woT, bq_c, bo_c, bk_r, bv_r,
                 cv_in, xout, kv_cc_in, kv_cc_out):
    es2 = contextlib.ExitStack()
    if True:
        sb = lambda name, shape, dt: es2.enter_context(
            nc.sbuf_tensor("A_" + name, shape, dt))
        init_done = B.init_done

        B.xn = sb("xn", [128, KC * TC], BF16)
        B.xs = sb("xs", [128, 4 * 512], F32)
        B.oscr = sb("oscr", [128, 2 * 512], F32)
        B.qb = sb("qb", [128, KC * TC], BF16)
        B.wslab = sb("wslab", [128, 2 * 4096], BF16)
        B.slab_stride = 4096
        B.kst = sb("kst", [128, 2 * 128], BF16)
        B.vst = sb("vst", [128, 2 * 128], BF16)
        B.kvacc = sb("kvacc", [128, D], F32)
        B.kvb = sb("kvb", [128, D], BF16)
        B.bqc = sb("bqc", [128, KC], F32)
        B.boc = sb("boc", [128, KC], F32)
        B.bkr = sb("bkr", [1, D], BF16)
        B.bvr = sb("bvr", [1, D], BF16)

        P.dma("sync", B.cv[:, :], cv_in, B.s_x)
        P.dma("sync", B.bqc[:, :], bq_c, B.s_x)
        P.dma("sync", B.boc[:, :], bo_c, B.s_x)
        P.dma("gpsimd", B.bkr[:, :], bk_r, B.s_x)
        P.dma("gpsimd", B.bvr[:, :], bv_r, B.s_x)
        x_loaded = B.s_x.n

        eps_ln = B.cv[0:1, CV_EPS_LN:CV_EPS_LN + 1]
        inv_d = 1.0 / 2048.0

        def slab_load(w, og, slot, cols=256):
            B.sp_hook(2)
            src = chunked(w[:, og * cols:(og + 1) * cols])
            return P.dma("gpsimd",
                         sb3(B.wslab[:, slot * 4096: slot * 4096 + KC * cols], KC),
                         src, B.s_w[slot])

        P.wait("vector", B.s_x, x_loaded)
        P.wait("scalar", B.s_x, x_loaded)
        P.wait("tensor", B.s_dve, init_done)

        xst = XStream(P, B, xin)

        def stats_src(k, th):
            ap, slot = xst.get(k, th)
            B.g["_cur_xs_slot"] = slot
            return ap

        def xn_put(k, th):
            return B.xn[:, k * TC + th * 512: k * TC + (th + 1) * 512]

        xn_done = []
        for th in range(2):
            emit_ln_stats(P, B, lambda k, th=th: stats_src(k, th), th, KC,
                          inv_d, B.m_b[:, th * 512:(th + 1) * 512],
                          B.r_b[:, th * 512:(th + 1) * 512], eps_ln,
                          cast_first=True)
            for k in range(KC):
                ap, slot = xst.get(k, th)
                m_ap = B.m_b[:, th * 512:(th + 1) * 512]
                r_ap = B.r_b[:, th * 512:(th + 1) * 512]
                P.op("vector", lambda e, x=ap, m=m_ap: e.tensor_sub(B.t1[:, :], x, m))
                t = P.op("vector", lambda e, o=xn_put(k, th), r=r_ap:
                         e.tensor_mul(o, B.t1[:, :], r), B.s_dve)
                xst.mark(slot, t)
            xn_done.append(B.s_dve.n)

        P.wait("tensor", B.s_dve, xn_done[1])

        def q_out(o, th):
            return B.qb[:, o * TC + th * 512: o * TC + (th + 1) * 512]

        emit_proj(P, B, 8, lambda og, slot: slab_load(wqT, og, slot),
                  lambda k, th: xn_put(k, th), q_out, KC, 256,
                  bias_col=lambda o: B.bqc[:, o:o + 1], og_cols=256)
        q_all = B.s_act.n

        def kv_slab_load(dg, slot):
            B.sp_hook(2)
            P.dma("gpsimd", sb3(B.wslab[:, slot * 4096: slot * 4096 + KC * 128], KC),
                  chunked(wkT[:, dg * 128:(dg + 1) * 128]), B.s_w[slot])
            return P.dma("gpsimd",
                         sb3(B.wslab[:, slot * 4096 + 2048:
                                     slot * 4096 + 2048 + KC * 128], KC),
                         chunked(wvT[:, dg * 128:(dg + 1) * 128]), B.s_w[slot])

        P.wait("gpsimd", B.s_act, q_all)
        kvt = {0: kv_slab_load(0, 0), 1: kv_slab_load(1, 1)}
        P.wait("tensor", B.s_act, q_all)
        for dg in range(16):
            slot = dg % 2
            P.wait("tensor", B.s_w[dg % 2], kvt[dg])
            P.wait("tensor", B.s_dve, B.g.get("kva_evac", 0))
            for s in range(8):
                sbk = s % 2
                for which, base, pm, dst, brow, gkey in (
                    ("k", 0, bank(B, 4 + sbk, width=128),
                     B.kst[:, sbk * 128:(sbk + 1) * 128], B.bkr, ("kprod", sbk)),
                    ("v", 2048, bank(B, 2 + sbk, width=128),
                     B.vst[:, sbk * 128:(sbk + 1) * 128], B.bvr, ("vprod", sbk)),
                ):
                    P.wait("tensor", B.s_act, B.g.get(gkey, 0))
                    for k in range(KC):
                        lhsT = B.xn[:, k * TC + s * 128: k * TC + s * 128 + 128]
                        rhs = B.wslab[:, slot * 4096 + base + k * 128:
                                      slot * 4096 + base + k * 128 + 128]
                        P.op("tensor", lambda e, p=pm, l=lhsT, r=rhs, st=(k == 0):
                             e.matmul(p, l, r, start=st, stop=False))
                    P.op("tensor", lambda e, p=pm, b=brow, dg=dg: e.matmul(
                        p, B.ones_row[0:1, 0:128], b[0:1, dg * 128:(dg + 1) * 128],
                        start=False, stop=True), B.s_mm)
                    mm_tgt = B.s_mm.n
                    P.wait("scalar", B.s_mm, mm_tgt)
                    P.op("scalar", lambda e, d=dst, p=pm:
                         e.activation(d, p, AF.Identity), B.s_act)
                    B.g[gkey] = B.s_act.n
                P.wait("tensor", B.s_act, B.g[("vprod", sbk)])
                lhsT = B.kst[:, sbk * 128:(sbk + 1) * 128]
                rhs = B.vst[:, sbk * 128:(sbk + 1) * 128]
                pm = B.pool[:, 6 * 512: 6 * 512 + 128]
                P.op("tensor", lambda e, p=pm, l=lhsT, r=rhs, st=(s == 0),
                     sp=(s == 7): e.matmul(p, l, r, start=st, stop=sp),
                     B.s_mm if s == 7 else None)
            P.wait("vector", B.s_mm, B.s_mm.n)
            P.op("vector", lambda e, dg=dg: e.tensor_copy(
                B.kvacc[:, dg * 128:(dg + 1) * 128], bank(B, 6, width=128)), B.s_dve)
            B.g["kva_evac"] = B.s_dve.n
            if dg + 2 < 16:
                P.wait("gpsimd", B.s_mm, B.s_mm.n)
                kvt[dg + 2] = kv_slab_load(dg + 2, slot)

        P.wait("sync", B.s_dve, B.g["kva_evac"])
        P.dma("sync", kv_cc_in, B.kvacc[:, :], B.s_st)
        P.wait("gpsimd", B.s_st, B.s_st.n)
        P.op("gpsimd", lambda e: e.collective_compute(
            "AllReduce", ALU.add, replica_groups=[list(range(NCORES))],
            ins=[kv_cc_in], outs=[kv_cc_out]), B.s_cc)
        P.wait("sync", B.s_cc, B.s_cc.n)
        P.dma("sync", B.kvacc[:, :], kv_cc_out, B.s_x)
        P.wait("vector", B.s_x, B.s_x.n)
        P.op("vector", lambda e: e.tensor_scalar(
            B.kvb[:, :], B.kvacc[:, :], float(128.0 ** -0.5),
            None, ALU.mult), B.s_dve)
        kvb_done = B.s_dve.n

        P.wait("tensor", B.s_dve, kvb_done)
        for h in range(16):
            for th in range(2):
                bk = (h * 2 + th) % 2
                pm = bank(B, 4 + bk)
                P.wait("tensor", B.s_act, B.g.get(("attn", bk), 0))
                lhsT = B.kvb[:, h * 128:(h + 1) * 128]
                rhs = B.qb[:, h * TC + th * 512: h * TC + (th + 1) * 512]
                P.op("tensor", lambda e, p=pm, l=lhsT, r=rhs:
                     e.matmul(p, l, r, start=True, stop=True), B.s_mm)
                P.wait("scalar", B.s_mm, B.s_mm.n)
                dst = B.qb[:, h * TC + th * 512: h * TC + (th + 1) * 512]
                P.op("scalar", lambda e, d=dst, p=pm:
                     e.activation(d, p, AF.Identity), B.s_act)
                B.g[("attn", bk)] = B.s_act.n
        attn_all = B.s_act.n

        P.wait("gpsimd", B.s_act, attn_all)
        P.wait("tensor", B.s_act, attn_all)

        def y_put(o, th):
            return B.xn[:, o * TC + th * 512: o * TC + (th + 1) * 512]

        emit_proj(P, B, 8, lambda og, slot: slab_load(woT, og, slot),
                  lambda k, th: B.qb[:, k * TC + th * 512: k * TC + (th + 1) * 512],
                  y_put, KC, 256, bias_col=lambda o: B.boc[:, o:o + 1], og_cols=256)
        y_all = B.s_act.n

        P.wait("vector", B.s_act, y_all)
        P.wait("scalar", B.s_act, y_all)
        P.wait("tensor", B.s_act, y_all)
        s_out = [P.sem("s_outA0"), P.sem("s_outA1")]
        for th in range(2):
            emit_ln_stats(P, B, lambda k, th=th: y_put(k, th), th, KC,
                          inv_d, B.m_b[:, th * 512:(th + 1) * 512],
                          B.r_b[:, th * 512:(th + 1) * 512], eps_ln,
                          cast_first=False)
            for k in range(KC):
                m_ap = B.m_b[:, th * 512:(th + 1) * 512]
                r_ap = B.r_b[:, th * 512:(th + 1) * 512]
                osl = k % 2
                o_ap = B.oscr[:, osl * 512:(osl + 1) * 512]
                ap, slot = xst.get(k, th)
                P.op("vector", lambda e, y=y_put(k, th), m=m_ap:
                     e.tensor_sub(B.t1[:, :], y, m))
                P.op("vector", lambda e, r=r_ap: e.tensor_mul(B.t1[:, :], B.t1[:, :], r))
                P.wait("vector", s_out[osl], B.g.get(("oscrA", osl), 0))
                t = P.op("vector", lambda e, o=o_ap, x=ap:
                         e.tensor_add(o, x, B.t1[:, :]), B.s_dve)
                xst.mark(slot, t)
                P.wait("sync", B.s_dve, t)
                B.g[("oscrA", osl)] = P.dma(
                    "sync", xout[k * 128:(k + 1) * 128, th * 512:(th + 1) * 512],
                    o_ap, s_out[osl])
        P.wait("sync", s_out[0], s_out[0].n)
        P.wait("sync", s_out[1], s_out[1].n)
    es2.close()



INTER = 8192
IG = 8
ICG = INTER // 128 // IG


class XStream:

    COUNT = [0]

    def __init__(self, P, B, xin, nslots=4):
        self.P, self.B, self.xin = P, B, xin
        self.nslots = nslots
        self.i = 0
        uid = XStream.COUNT[0]
        XStream.COUNT[0] += 1
        self.sems = [P.sem(f"s_xs{uid}_{j}") for j in range(nslots)]

    def get(self, k, th, waiters=("vector",)):
        P, B = self.P, self.B
        slot = self.i % self.nslots
        self.i += 1
        ap = B.xs[:, slot * 512:(slot + 1) * 512]
        P.wait("sync", B.s_dve, B.g.get(("xs", slot), 0))
        src = self.xin[k * 128:(k + 1) * 128, th * 512:(th + 1) * 512]
        tgt = P.dma("sync", ap, src, self.sems[slot])
        for w in waiters:
            P.wait(w, self.sems[slot], tgt)
        return ap, slot

    def mark(self, slot, tgt):
        self.B.g[("xs", slot)] = tgt




class Spline:

    def __init__(self, P, B, nc, es, spl, aggu_in, agd_in):
        self.P, self.B = P, B
        self.aggu_in, self.agd_in = aggu_in, agd_in
        sb = lambda name, shape, dt: es.enter_context(
            nc.sbuf_tensor("SP_" + name, shape, dt))
        self.spst = sb("spst", [128, 2 * 8 * TC], FP16)
        self.sp16 = sb("sp16", [128, 2 * TC], FP16)
        self.wout = sb("wout", [128, 2 * TC], BF16)
        self.s_sp = [P.sem("s_spl0"), P.sem("s_spl1")]
        self.s_agw = [P.sem("s_agwl0"), P.sem("s_agwl1")]
        self.chunks = []
        for hc in range(KC):
            self.chunks.append((
                [spl[f"p_g{k}"][hc * 128:(hc + 1) * 128, :] for k in range(4)]
                + [spl[f"c_gu{k}"][hc * 128:(hc + 1) * 128, :] for k in range(4)],
                aggu_in[hc * 128:(hc + 1) * 128, :]))
        for hc in range(KC):
            self.chunks.append((
                [spl[f"p_u{k}"][hc * 128:(hc + 1) * 128, :] for k in range(4)]
                + [spl[f"c_gu{k}"][hc * 128:(hc + 1) * 128, :] for k in range(4)],
                aggu_in[D + hc * 128: D + (hc + 1) * 128, :]))
        for ic in range(TC // 128):
            for h in range(2):
                cs_ = slice(h * 1024, (h + 1) * 1024)
                self.chunks.append((
                    [spl[f"p_d{k}"][ic * 128:(ic + 1) * 128, cs_] for k in range(4)]
                    + [spl[f"c_cd{k}"][ic * 128:(ic + 1) * 128, cs_] for k in range(4)],
                    agd_in[ic * 128:(ic + 1) * 128, cs_]))
        self.n = 0

    def hook(self, k=1):
        P, B = self.P, self.B
        for _ in range(k):
            if self.n >= len(self.chunks):
                return
            srcs, dst = self.chunks[self.n]
            sl = self.n % 2
            self.n += 1
            for j, sap in enumerate(srcs):
                P.dma("gpsimd", self.spst[:, sl * 8 * TC + j * TC:
                                          sl * 8 * TC + (j + 1) * TC],
                      sap, self.s_sp[sl])
            P.wait("gpsimd", self.s_sp[sl], self.s_sp[sl].n)
            pv = lambda i, sl=sl: self.spst[:, sl * 8 * TC + i * TC:
                                            sl * 8 * TC + (i + 1) * TC]
            ta = self.sp16[:, 0:TC]
            tb = self.sp16[:, TC:2 * TC]
            wo = self.wout[:, sl * TC:(sl + 1) * TC]
            P.op("gpsimd", lambda e, t=ta, x=pv(0), y=pv(4): e.tensor_mul(t, x, y))
            P.op("gpsimd", lambda e, t=tb, x=pv(1), y=pv(5): e.tensor_mul(t, x, y))
            P.op("gpsimd", lambda e, a=ta, b=tb: e.tensor_add(a, a, b))
            P.op("gpsimd", lambda e, t=tb, x=pv(2), y=pv(6): e.tensor_mul(t, x, y))
            P.op("gpsimd", lambda e, a=ta, b=tb: e.tensor_add(a, a, b))
            P.op("gpsimd", lambda e, t=tb, x=pv(3), y=pv(7): e.tensor_mul(t, x, y))
            P.wait("gpsimd", self.s_agw[sl], self.s_agw[sl].n)
            P.op("gpsimd", lambda e, w=wo, a=ta, b=tb: e.tensor_add(w, a, b))
            P.dma("gpsimd", dst, wo, self.s_agw[sl])

    def finish(self, s_cc, s_cc2):
        P, B = self.P, self.B
        self.hook(len(self.chunks))
        P.wait("gpsimd", self.s_agw[0], self.s_agw[0].n)
        P.wait("gpsimd", self.s_agw[1], self.s_agw[1].n)
        P.op("gpsimd", lambda e: e.collective_compute(
            "AllGather", ALU.bypass, replica_groups=[list(range(NCORES))],
            ins=[self.aggu_in], outs=[self.aggu_out_ap]), s_cc)
        P.op("gpsimd", lambda e: e.collective_compute(
            "AllGather", ALU.bypass, replica_groups=[list(range(NCORES))],
            ins=[self.agd_in], outs=[self.agd_out_ap]), s_cc2)


def emit_l0_ffn(P, B, nc, xin, cv_in, spl, xout, aggu_in, aggu_out,
                agd_in, agd_out, cc_waits=None):
    es2 = contextlib.ExitStack()
    if True:
        sb = lambda name, shape, dt: es2.enter_context(
            nc.sbuf_tensor("B_" + name, shape, dt))
        init_done = B.init_done

        B.xn = sb("xn", [128, KC * TC], BF16)
        B.inter = sb("inter", [128, ICG * TC], BF16)
        B.acc = sb("acc", [128, KC * TC], F32)
        B.wslab = sb("wslab", [128, 2 * 4096], BF16)
        B.slab_stride = 4096
        B.xs = sb("xs", [128, 4 * 512], F32)
        B.dslab = sb("dslab", [128, 2 * 1024], BF16)
        s_wd = [P.sem("s_wd0"), P.sem("s_wd1")]

        eps_ln = B.cv[0:1, CV_EPS_LN:CV_EPS_LN + 1]
        inv_d = 1.0 / 2048.0

        xst = XStream(P, B, xin)

        def stats_src(k, th):
            ap, slot = xst.get(k, th)
            B.g["_cur_xs_slot"] = slot
            return ap

        def xn_put(k, th):
            return B.xn[:, k * TC + th * 512: k * TC + (th + 1) * 512]

        xn_done = []
        for th in range(2):
            emit_ln_stats(P, B, lambda k, th=th: stats_src(k, th), th, KC,
                          inv_d, B.m_b[:, th * 512:(th + 1) * 512],
                          B.r_b[:, th * 512:(th + 1) * 512], eps_ln,
                          cast_first=True)
            for k in range(KC):
                ap, slot = xst.get(k, th)
                m_ap = B.m_b[:, th * 512:(th + 1) * 512]
                r_ap = B.r_b[:, th * 512:(th + 1) * 512]
                P.op("vector", lambda e, x=ap, m=m_ap: e.tensor_sub(B.t1[:, :], x, m))
                t = P.op("vector", lambda e, o=xn_put(k, th), r=r_ap:
                         e.tensor_mul(o, B.t1[:, :], r), B.s_dve)
                xst.mark(slot, t)
            xn_done.append(B.s_dve.n)

        for sem, tgt in cc_waits:
            P.wait("sync", sem, tgt)

        P.wait("tensor", B.s_dve, xn_done[1])

        def gu_slab_load(g, ic, slot):
            i0 = ic * 128
            gsrc = aggu_out[g * 2 * D: g * 2 * D + D, i0:i0 + 128].rearrange(
                "(k p) j -> p k j", p=128)
            usrc = aggu_out[g * 2 * D + D: (g + 1) * 2 * D, i0:i0 + 128].rearrange(
                "(k p) j -> p k j", p=128)
            P.dma("sync", sb3(B.wslab[:, slot * 4096: slot * 4096 + 2048], KC),
                  gsrc, B.s_w[slot])
            return P.dma("sync", sb3(B.wslab[:, slot * 4096 + 2048:
                                             slot * 4096 + 4096], KC), usrc, B.s_w[slot])

        inter_free = 0
        for g in range(IG):
            for ic in range(ICG):
                slot = ic % 2
                P.wait("sync", B.s_mm, B.g.get(("gu_slab", slot), 0))
                wt = gu_slab_load(g, ic, slot)
                P.wait("tensor", B.s_w[slot], wt)
                if ic == 0:
                    P.wait("tensor", B.s_mm, inter_free)
                for th in range(2):
                    gbk = (ic * 2 + th) % 2
                    pg = bank(B, 4 + gbk)
                    pu = bank(B, 2 + gbk)
                    P.wait("tensor", B.s_act, B.g.get(("gact", gbk), 0))
                    P.wait("tensor", B.s_dve, B.g.get(("uact", gbk), 0))
                    for k in range(KC):
                        lg = B.wslab[:, slot * 4096 + k * 128: slot * 4096 + (k + 1) * 128]
                        P.op("tensor", lambda e, p=pg, l=lg, r=xn_put(k, th),
                             st=(k == 0), sp=(k == KC - 1):
                             e.matmul(p, l, r, start=st, stop=sp),
                             B.s_mm if k == KC - 1 else None)
                    gmm = B.s_mm.n
                    for k in range(KC):
                        lu = B.wslab[:, slot * 4096 + 2048 + k * 128:
                                     slot * 4096 + 2048 + (k + 1) * 128]
                        P.op("tensor", lambda e, p=pu, l=lu, r=xn_put(k, th),
                             st=(k == 0), sp=(k == KC - 1):
                             e.matmul(p, l, r, start=st, stop=sp),
                             B.s_mm if k == KC - 1 else None)
                    umm = B.s_mm.n
                    sg = B.scr_a[:, th * 512:(th + 1) * 512]
                    P.wait("scalar", B.s_mm, gmm)
                    P.wait("scalar", B.s_dve, B.g.get(("sg", th), 0))
                    P.op("scalar", lambda e, d=sg, p=pg:
                         e.activation(d, p, AF.Silu), B.s_act)
                    B.g[("gact", gbk)] = B.s_act.n
                    it = B.inter[:, ic * TC + th * 512: ic * TC + (th + 1) * 512]
                    P.wait("vector", B.s_mm, umm)
                    P.wait("vector", B.s_act, B.g[("gact", gbk)])
                    P.op("vector", lambda e, d=it, s=sg, p=pu:
                         e.tensor_mul(d, s, p), B.s_dve)
                    B.g[("uact", gbk)] = B.s_dve.n
                    B.g[("sg", th)] = B.s_dve.n
                B.g[("gu_slab", slot)] = B.s_mm.n
            inter_done = B.s_dve.n
            P.wait("tensor", B.s_dve, inter_done)
            for o in range(KC):
                dslot = o % 2
                P.wait("sync", B.s_mm, B.g.get(("d_slab", dslot), 0))
                dsrc = agd_out[g * 1024: (g + 1) * 1024, o * 128:(o + 1) * 128]
                dt_ = P.dma("sync",
                            sb3(B.dslab[:, dslot * 1024:(dslot + 1) * 1024], ICG),
                            dsrc.rearrange("(k p) j -> p k j", p=128), s_wd[dslot])
                P.wait("tensor", s_wd[dslot], dt_)
                for th in range(2):
                    dbk = (o * 2 + th) % 2
                    pd = bank(B, 6 + dbk)
                    P.wait("tensor", B.s_dve, B.g.get(("dacc", dbk), 0))
                    for ic in range(ICG):
                        ld = B.dslab[:, dslot * 1024 + ic * 128:
                                     dslot * 1024 + (ic + 1) * 128]
                        rh = B.inter[:, ic * TC + th * 512: ic * TC + (th + 1) * 512]
                        P.op("tensor", lambda e, p=pd, l=ld, r=rh, st=(ic == 0),
                             sp=(ic == ICG - 1): e.matmul(p, l, r, start=st, stop=sp),
                             B.s_mm if ic == ICG - 1 else None)
                    dmm = B.s_mm.n
                    a_ap = B.acc[:, o * TC + th * 512: o * TC + (th + 1) * 512]
                    P.wait("vector", B.s_mm, dmm)
                    if g == 0:
                        P.op("vector", lambda e, a=a_ap, p=pd: e.tensor_copy(a, p),
                             B.s_dve)
                    else:
                        P.op("vector", lambda e, a=a_ap, p=pd: e.tensor_add(a, a, p),
                             B.s_dve)
                    B.g[("dacc", dbk)] = B.s_dve.n
                B.g[("d_slab", dslot)] = B.s_mm.n
            inter_free = B.s_mm.n

        acc_done = B.s_dve.n
        P.wait("vector", B.s_dve, acc_done)
        P.wait("scalar", B.s_dve, acc_done)
        P.wait("tensor", B.s_dve, acc_done)

        def acc_get(k, th):
            return B.acc[:, k * TC + th * 512: k * TC + (th + 1) * 512]

        res_done = 0
        for th in range(2):
            emit_ln_stats(P, B, lambda k, th=th: acc_get(k, th), th, KC,
                          inv_d, B.m_b[:, th * 512:(th + 1) * 512],
                          B.r_b[:, th * 512:(th + 1) * 512], eps_ln,
                          cast_first=True)
            for k in range(KC):
                a_ap = acc_get(k, th)
                m_ap = B.m_b[:, th * 512:(th + 1) * 512]
                r_ap = B.r_b[:, th * 512:(th + 1) * 512]
                ap, slot = xst.get(k, th)
                P.op("vector", lambda e, a=a_ap, m=m_ap: e.tensor_sub(B.t1[:, :], a, m))
                P.op("vector", lambda e, r=r_ap: e.tensor_mul(B.t1[:, :], B.t1[:, :], r))
                t = P.op("vector", lambda e, a=a_ap, x=ap:
                         e.tensor_add(a, x, B.t1[:, :]), B.s_dve)
                xst.mark(slot, t)
            res_done = B.s_dve.n

        P.wait("sync", B.s_dve, res_done)
        P.dma("sync", chunked(xout), sb3(B.acc[:, :], KC), B.s_st)
        P.wait("sync", B.s_st, B.s_st.n)
    es2.close()



NH = 16
QR = 768
KVR = 512
ROPE = 64
VD = 128


def emit_l1_attn(P, B, nc, xin, wq_a, qnw, wq_b, wkv_a, kvnw, wkvb_n, wkvb_v,
                 wo, cos_in, sin_in, cv_in, cv128_in, xout,
                 agk_in, agk_out, agv_in, agv_out):
    es2 = contextlib.ExitStack()
    if True:
        sb = lambda name, shape, dt: es2.enter_context(
            nc.sbuf_tensor("C_" + name, shape, dt))
        init_done = B.init_done

        B.xn = sb("xn", [128, KC * TC], BF16)
        B.qs = sb("qs", [128, KC * TC], BF16)
        B.qa = sb("qa", [128, 6 * TC], BF16)
        B.ckv = sb("ckv", [128, 4 * TC], BF16)
        B.kpe_b = sb("kpe_b", [128, TC], BF16)
        B.knope = sb("knope", [128, 8 * TC], BF16)
        B.vt = sb("vt", [128, 2 * 512], BF16)
        B.cosb = sb("cosb", [32, TC], BF16)
        B.sinb = sb("sinb", [32, TC], BF16)
        B.rp = sb("rp", [32, 3 * TC], BF16)
        B.xw = sb("xw", [32, 2 * TC], BF16)
        B.wslab = sb("wslab", [128, 2 * 4096], BF16)
        B.slab_stride = 4096
        B.kvsl = sb("kvsl", [128, 2 * 4096], BF16)
        B.psb = sb("psb", [128, 2 * 2048], BF16)
        B.xs = sb("xs", [128, 4 * 512], F32)
        B.oscr = sb("oscr", [128, 2 * 512], F32)
        B.qnw_sb = sb("qnw_sb", [128, QR // 128], F32)
        B.kvnw_sb = sb("kvnw_sb", [128, KVR // 128], F32)
        B.cv128 = sb("cv128_sb", [128, NCONST], F32)
        B.den_row = sb("den_row", [1, 1024], F32)
        B.rec_bf = sb("rec_bf", [1, 1024], BF16)

        eps_ln = B.cv[0:1, CV_EPS_LN:CV_EPS_LN + 1]
        eps_rms = B.cv[0:1, CV_EPS_RMS:CV_EPS_RMS + 1]
        inv_d = 1.0 / 2048.0

        P.dma("sync", B.cv[:, :], cv_in, B.s_x)
        P.dma("sync", B.cv128[:, :], cv128_in, B.s_x)
        P.dma("sync", B.qnw_sb[:, :], qnw, B.s_x)
        P.dma("sync", B.kvnw_sb[:, :], kvnw, B.s_x)
        P.dma("gpsimd", B.cosb[:, :], cos_in, B.s_x)
        P.dma("gpsimd", B.sinb[:, :], sin_in, B.s_x)
        consts_tgt = B.s_x.n
        P.wait("scalar", B.s_x, consts_tgt)
        P.wait("vector", B.s_x, consts_tgt)

        xst = XStream(P, B, xin)

        def stats_src(k, th):
            ap, slot = xst.get(k, th)
            B.g["_cur_xs_slot"] = slot
            return ap

        def xn_put(k, th):
            return B.xn[:, k * TC + th * 512: k * TC + (th + 1) * 512]

        xn_done = []
        for th in range(2):
            emit_ln_stats(P, B, lambda k, th=th: stats_src(k, th), th, KC,
                          inv_d, B.m_b[:, th * 512:(th + 1) * 512],
                          B.r_b[:, th * 512:(th + 1) * 512], eps_ln,
                          cast_first=True)
            for k in range(KC):
                ap, slot = xst.get(k, th)
                m_ap = B.m_b[:, th * 512:(th + 1) * 512]
                r_ap = B.r_b[:, th * 512:(th + 1) * 512]
                P.op("vector", lambda e, x=ap, m=m_ap: e.tensor_sub(B.t1[:, :], x, m))
                t = P.op("vector", lambda e, o=xn_put(k, th), r=r_ap:
                         e.tensor_mul(o, B.t1[:, :], r), B.s_dve)
                xst.mark(slot, t)
            xn_done.append(B.s_dve.n)
        P.wait("tensor", B.s_dve, xn_done[1])

        def slab_load(w, og, slot, cols=256, n_k=KC):
            src = w[:, og * cols:(og + 1) * cols].rearrange("(k p) j -> p k j", p=128)
            return P.dma("gpsimd",
                         sb3(B.wslab[:, slot * 4096: slot * 4096 + n_k * cols], n_k),
                         src, B.s_w[slot])

        emit_proj(P, B, 3, lambda og, slot: slab_load(wq_a, og, slot),
                  xn_put, lambda o, th: B.qa[:, o * TC + th * 512: o * TC + (th + 1) * 512],
                  KC, 256, og_cols=256)

        qa_get = lambda k, th: B.qa[:, k * TC + th * 512: k * TC + (th + 1) * 512]
        for th in range(2):
            emit_ln_stats(P, B, lambda k, th=th: qa_get(k, th), th, 6,
                          1.0 / QR, None, B.r_b[:, th * 512:(th + 1) * 512],
                          eps_rms, cast_first=False)
            for k in range(6):
                r_ap = B.r_b[:, th * 512:(th + 1) * 512]
                P.op("vector", lambda e, q=qa_get(k, th), r=r_ap:
                     e.tensor_mul(q, q, r))
                P.op("vector", lambda e, q=qa_get(k, th), k_=k: e.tensor_scalar(
                    q, q, B.qnw_sb[:, k_:k_ + 1], None, ALU.mult), B.s_dve)
        qan_done = B.s_dve.n
        P.wait("tensor", B.s_dve, qan_done)

        emit_proj(P, B, 8, lambda og, slot: slab_load(wq_b, og, slot, n_k=6),
                  qa_get, lambda o, th: B.qs[:, o * TC + th * 512: o * TC + (th + 1) * 512],
                  6, 256, og_cols=256)
        q_evac_done = B.s_act.n

        emit_proj(P, B, 2, lambda og, slot: slab_load(wkv_a, og, slot),
                  xn_put, lambda o, th: B.ckv[:, o * TC + th * 512: o * TC + (th + 1) * 512],
                  KC, 256, og_cols=256)
        P.wait("gpsimd", B.s_mm, B.s_mm.n)
        kpe_slab = P.dma("gpsimd",
                         sb3(B.wslab[:, 0:KC * 64], KC),
                         wkv_a[:, KVR:KVR + ROPE].rearrange("(k p) j -> p k j", p=128),
                         B.s_w[0])
        P.wait("tensor", B.s_w[0], kpe_slab)
        P.wait("tensor", B.s_act, B.s_act.n)
        for th in range(2):
            pm = bank(B, 4 + th, parts=64)
            for k in range(KC):
                lhsT = B.wslab[:, k * 64:(k + 1) * 64]
                P.op("tensor", lambda e, p=pm, l=lhsT, r=xn_put(k, th),
                     st=(k == 0), sp=(k == KC - 1):
                     e.matmul(p, l, r, start=st, stop=sp),
                     B.s_mm if k == KC - 1 else None)
            P.wait("scalar", B.s_mm, B.s_mm.n)
            P.op("scalar", lambda e, th=th, p=pm: e.activation(
                B.kpe_b[0:64, th * 512:(th + 1) * 512], p, AF.Identity), B.s_act)
        kpe_done = B.s_act.n
        B.g[("proj_evac", 0)] = kpe_done
        B.g[("proj_evac", 1)] = kpe_done

        c_get = lambda k, th: B.ckv[:, k * TC + th * 512: k * TC + (th + 1) * 512]
        P.wait("vector", B.s_act, B.s_act.n)
        for th in range(2):
            emit_ln_stats(P, B, lambda k, th=th: c_get(k, th), th, 4,
                          1.0 / KVR, None, B.r_b[:, th * 512:(th + 1) * 512],
                          eps_rms, cast_first=False)
            for k in range(4):
                r_ap = B.r_b[:, th * 512:(th + 1) * 512]
                P.op("vector", lambda e, q=c_get(k, th), r=r_ap:
                     e.tensor_mul(q, q, r))
                P.op("vector", lambda e, q=c_get(k, th), k_=k: e.tensor_scalar(
                    q, q, B.kvnw_sb[:, k_:k_ + 1], None, ALU.mult), B.s_dve)
        cn_done = B.s_dve.n
        P.wait("tensor", B.s_dve, cn_done)

        s_rp = [P.sem("s_rp0"), P.sem("s_rp1")]

        def emit_rope(x1, x2_src, o1, o2_dst, slot):
            xi = B.xw[0:32, slot * TC: (slot + 1) * TC]
            xo = xi
            P.wait("sync", s_rp[slot], B.g.get(("xw_dma", slot), 0))
            P.wait("sync", B.s_dve, B.g.get(("xw", slot), 0))
            it = P.dma("sync", xi, x2_src, s_rp[slot])
            a = B.rp[0:32, 0:TC]
            b = B.rp[0:32, TC:2 * TC]
            c = B.rp[0:32, 2 * TC:3 * TC]
            cos = B.cosb[0:32, :]
            sin = B.sinb[0:32, :]
            P.wait("vector", s_rp[slot], it)
            P.op("vector", lambda e: e.tensor_mul(a, x1, cos))
            P.op("vector", lambda e: e.tensor_mul(b, xi, sin))
            P.op("vector", lambda e: e.tensor_mul(c, x1, sin))
            P.op("vector", lambda e: e.tensor_mul(xi, xi, cos))
            P.op("vector", lambda e: e.tensor_sub(o1, a, b))
            t = P.op("vector", lambda e: e.tensor_add(xo, c, xi), B.s_dve)
            B.g[("xw", slot)] = t
            P.wait("sync", B.s_dve, t)
            B.g[("xw_dma", slot)] = P.dma("sync", o2_dst, xo, s_rp[slot])
            return B.g[("xw_dma", slot)]

        P.wait("vector", B.s_act, q_evac_done)
        P.wait("sync", B.s_act, q_evac_done)
        for h in range(NH):
            base = h * TC
            emit_rope(B.qs[0:32, base:base + TC], B.qs[32:64, base:base + TC],
                      B.qs[0:32, base:base + TC], B.qs[32:64, base:base + TC],
                      h % 2)
        P.wait("vector", B.s_act, kpe_done)
        P.wait("sync", B.s_act, kpe_done)
        emit_rope(B.kpe_b[0:32, :], B.kpe_b[32:64, :],
                  B.kpe_b[0:32, :], B.kpe_b[32:64, :], 0)
        rope_done = B.s_dve.n
        rope_dma = [s_rp[0].n, s_rp[1].n]

        emit_proj(P, B, 4, lambda og, slot: slab_load(wkvb_n, og, slot, n_k=4),
                  c_get, lambda o, th: B.knope[:, o * TC + th * 512: o * TC + (th + 1) * 512],
                  4, 256, og_cols=256)
        knope_done = B.s_act.n

        s_cck = B.s_cc
        s_ccv = P.sem("s_ccv")
        P.wait("sync", B.s_dve, rope_done)
        P.wait("sync", s_rp[0], rope_dma[0])
        P.wait("sync", s_rp[1], rope_dma[1])
        for h in range(NH):
            P.dma("sync", agk_in[h * 128: h * 128 + 64, :], B.kpe_b[0:64, :], B.s_st)
        P.wait("sync", B.s_act, knope_done)
        for j in range(8):
            P.dma("sync", agk_in[j * 256 + 64: j * 256 + 128, :],
                  B.knope[0:64, j * TC:(j + 1) * TC], B.s_st)
            P.dma("sync", agk_in[j * 256 + 192: j * 256 + 256, :],
                  B.knope[64:128, j * TC:(j + 1) * TC], B.s_st)
        kstage_done = B.s_st.n
        P.wait("gpsimd", B.s_st, kstage_done)
        P.op("gpsimd", lambda e: e.collective_compute(
            "AllGather", ALU.bypass, replica_groups=[list(range(NCORES))],
            ins=[agk_in], outs=[agk_out]), s_cck)

        s_vt = [P.sem("s_vt0"), P.sem("s_vt1")]
        P.wait("tensor", B.s_act, knope_done)
        P.wait("gpsimd", B.s_mm, B.s_mm.n)
        for vg in range(4):
            slot = vg % 2
            P.wait("gpsimd", B.s_mm, B.g.get(("vslab", slot), 0))
            wt = P.dma("gpsimd",
                       sb3(B.wslab[:, slot * 4096: slot * 4096 + 4 * 512], 4),
                       wkvb_v[:, vg * 512:(vg + 1) * 512].rearrange(
                           "(k p) j -> p k j", p=128), B.s_w[slot])
            P.wait("tensor", B.s_w[slot], wt)
            for s in range(8):
                vbk = s % 2
                pm = bank(B, 4 + vbk)
                P.wait("tensor", B.s_act, B.g.get(("vprod", vbk), 0))
                for k in range(4):
                    lhsT = B.ckv[:, k * TC + s * 128: k * TC + s * 128 + 128]
                    rhs = B.wslab[:, slot * 4096 + k * 512:(slot * 4096) + (k + 1) * 512]
                    P.op("tensor", lambda e, p=pm, l=lhsT, r=rhs, st=(k == 0),
                         sp=(k == 3): e.matmul(p, l, r, start=st, stop=sp),
                         B.s_mm if k == 3 else None)
                mm_tgt = B.s_mm.n
                vsl = vbk
                P.wait("scalar", B.s_mm, mm_tgt)
                P.wait("scalar", s_vt[vsl], B.g.get(("vt", vsl), 0))
                dst = B.vt[:, vsl * 512:(vsl + 1) * 512]
                P.op("scalar", lambda e, d=dst, p=pm:
                     e.activation(d, p, AF.Identity), B.s_act)
                B.g[("vprod", vbk)] = B.s_act.n
                P.wait("sync", B.s_act, B.s_act.n)
                B.g[("vt", vsl)] = P.dma(
                    "sync", agv_in[s * 128:(s + 1) * 128, vg * 512:(vg + 1) * 512],
                    dst, s_vt[vsl])
            B.g[("vslab", slot)] = B.s_mm.n

        for sl in range(2):
            P.wait("sync", s_vt[sl], s_vt[sl].n)
        P.op("sync", lambda e: e.nop(), B.s_st, inc=1)
        vstage_done = B.s_st.n
        P.wait("gpsimd", B.s_st, vstage_done)
        P.op("gpsimd", lambda e: e.collective_compute(
            "AllGather", ALU.bypass, replica_groups=[list(range(NCORES))],
            ins=[agv_in], outs=[agv_out]), s_ccv)
        P.wait("sync", s_cck, s_cck.n)
        P.wait("sync", s_ccv, s_ccv.n)

        s_kv = [P.sem("s_kv0"), P.sem("s_kv1")]
        P.wait("tensor", B.s_dve, B.s_dve.n)
        P.wait("tensor", B.s_act, B.s_act.n)
        P.wait("tensor", s_rp[0], rope_dma[0])
        P.wait("tensor", s_rp[1], rope_dma[1])
        attnscale = B.cv128[:, CV_ATTNSCALE:CV_ATTNSCALE + 1]
        for qh in range(2):
            for hp in range(8):
                for r in range(NCORES):
                    slot = (hp * 8 + r) % 2
                    P.wait("sync", B.s_mm, B.g.get(("kvsl", slot), 0))
                    for h01 in range(2):
                        h = hp * 2 + h01
                        P.dma("sync", B.kvsl[:, slot * 4096 + h01 * 1024:
                                             slot * 4096 + (h01 + 1) * 1024],
                              agk_out[r * 2048 + h * 128: r * 2048 + (h + 1) * 128, :],
                              s_kv[slot])
                    kvt = P.dma("sync",
                                sb3(B.kvsl[:, slot * 4096 + 2048: (slot + 1) * 4096], 8),
                                agv_out[r * 1024:(r + 1) * 1024,
                                        hp * 256:(hp + 1) * 256].rearrange(
                                            "(kc p) j -> p kc j", p=128),
                                s_kv[slot])
                    P.wait("tensor", s_kv[slot], kvt)
                    first = (r == 0)
                    last = (r == NCORES - 1)
                    for h01 in range(2):
                        h = hp * 2 + h01
                        if first:
                            P.wait("tensor", B.s_dve, B.g.get(("den", h01), 0))
                            P.wait("tensor", B.s_dve, B.g.get(("pv", h01), 0))
                        for kcg in range(2):
                            psl = (h01 * 2 + kcg) % 2
                            P.wait("tensor", B.s_act, B.g.get("exp_done", 0))
                            for kc4 in range(4):
                                kc = kcg * 4 + kc4
                                lhsT = B.kvsl[:, slot * 4096 + h01 * 1024 + kc * 128:
                                              slot * 4096 + h01 * 1024 + (kc + 1) * 128]
                                rhs = B.qs[:, h * TC + qh * 512: h * TC + (qh + 1) * 512]
                                P.op("tensor", lambda e, p=bank(B, 2 + kc4), l=lhsT, r=rhs:
                                     e.matmul(p, l, r, start=True, stop=True),
                                     B.s_mm if kc4 == 3 else None)
                            sc_tgt = B.s_mm.n
                            P.wait("scalar", B.s_mm, sc_tgt)
                            P.wait("scalar", B.s_mm, B.g.get(("psb", psl), 0))
                            pdst = B.psb[:, psl * 2048:(psl + 1) * 2048]
                            P.op("scalar", lambda e, d=pdst: e.activation(
                                d, B.pool[:, 1024:3072], AF.Exp, scale=attnscale),
                                B.s_act)
                            B.g["exp_done"] = B.s_act.n
                            exp_tgt = B.s_act.n
                            P.wait("tensor", B.s_act, exp_tgt)
                            st = first and kcg == 0
                            for kc4 in range(4):
                                kc = kcg * 4 + kc4
                                sp = last and kcg == 1 and kc4 == 3
                                vtile = B.kvsl[:, slot * 4096 + 2048 + kc * 256 +
                                               h01 * 128: slot * 4096 + 2048 +
                                               kc * 256 + h01 * 128 + 128]
                                prh = pdst[:, kc4 * 512:(kc4 + 1) * 512]
                                P.op("tensor", lambda e, p=bank(B, 6 + h01), l=vtile,
                                     r=prh, st=(st and kc4 == 0), sp=sp:
                                     e.matmul(p, l, r, start=st, stop=sp),
                                     B.s_mm if sp else None)
                                P.op("tensor", lambda e, p=bank(B, h01, parts=1),
                                     r=prh, st=(st and kc4 == 0), sp=sp:
                                     e.matmul(p, B.ones_col[:, :], r, start=st, stop=sp),
                                     B.s_mm if (sp or kc4 == 3) else None)
                            B.g[("psb", psl)] = B.s_mm.n
                    B.g[("kvsl", slot)] = B.s_mm.n
                pv_tgt = B.s_mm.n
                P.wait("vector", B.s_mm, pv_tgt)
                P.op("vector", lambda e: e.tensor_copy(B.den_row[0:1, 0:1024],
                                                       B.pool[0:1, 0:1024]), B.s_dve)
                den_copied = B.s_dve.n
                for h01 in range(2):
                    B.g[("den", h01)] = den_copied
                P.op("vector", lambda e: e.reciprocal(B.den_row[0:1, 0:1024],
                                                      B.den_row[0:1, 0:1024]))
                P.op("vector", lambda e: e.tensor_copy(B.rec_bf[0:1, 0:1024],
                                                       B.den_row[0:1, 0:1024]), B.s_dve)
                P.wait("tensor", B.s_dve, B.s_dve.n)
                P.wait("tensor", B.s_act, B.g.get("exp_done", 0))
                for h01 in range(2):
                    P.op("tensor", lambda e, h01=h01: e.matmul(
                        bank(B, 2 + h01), B.ones_row[0:1, 0:128],
                        B.rec_bf[0:1, h01 * 512:(h01 + 1) * 512],
                        start=True, stop=True), B.s_mm)
                rb_tgt = B.s_mm.n
                P.wait("vector", B.s_mm, rb_tgt)
                for h01 in range(2):
                    h = hp * 2 + h01
                    P.op("vector", lambda e, h01=h01: e.tensor_copy(
                        B.m_b[:, h01 * 512:(h01 + 1) * 512], bank(B, 2 + h01)), B.s_dve)
                    dst = B.xn[:, h * TC + qh * 512: h * TC + (qh + 1) * 512]
                    P.op("vector", lambda e, d=dst, h01=h01: e.tensor_mul(
                        d, bank(B, 6 + h01), B.m_b[:, h01 * 512:(h01 + 1) * 512]),
                        B.s_dve)
                    B.g[("pv", h01)] = B.s_dve.n
                B.g["exp_done"] = max(B.g.get("exp_done", 0), 0)
                B.g["exp_done"] = B.s_act.n
                B.g[("rb_read")] = B.s_dve.n
                P.wait("tensor", B.s_dve, B.g[("rb_read")])
        attn_done = B.s_dve.n

        P.wait("tensor", B.s_dve, attn_done)
        P.wait("gpsimd", B.s_dve, attn_done)
        emit_proj(P, B, 8, lambda og, slot: slab_load(wo, og, slot),
                  lambda k, th: B.xn[:, k * TC + th * 512: k * TC + (th + 1) * 512],
                  lambda o, th: B.qs[:, o * TC + th * 512: o * TC + (th + 1) * 512],
                  KC, 256, og_cols=256)
        y_all = B.s_act.n

        P.wait("vector", B.s_act, y_all)
        P.wait("scalar", B.s_act, y_all)
        P.wait("tensor", B.s_act, y_all)
        y_get = lambda k, th: B.qs[:, k * TC + th * 512: k * TC + (th + 1) * 512]
        s_out = [P.sem("s_out0"), P.sem("s_out1")]
        for th in range(2):
            emit_ln_stats(P, B, lambda k, th=th: y_get(k, th), th, KC,
                          inv_d, B.m_b[:, th * 512:(th + 1) * 512],
                          B.r_b[:, th * 512:(th + 1) * 512], eps_ln,
                          cast_first=False)
            for k in range(KC):
                m_ap = B.m_b[:, th * 512:(th + 1) * 512]
                r_ap = B.r_b[:, th * 512:(th + 1) * 512]
                osl = k % 2
                o_ap = B.oscr[:, osl * 512:(osl + 1) * 512]
                ap, slot = xst.get(k, th)
                P.op("vector", lambda e, y=y_get(k, th), m=m_ap:
                     e.tensor_sub(B.t1[:, :], y, m))
                P.op("vector", lambda e, r=r_ap: e.tensor_mul(B.t1[:, :], B.t1[:, :], r))
                P.wait("vector", s_out[osl], B.g.get(("oscr", osl), 0))
                t = P.op("vector", lambda e, o=o_ap, x=ap:
                         e.tensor_add(o, x, B.t1[:, :]), B.s_dve)
                xst.mark(slot, t)
                P.wait("sync", B.s_dve, t)
                B.g[("oscr", osl)] = P.dma(
                    "sync", xout[k * 128:(k + 1) * 128, th * 512:(th + 1) * 512],
                    o_ap, s_out[osl])
        P.wait("sync", s_out[0], s_out[0].n)
        P.wait("sync", s_out[1], s_out[1].n)
    es2.close()


def prep_l1_attn(inp):
    wq_b = inp["L1_attn_wq_b"]
    perm = np.zeros(D, dtype=np.int64)
    for h in range(NH):
        perm[h * 128: h * 128 + 64] = h * 128 + 64 + np.arange(64)
        perm[h * 128 + 64: h * 128 + 128] = h * 128 + np.arange(64)
    wq_b_p = np.ascontiguousarray(wq_b[:, perm])
    wkv_b = inp["L1_attn_wkv_b"]
    cols_n = np.concatenate([h * 192 + np.arange(64) for h in range(NH)])
    cols_v = np.concatenate([h * 192 + 64 + np.arange(128) for h in range(NH)])
    wkvb_n = np.ascontiguousarray(wkv_b[:, cols_n])
    wkvb_v = np.ascontiguousarray(wkv_b[:, cols_v])
    qnw = np.ascontiguousarray(inp["L1_attn_q_a_norm_w"].reshape(6, 128).T)
    kvnw = np.ascontiguousarray(inp["L1_attn_kv_a_norm_w"].reshape(4, 128).T)
    cv128 = np.repeat(CV_VALS, 128, axis=0)
    inv_freq = (1.0 / 10000.0 ** (np.arange(0, ROPE, 2, dtype=np.float64) / ROPE))
    maps = []
    for c in range(NCORES):
        tpos = np.arange(c * TC, (c + 1) * TC, dtype=np.float64)
        freqs = np.outer(inv_freq, tpos)
        maps.append({
            "wq_a": inp["L1_attn_wq_a"], "qnw": qnw, "wq_b": wq_b_p,
            "wkv_a": inp["L1_attn_wkv_a"], "kvnw": kvnw,
            "wkvb_n": wkvb_n, "wkvb_v": wkvb_v, "wo": inp["L1_attn_wo"],
            "cos_in": np.cos(freqs).astype(np.float32),
            "sin_in": np.sin(freqs).astype(np.float32),
            "cv": CV_VALS, "cv128": cv128,
        })
    return maps



HID = 5632
HC = HID // 128
DG = 4
HCG = HC // DG
D_TRUNC = 4


def emit_l1_ffn(P, B, nc, xin, wqgT, wquT, wqdT, sg_c, su_c, sd_c, cv_in, xout):
    es2 = contextlib.ExitStack()
    if True:
        sb = lambda name, shape, dt: es2.enter_context(
            nc.sbuf_tensor("D_" + name, shape, dt))
        init_done = B.init_done

        B.xn = sb("xn", [128, KC * TC], BF16)
        B.inter = sb("inter", [128, HCG * TC], BF16)
        B.acc = sb("acc", [128, KC * TC], F32)
        B.wslab = sb("wslab", [128, 2 * 4096], BF16)
        B.slab_stride = 4096
        B.dslab = sb("dslab", [128, 2 * HCG * 128], BF16)
        B.xs = sb("xs", [128, 4 * 512], F32)
        B.sgc = sb("sgc", [128, HC], F32)
        B.suc = sb("suc", [128, HC], F32)
        B.sdc = sb("sdc", [128, KC], F32)
        B.m128 = sb("m128", [1, 128], BF16)
        B.sx_bf = sb("sx_bf", [1, TC], BF16)
        B.si_bf = sb("si_bf", [1, TC], BF16)

        eps_ln = B.cv[0:1, CV_EPS_LN:CV_EPS_LN + 1]
        inv_d = 1.0 / 2048.0
        P.dma("sync", B.sgc[:, :], sg_c, B.s_x)
        P.dma("sync", B.suc[:, :], su_c, B.s_x)
        P.dma("sync", B.sdc[:, :], sd_c, B.s_x)
        P.wait("scalar", B.s_x, B.s_x.n)
        P.wait("vector", B.s_x, B.s_x.n)
        P.op("vector", lambda e: e.memset(B.m128[:, :], -128.0), B.s_dve)

        xst = XStream(P, B, xin)

        def stats_src(k, th):
            ap, slot = xst.get(k, th)
            B.g["_cur_xs_slot"] = slot
            return ap

        def xn_put(k, th):
            return B.xn[:, k * TC + th * 512: k * TC + (th + 1) * 512]

        xn_done = []
        for th in range(2):
            emit_ln_stats(P, B, lambda k, th=th: stats_src(k, th), th, KC,
                          inv_d, B.m_b[:, th * 512:(th + 1) * 512],
                          B.r_b[:, th * 512:(th + 1) * 512], eps_ln,
                          cast_first=True)
            for k in range(KC):
                ap, slot = xst.get(k, th)
                m_ap = B.m_b[:, th * 512:(th + 1) * 512]
                r_ap = B.r_b[:, th * 512:(th + 1) * 512]
                P.op("vector", lambda e, x=ap, m=m_ap: e.tensor_sub(B.t1[:, :], x, m))
                t = P.op("vector", lambda e, o=xn_put(k, th), r=r_ap:
                         e.tensor_mul(o, B.t1[:, :], r), B.s_dve)
                xst.mark(slot, t)
            xn_done.append(B.s_dve.n)
        P.wait("tensor", B.s_dve, xn_done[1])

        for th in range(2 if D_TRUNC >= 1 else 0):
            P.wait("tensor", B.s_dve, B.s_dve.n)
            for k in range(KC):
                P.op("tensor", lambda e, r=xn_put(k, th), st=(k == 0), sp=(k == KC - 1):
                     e.matmul(bank(B, 0, parts=1), B.ones_col[:, :], r,
                              start=st, stop=sp),
                     B.s_mm if k == KC - 1 else None)
            P.wait("vector", B.s_mm, B.s_mm.n)
            P.op("vector", lambda e, th=th: e.tensor_copy(
                B.sx_bf[0:1, th * 512:(th + 1) * 512], bank(B, 0, parts=1)), B.s_dve)
        sx_done = B.s_dve.n
        P.wait("tensor", B.s_dve, sx_done)
        if D_TRUNC < 1:
            sx_done = B.s_dve.n

        s_wd = [P.sem("s_wdq0"), P.sem("s_wdq1")]
        inter_free = 0
        for g in range(DG if D_TRUNC >= 4 else (1 if D_TRUNC >= 2 else 0)):
            for ic in range(HCG):
                hc = g * HCG + ic
                slot = ic % 2
                P.wait("gpsimd", B.s_mm, B.g.get(("gu_slab", slot), 0))
                P.dma("gpsimd", sb3(B.wslab[:, slot * 4096: slot * 4096 + 2048], KC),
                      chunked(wqgT[:, hc * 128:(hc + 1) * 128]), B.s_w[slot])
                wt = P.dma("gpsimd",
                           sb3(B.wslab[:, slot * 4096 + 2048: slot * 4096 + 4096], KC),
                           chunked(wquT[:, hc * 128:(hc + 1) * 128]), B.s_w[slot])
                P.wait("tensor", B.s_w[slot], wt)
                if ic == 0:
                    P.wait("tensor", B.s_mm, inter_free)
                for th in range(2):
                    gbk = (ic * 2 + th) % 2
                    pg = bank(B, 4 + gbk)
                    pu = bank(B, 2 + gbk)
                    P.wait("tensor", B.s_act, B.g.get(("gact", gbk), 0))
                    P.wait("tensor", B.s_dve, B.g.get(("uact", gbk), 0))
                    for which, base, pm in (("g", 0, pg), ("u", 2048, pu)):
                        for k in range(KC):
                            lw = B.wslab[:, slot * 4096 + base + k * 128:
                                         slot * 4096 + base + (k + 1) * 128]
                            P.op("tensor", lambda e, p=pm, l=lw, r=xn_put(k, th),
                                 st=(k == 0): e.matmul(p, l, r, start=st, stop=False))
                        P.op("tensor", lambda e, p=pm, th=th: e.matmul(
                            p, B.m128[:, :], B.sx_bf[0:1, th * 512:(th + 1) * 512],
                            start=False, stop=True),
                            B.s_mm)
                    gu_mm = B.s_mm.n
                    sg = B.scr_a[:, th * 512:(th + 1) * 512]
                    P.wait("scalar", B.s_mm, gu_mm)
                    P.wait("scalar", B.s_dve, B.g.get(("sg", th), 0))
                    P.op("scalar", lambda e, d=sg, p=pg, hc=hc: e.activation(
                        d, p, AF.Silu, scale=B.sgc[:, hc:hc + 1]), B.s_act)
                    B.g[("gact", gbk)] = B.s_act.n
                    it = B.inter[:, ic * TC + th * 512: ic * TC + (th + 1) * 512]
                    P.wait("vector", B.s_mm, gu_mm)
                    P.wait("vector", B.s_act, B.g[("gact", gbk)])
                    P.op("vector", lambda e, d=it, s=sg, p=pu, hc=hc:
                         e.scalar_tensor_tensor(d, p, B.suc[:, hc:hc + 1], s,
                                                ALU.mult, ALU.mult), B.s_dve)
                    B.g[("uact", gbk)] = B.s_dve.n
                    B.g[("sg", th)] = B.s_dve.n
                B.g[("gu_slab", slot)] = B.s_mm.n
            inter_done = B.s_dve.n
            P.wait("tensor", B.s_dve, inter_done)
            for th in range(2):
                P.wait("tensor", B.s_dve, B.s_dve.n)
                for ic in range(HCG):
                    it = B.inter[:, ic * TC + th * 512: ic * TC + (th + 1) * 512]
                    P.op("tensor", lambda e, r=it, st=(ic == 0), sp=(ic == HCG - 1):
                         e.matmul(bank(B, 1, parts=1), B.ones_col[:, :], r,
                                  start=st, stop=sp),
                         B.s_mm if ic == HCG - 1 else None)
                P.wait("vector", B.s_mm, B.s_mm.n)
                P.op("vector", lambda e, th=th: e.tensor_copy(
                    B.si_bf[0:1, th * 512:(th + 1) * 512], bank(B, 1, parts=1)),
                    B.s_dve)
            si_done = B.s_dve.n
            P.wait("tensor", B.s_dve, si_done)
            for o in range(KC if D_TRUNC >= 3 else 0):
                dslot = o % 2
                P.wait("gpsimd", B.s_mm, B.g.get(("d_slab", dslot), 0))
                dt_ = P.dma("gpsimd",
                            sb3(B.dslab[:, dslot * HCG * 128:(dslot + 1) * HCG * 128],
                                HCG),
                            chunked(wqdT[g * HCG * 128:(g + 1) * HCG * 128,
                                         o * 128:(o + 1) * 128]), s_wd[dslot])
                P.wait("tensor", s_wd[dslot], dt_)
                for th in range(2):
                    dbk = (o * 2 + th) % 2
                    pd = bank(B, 6 + dbk)
                    P.wait("tensor", B.s_dve, B.g.get(("dacc", dbk), 0))
                    for ic in range(HCG):
                        ld = B.dslab[:, dslot * HCG * 128 + ic * 128:
                                     dslot * HCG * 128 + (ic + 1) * 128]
                        rh = B.inter[:, ic * TC + th * 512: ic * TC + (th + 1) * 512]
                        P.op("tensor", lambda e, p=pd, l=ld, r=rh, st=(ic == 0):
                             e.matmul(p, l, r, start=st, stop=False))
                    P.op("tensor", lambda e, p=pd, th=th: e.matmul(
                        p, B.m128[:, :], B.si_bf[0:1, th * 512:(th + 1) * 512],
                        start=False, stop=True), B.s_mm)
                    dmm = B.s_mm.n
                    a_ap = B.acc[:, o * TC + th * 512: o * TC + (th + 1) * 512]
                    P.wait("vector", B.s_mm, dmm)
                    if g == 0:
                        P.op("vector", lambda e, a=a_ap, p=pd, o=o: e.tensor_scalar(
                            a, p, B.sdc[:, o:o + 1], None, ALU.mult), B.s_dve)
                    else:
                        P.op("vector", lambda e, a=a_ap, p=pd, o=o:
                             e.scalar_tensor_tensor(a, p, B.sdc[:, o:o + 1], a,
                                                    ALU.mult, ALU.add), B.s_dve)
                    B.g[("dacc", dbk)] = B.s_dve.n
                B.g[("d_slab", dslot)] = B.s_mm.n
            inter_free = B.s_mm.n

        acc_done = B.s_dve.n
        P.wait("vector", B.s_dve, acc_done)
        P.wait("scalar", B.s_dve, acc_done)
        P.wait("tensor", B.s_dve, acc_done)

        def acc_get(k, th):
            return B.acc[:, k * TC + th * 512: k * TC + (th + 1) * 512]

        res_done = 0
        for th in range(2):
            emit_ln_stats(P, B, lambda k, th=th: acc_get(k, th), th, KC,
                          inv_d, B.m_b[:, th * 512:(th + 1) * 512],
                          B.r_b[:, th * 512:(th + 1) * 512], eps_ln,
                          cast_first=True)
            for k in range(KC):
                a_ap = acc_get(k, th)
                m_ap = B.m_b[:, th * 512:(th + 1) * 512]
                r_ap = B.r_b[:, th * 512:(th + 1) * 512]
                ap, slot = xst.get(k, th)
                P.op("vector", lambda e, a=a_ap, m=m_ap: e.tensor_sub(B.t1[:, :], a, m))
                P.op("vector", lambda e, r=r_ap: e.tensor_mul(B.t1[:, :], B.t1[:, :], r))
                t = P.op("vector", lambda e, a=a_ap, x=ap:
                         e.tensor_add(a, x, B.t1[:, :]), B.s_dve)
                xst.mark(slot, t)
            res_done = B.s_dve.n

        P.wait("sync", B.s_dve, res_done)
        P.dma("sync", chunked(xout), sb3(B.acc[:, :], KC), B.s_st)
        P.wait("sync", B.s_st, B.s_st.n)
    es2.close()


def _u8(a):
    return np.ascontiguousarray(a.view(np.uint8).reshape(*a.shape, 4)[..., 0])


def prep_l1_ffn(inp):
    wqgT = np.ascontiguousarray(_u8(inp["L1_ffn_wq_gate"]).T)
    wquT = np.ascontiguousarray(_u8(inp["L1_ffn_wq_up"]).T)
    wqdT = np.ascontiguousarray(_u8(inp["L1_ffn_wq_down"]).T)
    sg_c = np.ascontiguousarray(inp["L1_ffn_s_gate"].reshape(HC, 128).T)
    su_c = np.ascontiguousarray(inp["L1_ffn_s_up"].reshape(HC, 128).T)
    sd_c = np.ascontiguousarray(inp["L1_ffn_s_down"].reshape(KC, 128).T)
    shared = {"wqgT": wqgT, "wquT": wquT, "wqdT": wqdT,
              "sg_c": sg_c, "su_c": su_c, "sd_c": sd_c, "cv": CV_VALS}
    return [dict(shared) for _ in range(NCORES)]



def _shard_x(x):
    xs = x.reshape(NCORES, TC, D)
    return [np.ascontiguousarray(xs[c].T) for c in range(NCORES)]


N_CTRL = 130156


def _spline_grid():
    n = INTER * D
    t = np.linspace(0.0, N_CTRL - 1.0, n, dtype=np.float64)
    i = np.clip(np.floor(t).astype(np.int64), 0, N_CTRL - 2)
    f = (t - i).astype(np.float32)
    idx = [np.clip(i + k, 0, N_CTRL - 1).astype(np.int32) for k in (-1, 0, 1, 2)]
    f64 = f.astype(np.float64)
    f2, f3 = f64 * f64, f64 * f64 * f64
    c0 = (0.5 * (-f64 + 2.0 * f2 - f3)).astype(np.float32)
    c1 = (0.5 * (2.0 - 5.0 * f2 + 3.0 * f3)).astype(np.float32)
    c2 = (0.5 * (f64 + 4.0 * f2 - 3.0 * f3)).astype(np.float32)
    c3 = (0.5 * (f3 - f2)).astype(np.float32)
    return idx, (c0, c1, c2, c3)


def _spline_shards(inp):
    idx, cs = _spline_grid()
    cp_g = inp["L0_ffn_gate_cp"]
    cp_u = inp["L0_ffn_up_cp"]
    cp_d = inp["L0_ffn_down_cp"]
    shards = [dict() for _ in range(NCORES)]
    for k in range(4):
        ik = idx[k]
        for name, cp in (("p_g", cp_g), ("p_u", cp_u)):
            G = np.take(cp, ik).reshape(NCORES, TC, D)
            for c in range(NCORES):
                shards[c][f"{name}{k}"] = np.ascontiguousarray(G[c].T)
        C = cs[k].reshape(NCORES, TC, D)
        for c in range(NCORES):
            shards[c][f"c_gu{k}"] = np.ascontiguousarray(C[c].T)
        Gd = np.take(cp_d, ik).reshape(D, INTER)
        Cd = cs[k].reshape(D, INTER)
        for c in range(NCORES):
            shards[c][f"p_d{k}"] = np.ascontiguousarray(Gd[:, c * TC:(c + 1) * TC].T)
            shards[c][f"c_cd{k}"] = np.ascontiguousarray(Cd[:, c * TC:(c + 1) * TC].T)
    return shards


def prep_l0_ffn(inp):
    shards = _spline_shards(inp)
    for m in shards:
        m["cv"] = CV_VALS
    return shards


def prep_l0_attn(inp):
    shared = {
        "wqT": np.ascontiguousarray(inp["L0_attn_Wq"].T),
        "wkT": np.ascontiguousarray(inp["L0_attn_Wk"].T),
        "wvT": np.ascontiguousarray(inp["L0_attn_Wv"].T),
        "woT": np.ascontiguousarray(inp["L0_attn_Wo"].T),
        "bq_c": np.ascontiguousarray(inp["L0_attn_bq"].reshape(KC, 128).T),
        "bo_c": np.ascontiguousarray(inp["L0_attn_bo"].reshape(KC, 128).T),
        "bk_r": inp["L0_attn_bk"].reshape(1, D),
        "bv_r": inp["L0_attn_bv"].reshape(1, D),
        "cv": CV_VALS,
    }
    return [dict(shared) for _ in range(NCORES)]



def phase_barrier(P, B):
    s = B.s_bar
    for eng in P.ENGINES:
        P.op(eng, lambda e: e.nop(), s)
    tgt = s.n
    for eng in P.ENGINES:
        P.wait(eng, s, tgt)
    B.g = {}


def declare_drams(nc):
    dr = {}

    def inp(name, shape, dt=F32):
        dr[name] = nc.dram_tensor(name, shape, dt, kind="ExternalInput").ap()

    def internal(name, shape, dt=F32):
        dr[name] = nc.dram_tensor(name, shape, dt).ap()

    inp("xin", [D, TC])
    inp("cv", [1, NCONST])
    for n in ("wqT", "wkT", "wvT", "woT"):
        inp(n, [D, D])
    inp("bq_c", [128, KC]); inp("bo_c", [128, KC])
    inp("bk_r", [1, D]); inp("bv_r", [1, D])
    internal("kv_cc_in", [128, D]); internal("kv_cc_out", [128, D])
    for w, shape in (("g", [D, TC]), ("u", [D, TC]), ("d", [TC, D])):
        for k in range(4):
            inp(f"p_{w}{k}", shape)
    for w, shape in (("gu", [D, TC]), ("cd", [TC, D])):
        for k in range(4):
            inp(f"c_{w}{k}", shape)
    internal("aggu_in", [2 * D, TC], BF16)
    internal("aggu_out", [NCORES * 2 * D, TC], BF16)
    internal("agd_in", [TC, D], BF16)
    internal("agd_out", [T, D], BF16)
    inp("wq_a", [D, QR]); inp("qnw", [128, QR // 128]); inp("wq_b", [QR, D])
    inp("wkv_a", [D, KVR + ROPE]); inp("kvnw", [128, KVR // 128])
    inp("wkvb_n", [KVR, NH * 64]); inp("wkvb_v", [KVR, NH * VD])
    inp("wo", [D, D]); inp("cos_in", [32, TC]); inp("sin_in", [32, TC])
    inp("cv128", [128, NCONST])
    internal("agk_in", [NH * 128, TC], BF16)
    internal("agk_out", [NCORES * NH * 128, TC], BF16)
    internal("agv_in", [TC, NH * VD], BF16)
    internal("agv_out", [T, NH * VD], BF16)
    inp("wqgT", [D, HID], U8); inp("wquT", [D, HID], U8); inp("wqdT", [HID, D], U8)
    inp("sg_c", [128, HC]); inp("su_c", [128, HC]); inp("sd_c", [128, KC])
    internal("xmid1", [D, TC]); internal("xmid2", [D, TC]); internal("xmid3", [D, TC])
    dr["xout"] = nc.dram_tensor("xout", [D, TC], F32, kind="ExternalOutput").ap()
    return dr


def build_fused():
    nc = bass.Bass("TRN2", target_bir_lowering=False)
    es = contextlib.ExitStack()
    P = Plan(nc, es)
    dr = declare_drams(nc)
    with es:
        class B:
            pass
        make_common(P, nc, es, B)
        B.s_bar = P.sem("s_bar")
        spl = {k: dr[k] for k in dr if k.startswith(("p_g", "p_u", "p_d", "c_gu", "c_cd"))}
        es_sp = contextlib.ExitStack()
        B.spline = Spline(P, B, nc, es_sp, spl, dr["aggu_in"], dr["agd_in"])
        B.sp_hook = B.spline.hook
        emit_l0_attn(P, B, nc, dr["xin"], dr["wqT"], dr["wkT"], dr["wvT"],
                     dr["woT"], dr["bq_c"], dr["bo_c"], dr["bk_r"], dr["bv_r"],
                     dr["cv"], dr["xmid1"], dr["kv_cc_in"], dr["kv_cc_out"])
        B.spline.aggu_out_ap = dr["aggu_out"]
        B.spline.agd_out_ap = dr["agd_out"]
        s_ccB = P.sem("s_ccB")
        s_ccB2 = P.sem("s_ccB2")
        B.spline.finish(s_ccB, s_ccB2)
        es_sp.close()
        phase_barrier(P, B)
        emit_l0_ffn(P, B, nc, dr["xmid1"], dr["cv"], spl, dr["xmid2"],
                    dr["aggu_in"], dr["aggu_out"], dr["agd_in"], dr["agd_out"],
                    cc_waits=[(s_ccB, s_ccB.n), (s_ccB2, s_ccB2.n)])
        phase_barrier(P, B)
        emit_l1_attn(P, B, nc, dr["xmid2"], dr["wq_a"], dr["qnw"], dr["wq_b"],
                     dr["wkv_a"], dr["kvnw"], dr["wkvb_n"], dr["wkvb_v"],
                     dr["wo"], dr["cos_in"], dr["sin_in"], dr["cv"],
                     dr["cv128"], dr["xmid3"],
                     dr["agk_in"], dr["agk_out"], dr["agv_in"], dr["agv_out"])
        phase_barrier(P, B)
        emit_l1_ffn(P, B, nc, dr["xmid3"], dr["wqgT"], dr["wquT"], dr["wqdT"],
                    dr["sg_c"], dr["su_c"], dr["sd_c"], dr["cv"], dr["xout"])
        P.build()
    return nc


def kernel(**inputs):
    x = inputs["L0_attn_x"].astype(np.float32, copy=False)
    xs = _shard_x(x)
    pa = prep_l0_attn(inputs)
    pb = prep_l0_ffn(inputs)
    pc = prep_l1_attn(inputs)
    pd = prep_l1_ffn(inputs)
    in_maps = []
    for c in range(NCORES):
        m = {}
        m.update(pa[c]); m.update(pb[c]); m.update(pc[c]); m.update(pd[c])
        m["xin"] = xs[c]
        in_maps.append(m)
    nc = build_fused()
    res = run_bass_kernel_spmd(nc, in_maps, core_ids=list(range(NCORES)))
    out = np.concatenate([r["xout"].T[None] for r in res.results], axis=1)
    return np.ascontiguousarray(out.reshape(1, T, D))


# version 04
# speedup vs baseline: 1.2435x; 1.2024x over previous
import contextlib
import numpy as np
import ml_dtypes
import concourse.bass as bass
import concourse.mybir as mybir
from concourse.bass_utils import run_bass_kernel_spmd

F32 = mybir.dt.float32
BF16 = mybir.dt.bfloat16
FP16 = mybir.dt.float16
U8 = mybir.dt.uint8
AF = mybir.ActivationFunctionType
ALU = mybir.AluOpType

NCORES = 8
T = 8192
TC = T // NCORES
D = 2048
KC = D // 128
BFD = np.dtype(ml_dtypes.bfloat16)

CV_EPS_LN = 0
CV_NHALF = 1
CV_INV_D = 2
CV_KVSCALE = 3
CV_EPS_RMS = 4
CV_INV_QR = 5
CV_INV_KVR = 6
CV_ATTNSCALE = 7
NCONST = 8
CV_VALS = np.array([[1e-5, -0.5, 1.0 / 2048.0, 1.0 / np.sqrt(128.0),
                     float(np.finfo(np.float32).eps), 1.0 / 768.0,
                     1.0 / 512.0, 128.0 ** -0.5]], dtype=np.float32)


class Sem:
    __slots__ = ("h", "n", "name")

    def __init__(self, h, name):
        self.h = h
        self.n = 0
        self.name = name


class Plan:

    ENGINES = ("sync", "tensor", "vector", "scalar", "gpsimd")

    def __init__(self, nc, es):
        self.nc = nc
        self.es = es
        self.ops = {e: [] for e in self.ENGINES}

    def sem(self, name):
        return Sem(self.es.enter_context(self.nc.semaphore(name)), name)

    def emit(self, eng, fn):
        self.ops[eng].append(fn)

    def wait(self, eng, sem, val=None):
        v = sem.n if val is None else val
        if v > 0:
            self.emit(eng, lambda e, s=sem.h, v=v: e.wait_ge(s, v))

    def dma(self, eng, out, in_, sem=None, **kw):
        tgt = None
        if sem is not None:
            sem.n += 16
            tgt = sem.n
        def fn(e, out=out, in_=in_, sem=sem, kw=kw):
            ins = e.dma_start(out=out, in_=in_, **kw)
            if sem is not None:
                ins.then_inc(sem.h, 16)
        self.emit(eng, fn)
        return tgt

    def op(self, eng, fn, sem=None, inc=1):
        tgt = None
        if sem is not None:
            sem.n += inc
            tgt = sem.n
        def wrapped(e, fn=fn, sem=sem, inc=inc):
            ins = fn(e)
            if sem is not None:
                ins.then_inc(sem.h, inc)
        self.emit(eng, wrapped)
        return tgt

    def build(self):
        with self.nc.Block() as block:
            for eng_name in self.ENGINES:
                ops = self.ops[eng_name]
                if not ops:
                    continue
                def body(e, ops=ops):
                    for fn in ops:
                        fn(e)
                getattr(block, eng_name)(body)



def bank(B, i, parts=128, width=512):
    return B.pool[0:parts, i * 512:i * 512 + width]


def chunked(dram_ap):
    return dram_ap.rearrange("(k p) t -> p k t", p=128)


def sb3(sbuf_ap, n):
    return sbuf_ap.rearrange("p (k t) -> p k t", k=n)


def emit_ln_stats(P, B, src, th, n_chunks, inv_n, out_m, out_r,
                  eps_col, cast_first):
    nc = P.nc
    sA, sT, sV = B.s_act, B.s_mm, B.s_dve
    want_m = out_m is not None
    for k in range(n_chunks):
        slot = k % 2
        a = B.scr_a[:, slot * 512:(slot + 1) * 512]
        b = B.scr_b[:, slot * 512:(slot + 1) * 512]
        guard = B.g.get(("lnmm", slot), 0)
        if cast_first:
            P.wait("vector", sT, guard)
            P.op("vector", lambda e, a=a, s=src(k): e.tensor_copy(a, s), sV)
            cast_tgt = sV.n
            xslot = B.g.pop("_cur_xs_slot", None)
            if xslot is not None:
                B.g[("xs", xslot)] = cast_tgt
            P.wait("scalar", sV, cast_tgt)
            stat_in = a
        else:
            stat_in = src(k)
            P.wait("scalar", sT, guard)
        P.op("scalar", lambda e, b=b, s=stat_in: e.activation(b, s, AF.Square), sA)
        sq_tgt = sA.n
        if cast_first:
            P.wait("tensor", sV, cast_tgt)
        st = (k == 0)
        sp = (k == n_chunks - 1)
        if st:
            P.wait("tensor", sV, B.g.get("ln_stats_free", 0))
        if want_m:
            P.op("tensor", lambda e, s=stat_in, st=st, sp=sp:
                 e.matmul(bank(B, 0, parts=1), B.ones_col[:, :], s, start=st, stop=sp))
        P.wait("tensor", sA, sq_tgt)
        P.op("tensor", lambda e, b=b, st=st, sp=sp:
             e.matmul(bank(B, 1, parts=1), B.ones_col[:, :], b, start=st, stop=sp), sT)
        B.g[("lnmm", slot)] = sT.n
    mm_done = sT.n
    ts = th * 512
    m_row = B.m_row[0:1, ts:ts + 512]
    v_row = B.v_row[0:1, ts:ts + 512]
    r_row = B.r_row[0:1, ts:ts + 512]
    m_bf = B.m_bf[0:1, ts:ts + 512]
    r_bf = B.r_bf[0:1, ts:ts + 512]
    P.wait("vector", sT, mm_done)
    P.wait("vector", sT, B.g.get(("row_free", th), 0))
    if want_m:
        P.op("vector", lambda e, m=m_row: e.tensor_scalar(
            m, bank(B, 0, parts=1), inv_n, None, ALU.mult), sV)
        P.op("vector", lambda e, m=m_row, v=v_row: e.tensor_mul(v, m, m), sV)
        P.op("vector", lambda e, v=v_row: e.scalar_tensor_tensor(
            v, bank(B, 1, parts=1), inv_n, v, ALU.mult, ALU.subtract), sV)
    else:
        P.op("vector", lambda e, v=v_row: e.tensor_scalar(
            v, bank(B, 1, parts=1), inv_n, None, ALU.mult), sV)
    B.g["ln_stats_free"] = sV.n
    P.wait("scalar", sV, sV.n)
    P.op("scalar", lambda e, r=r_row, v=v_row, b=eps_col:
         e.activation(r, v, AF.Ln, bias=b), sA)
    P.op("scalar", lambda e, r=r_row: e.activation(
        r, r, AF.Exp, scale=B.cv[0:1, CV_NHALF:CV_NHALF + 1]), sA)
    P.wait("vector", sA, sA.n)
    if want_m:
        P.op("vector", lambda e, m=m_row, mb=m_bf: e.tensor_copy(mb, m), sV)
    P.op("vector", lambda e, r=r_row, rb=r_bf: e.tensor_copy(rb, r), sV)
    rows_done = sV.n
    P.wait("tensor", sV, rows_done)
    P.wait("tensor", sV, B.g.get("ln_bc_free", 0))
    if want_m:
        P.op("tensor", lambda e, mb=m_bf: e.matmul(
            bank(B, 2), B.ones_row[0:1, 0:128], mb, start=True, stop=True))
    P.op("tensor", lambda e, rb=r_bf: e.matmul(
        bank(B, 3), B.ones_row[0:1, 0:128], rb, start=True, stop=True), sT)
    bc_done = sT.n
    B.g[("row_free", th)] = bc_done
    P.wait("vector", sT, bc_done)
    if want_m:
        P.op("vector", lambda e, o=out_m: e.tensor_copy(o, bank(B, 2)), sV)
    P.op("vector", lambda e, o=out_r: e.tensor_copy(o, bank(B, 3)), sV)
    B.g["ln_bc_free"] = sV.n
    return sV.n


def emit_proj(P, B, n_og, wget, rhs_get, out_put, n_k, kstride, bias_col=None,
              scale_col=None, act=None, corr=None, og_cols=512, rhs_parts=128):
    act = act or AF.Identity
    n_oc = og_cols // 128
    P.wait("gpsimd", B.s_mm, B.s_mm.n)
    tgts = {0: wget(0, 0)}
    if n_og > 1:
        tgts[1] = wget(1, 1)
    for og in range(n_og):
        slot = og % 2
        P.wait("tensor", B.s_w[og % 2], tgts[og])
        for oc in range(n_oc):
            o = og * n_oc + oc
            for th in range(2):
                bk = (oc * 2 + th) % 2
                pm = bank(B, 4 + bk)
                prev = B.g.get(("proj_evac", bk), 0)
                P.wait("tensor", B.s_act, prev)
                mms = []
                ss = B.slab_stride
                for k in range(n_k):
                    lhsT = B.wslab[:, slot * ss + k * kstride + oc * 128:
                                   slot * ss + k * kstride + oc * 128 + 128]
                    mms.append((lhsT, rhs_get(k, th)))
                if corr is not None:
                    mms.extend(corr(o, th))
                for i, (l, r) in enumerate(mms):
                    last = (i == len(mms) - 1)
                    P.op("tensor", lambda e, p=pm, l=l, r=r, st=(i == 0), sp=last:
                         e.matmul(p, l, r, start=st, stop=sp),
                         B.s_mm if last else None)
                mm_tgt = B.s_mm.n
                P.wait("scalar", B.s_mm, mm_tgt)
                kw = {}
                if bias_col is not None:
                    kw["bias"] = bias_col(o)
                if scale_col is not None:
                    kw["scale"] = scale_col(o)
                P.op("scalar", lambda e, d=out_put(o, th), p=pm, kw=kw:
                     e.activation(d, p, act, **kw), B.s_act)
                B.g[("proj_evac", bk)] = B.s_act.n
        if og + 2 < n_og:
            P.wait("gpsimd", B.s_mm, B.s_mm.n)
            tgts[og + 2] = wget(og + 2, slot)
    return B.s_act.n


def emit_res_ln(P, B, ybuf_get, x_get, n_chunks, eps_col, inv_n):
    for th in range(2):
        emit_ln_stats(P, B, lambda k, th=th: ybuf_get(k, th), th, n_chunks,
                      inv_n, B.m_b[:, th * 512:(th + 1) * 512],
                      B.r_b[:, th * 512:(th + 1) * 512], eps_col, cast_first=False)
        for k in range(n_chunks):
            y_ap = ybuf_get(k, th)
            x_ap = x_get(k, th)
            m_ap = B.m_b[:, th * 512:(th + 1) * 512]
            r_ap = B.r_b[:, th * 512:(th + 1) * 512]
            P.op("vector", lambda e, y=y_ap, m=m_ap: e.tensor_sub(B.t1[:, :], y, m))
            P.op("vector", lambda e, r=r_ap: e.tensor_mul(B.t1[:, :], B.t1[:, :], r))
            P.op("vector", lambda e, x=x_ap: e.tensor_add(x, x, B.t1[:, :]), B.s_dve)
    return B.s_dve.n


def emit_inner_ln(P, B, x_get, xn_put, n_chunks, eps_col, inv_n):
    done = []
    for th in range(2):
        emit_ln_stats(P, B, lambda k, th=th: x_get(k, th), th, n_chunks,
                      inv_n, B.m_b[:, th * 512:(th + 1) * 512],
                      B.r_b[:, th * 512:(th + 1) * 512], eps_col, cast_first=True)
        for k in range(n_chunks):
            x_ap = x_get(k, th)
            xn_ap = xn_put(k, th)
            m_ap = B.m_b[:, th * 512:(th + 1) * 512]
            r_ap = B.r_b[:, th * 512:(th + 1) * 512]
            P.op("vector", lambda e, x=x_ap, m=m_ap: e.tensor_sub(B.t1[:, :], x, m))
            P.op("vector", lambda e, o=xn_ap, r=r_ap:
                 e.tensor_mul(o, B.t1[:, :], r), B.s_dve)
        done.append(B.s_dve.n)
    return done


def make_common(P, nc, es, B):
    sb = lambda name, shape, dt: es.enter_context(nc.sbuf_tensor(name, shape, dt))
    B.g = {}
    B.pool = es.enter_context(nc.psum_tensor("pool", [128, 4096], F32))
    B.scr_a = sb("scr_a", [128, 1024], BF16)
    B.scr_b = sb("scr_b", [128, 1024], BF16)
    B.t1 = sb("t1", [128, 512], F32)
    B.m_b = sb("m_b", [128, 1024], F32)
    B.r_b = sb("r_b", [128, 1024], F32)
    B.m_row = sb("m_row", [1, 1024], F32)
    B.v_row = sb("v_row", [1, 1024], F32)
    B.r_row = sb("r_row", [1, 1024], F32)
    B.m_bf = sb("m_bf", [1, 1024], BF16)
    B.r_bf = sb("r_bf", [1, 1024], BF16)
    B.ones_col = sb("ones_col", [128, 1], BF16)
    B.ones_row = sb("ones_row", [1, 512], BF16)
    B.cv = sb("cv_sb", [1, NCONST], F32)
    B.s_x = P.sem("s_x")
    B.s_w = [P.sem("s_w0"), P.sem("s_w1")]
    B.s_dve = P.sem("s_dve")
    B.s_act = P.sem("s_act")
    B.s_mm = P.sem("s_mm")
    B.s_cc = P.sem("s_cc")
    B.s_st = P.sem("s_st")
    P.op("vector", lambda e: e.memset(B.ones_col[:, :], 1.0))
    P.op("vector", lambda e: e.memset(B.ones_row[:, :], 1.0), B.s_dve)
    B.init_done = B.s_dve.n
    return B.init_done



def emit_l0_attn(P, B, nc, xin, wqT, wkT, wvT, woT, bq_c, bo_c, bk_r, bv_r,
                 cv_in, xout, kv_cc_in, kv_cc_out):
    es2 = contextlib.ExitStack()
    if True:
        sb = lambda name, shape, dt: es2.enter_context(
            nc.sbuf_tensor("A_" + name, shape, dt))
        init_done = B.init_done

        B.xn = sb("xn", [128, KC * TC], BF16)
        B.xs = sb("xs", [128, 4 * 512], F32)
        B.oscr = sb("oscr", [128, 2 * 512], F32)
        B.qb = sb("qb", [128, KC * TC], BF16)
        B.wslab = sb("wslab", [128, 2 * 4096], BF16)
        B.slab_stride = 4096
        B.kst = sb("kst", [128, 2 * 128], BF16)
        B.vst = sb("vst", [128, 2 * 128], BF16)
        B.kvacc = sb("kvacc", [128, D], F32)
        B.kvb = sb("kvb", [128, D], BF16)
        B.bqc = sb("bqc", [128, KC], F32)
        B.boc = sb("boc", [128, KC], F32)
        B.bkr = sb("bkr", [1, D], BF16)
        B.bvr = sb("bvr", [1, D], BF16)

        P.dma("sync", B.cv[:, :], cv_in, B.s_x)
        P.dma("sync", B.bqc[:, :], bq_c, B.s_x)
        P.dma("sync", B.boc[:, :], bo_c, B.s_x)
        P.dma("gpsimd", B.bkr[:, :], bk_r, B.s_x)
        P.dma("gpsimd", B.bvr[:, :], bv_r, B.s_x)
        x_loaded = B.s_x.n
        B.sp_hook(6)

        eps_ln = B.cv[0:1, CV_EPS_LN:CV_EPS_LN + 1]
        inv_d = 1.0 / 2048.0

        def slab_load(w, og, slot, cols=256):
            src = chunked(w[:, og * cols:(og + 1) * cols])
            t = P.dma("gpsimd",
                      sb3(B.wslab[:, slot * 4096: slot * 4096 + KC * cols], KC),
                      src, B.s_w[slot])
            B.sp_hook(2)
            return t

        P.wait("vector", B.s_x, x_loaded)
        P.wait("scalar", B.s_x, x_loaded)
        P.wait("tensor", B.s_dve, init_done)

        xst = XStream(P, B, xin)

        def stats_src(k, th):
            ap, slot = xst.get(k, th)
            B.g["_cur_xs_slot"] = slot
            return ap

        def xn_put(k, th):
            return B.xn[:, k * TC + th * 512: k * TC + (th + 1) * 512]

        xn_done = []
        for th in range(2):
            emit_ln_stats(P, B, lambda k, th=th: stats_src(k, th), th, KC,
                          inv_d, B.m_b[:, th * 512:(th + 1) * 512],
                          B.r_b[:, th * 512:(th + 1) * 512], eps_ln,
                          cast_first=True)
            for k in range(KC):
                ap, slot = xst.get(k, th)
                m_ap = B.m_b[:, th * 512:(th + 1) * 512]
                r_ap = B.r_b[:, th * 512:(th + 1) * 512]
                P.op("vector", lambda e, x=ap, m=m_ap: e.tensor_sub(B.t1[:, :], x, m))
                t = P.op("vector", lambda e, o=xn_put(k, th), r=r_ap:
                         e.tensor_mul(o, B.t1[:, :], r), B.s_dve)
                xst.mark(slot, t)
            xn_done.append(B.s_dve.n)

        P.wait("tensor", B.s_dve, xn_done[1])

        def q_out(o, th):
            return B.qb[:, o * TC + th * 512: o * TC + (th + 1) * 512]

        emit_proj(P, B, 8, lambda og, slot: slab_load(wqT, og, slot),
                  lambda k, th: xn_put(k, th), q_out, KC, 256,
                  bias_col=lambda o: B.bqc[:, o:o + 1], og_cols=256)
        q_all = B.s_act.n

        def kv_slab_load(dg, slot):
            P.dma("gpsimd", sb3(B.wslab[:, slot * 4096: slot * 4096 + KC * 128], KC),
                  chunked(wkT[:, dg * 128:(dg + 1) * 128]), B.s_w[slot])
            t = P.dma("gpsimd",
                      sb3(B.wslab[:, slot * 4096 + 2048:
                                  slot * 4096 + 2048 + KC * 128], KC),
                      chunked(wvT[:, dg * 128:(dg + 1) * 128]), B.s_w[slot])
            B.sp_hook(2)
            return t

        P.wait("gpsimd", B.s_act, q_all)
        kvt = {0: kv_slab_load(0, 0), 1: kv_slab_load(1, 1)}
        P.wait("tensor", B.s_act, q_all)
        for dg in range(16):
            slot = dg % 2
            P.wait("tensor", B.s_w[dg % 2], kvt[dg])
            P.wait("tensor", B.s_dve, B.g.get("kva_evac", 0))
            for s in range(8):
                sbk = s % 2
                for which, base, pm, dst, brow, gkey in (
                    ("k", 0, bank(B, 4 + sbk, width=128),
                     B.kst[:, sbk * 128:(sbk + 1) * 128], B.bkr, ("kprod", sbk)),
                    ("v", 2048, bank(B, 2 + sbk, width=128),
                     B.vst[:, sbk * 128:(sbk + 1) * 128], B.bvr, ("vprod", sbk)),
                ):
                    P.wait("tensor", B.s_act, B.g.get(gkey, 0))
                    for k in range(KC):
                        lhsT = B.xn[:, k * TC + s * 128: k * TC + s * 128 + 128]
                        rhs = B.wslab[:, slot * 4096 + base + k * 128:
                                      slot * 4096 + base + k * 128 + 128]
                        P.op("tensor", lambda e, p=pm, l=lhsT, r=rhs, st=(k == 0):
                             e.matmul(p, l, r, start=st, stop=False))
                    P.op("tensor", lambda e, p=pm, b=brow, dg=dg: e.matmul(
                        p, B.ones_row[0:1, 0:128], b[0:1, dg * 128:(dg + 1) * 128],
                        start=False, stop=True), B.s_mm)
                    mm_tgt = B.s_mm.n
                    P.wait("scalar", B.s_mm, mm_tgt)
                    P.op("scalar", lambda e, d=dst, p=pm:
                         e.activation(d, p, AF.Identity), B.s_act)
                    B.g[gkey] = B.s_act.n
                P.wait("tensor", B.s_act, B.g[("vprod", sbk)])
                lhsT = B.kst[:, sbk * 128:(sbk + 1) * 128]
                rhs = B.vst[:, sbk * 128:(sbk + 1) * 128]
                pm = B.pool[:, 6 * 512: 6 * 512 + 128]
                P.op("tensor", lambda e, p=pm, l=lhsT, r=rhs, st=(s == 0),
                     sp=(s == 7): e.matmul(p, l, r, start=st, stop=sp),
                     B.s_mm if s == 7 else None)
            P.wait("vector", B.s_mm, B.s_mm.n)
            P.op("vector", lambda e, dg=dg: e.tensor_copy(
                B.kvacc[:, dg * 128:(dg + 1) * 128], bank(B, 6, width=128)), B.s_dve)
            B.g["kva_evac"] = B.s_dve.n
            if dg + 2 < 16:
                P.wait("gpsimd", B.s_mm, B.s_mm.n)
                kvt[dg + 2] = kv_slab_load(dg + 2, slot)

        P.wait("sync", B.s_dve, B.g["kva_evac"])
        P.dma("sync", kv_cc_in, B.kvacc[:, :], B.s_st)
        P.wait("gpsimd", B.s_st, B.s_st.n)
        P.op("gpsimd", lambda e: e.collective_compute(
            "AllReduce", ALU.add, replica_groups=[list(range(NCORES))],
            ins=[kv_cc_in], outs=[kv_cc_out]), B.s_cc)
        P.wait("sync", B.s_cc, B.s_cc.n)
        P.dma("sync", B.kvacc[:, :], kv_cc_out, B.s_x)
        P.wait("vector", B.s_x, B.s_x.n)
        P.op("vector", lambda e: e.tensor_scalar(
            B.kvb[:, :], B.kvacc[:, :], float(128.0 ** -0.5),
            None, ALU.mult), B.s_dve)
        kvb_done = B.s_dve.n

        P.wait("tensor", B.s_dve, kvb_done)
        for h in range(16):
            for th in range(2):
                bk = (h * 2 + th) % 2
                pm = bank(B, 4 + bk)
                P.wait("tensor", B.s_act, B.g.get(("attn", bk), 0))
                lhsT = B.kvb[:, h * 128:(h + 1) * 128]
                rhs = B.qb[:, h * TC + th * 512: h * TC + (th + 1) * 512]
                P.op("tensor", lambda e, p=pm, l=lhsT, r=rhs:
                     e.matmul(p, l, r, start=True, stop=True), B.s_mm)
                P.wait("scalar", B.s_mm, B.s_mm.n)
                dst = B.qb[:, h * TC + th * 512: h * TC + (th + 1) * 512]
                P.op("scalar", lambda e, d=dst, p=pm:
                     e.activation(d, p, AF.Identity), B.s_act)
                B.g[("attn", bk)] = B.s_act.n
        attn_all = B.s_act.n

        P.wait("gpsimd", B.s_act, attn_all)
        P.wait("tensor", B.s_act, attn_all)

        def y_put(o, th):
            return B.xn[:, o * TC + th * 512: o * TC + (th + 1) * 512]

        emit_proj(P, B, 8, lambda og, slot: slab_load(woT, og, slot),
                  lambda k, th: B.qb[:, k * TC + th * 512: k * TC + (th + 1) * 512],
                  y_put, KC, 256, bias_col=lambda o: B.boc[:, o:o + 1], og_cols=256)
        y_all = B.s_act.n

        P.wait("vector", B.s_act, y_all)
        P.wait("scalar", B.s_act, y_all)
        P.wait("tensor", B.s_act, y_all)
        s_out = [P.sem("s_outA0"), P.sem("s_outA1")]
        for th in range(2):
            emit_ln_stats(P, B, lambda k, th=th: y_put(k, th), th, KC,
                          inv_d, B.m_b[:, th * 512:(th + 1) * 512],
                          B.r_b[:, th * 512:(th + 1) * 512], eps_ln,
                          cast_first=False)
            for k in range(KC):
                m_ap = B.m_b[:, th * 512:(th + 1) * 512]
                r_ap = B.r_b[:, th * 512:(th + 1) * 512]
                osl = k % 2
                o_ap = B.oscr[:, osl * 512:(osl + 1) * 512]
                ap, slot = xst.get(k, th)
                P.op("vector", lambda e, y=y_put(k, th), m=m_ap:
                     e.tensor_sub(B.t1[:, :], y, m))
                P.op("vector", lambda e, r=r_ap: e.tensor_mul(B.t1[:, :], B.t1[:, :], r))
                P.wait("vector", s_out[osl], B.g.get(("oscrA", osl), 0))
                t = P.op("vector", lambda e, o=o_ap, x=ap:
                         e.tensor_add(o, x, B.t1[:, :]), B.s_dve)
                xst.mark(slot, t)
                P.wait("sync", B.s_dve, t)
                B.g[("oscrA", osl)] = P.dma(
                    "sync", xout[k * 128:(k + 1) * 128, th * 512:(th + 1) * 512],
                    o_ap, s_out[osl])
        P.wait("sync", s_out[0], s_out[0].n)
        P.wait("sync", s_out[1], s_out[1].n)
    es2.close()



INTER = 8192
IG = 8
ICG = INTER // 128 // IG


class XStream:

    COUNT = [0]

    def __init__(self, P, B, xin, nslots=4):
        self.P, self.B, self.xin = P, B, xin
        self.nslots = nslots
        self.i = 0
        uid = XStream.COUNT[0]
        XStream.COUNT[0] += 1
        self.sems = [P.sem(f"s_xs{uid}_{j}") for j in range(nslots)]

    def get(self, k, th, waiters=("vector",)):
        P, B = self.P, self.B
        slot = self.i % self.nslots
        self.i += 1
        ap = B.xs[:, slot * 512:(slot + 1) * 512]
        P.wait("sync", B.s_dve, B.g.get(("xs", slot), 0))
        src = self.xin[k * 128:(k + 1) * 128, th * 512:(th + 1) * 512]
        tgt = P.dma("sync", ap, src, self.sems[slot])
        for w in waiters:
            P.wait(w, self.sems[slot], tgt)
        return ap, slot

    def mark(self, slot, tgt):
        self.B.g[("xs", slot)] = tgt




class Spline:

    def __init__(self, P, B, nc, es, spl, aggu_in, agd_in):
        self.P, self.B = P, B
        self.aggu_in, self.agd_in = aggu_in, agd_in
        sb = lambda name, shape, dt: es.enter_context(
            nc.sbuf_tensor("SP_" + name, shape, dt))
        self.spst = sb("spst", [128, 2 * 8 * TC], FP16)
        self.sp16 = sb("sp16", [128, 2 * TC], FP16)
        self.wout = sb("wout", [128, 2 * TC], BF16)
        self.s_sp = [P.sem("s_spl0"), P.sem("s_spl1")]
        self.s_agw = [P.sem("s_agwl0"), P.sem("s_agwl1")]
        self.chunks = []
        for hc in range(KC):
            self.chunks.append((
                [spl[f"p_g{k}"][hc * 128:(hc + 1) * 128, :] for k in range(4)]
                + [spl[f"c_gu{k}"][hc * 128:(hc + 1) * 128, :] for k in range(4)],
                aggu_in[hc * 128:(hc + 1) * 128, :]))
        for hc in range(KC):
            self.chunks.append((
                [spl[f"p_u{k}"][hc * 128:(hc + 1) * 128, :] for k in range(4)]
                + [spl[f"c_gu{k}"][hc * 128:(hc + 1) * 128, :] for k in range(4)],
                aggu_in[D + hc * 128: D + (hc + 1) * 128, :]))
        for ic in range(TC // 128):
            for h in range(2):
                cs_ = slice(h * 1024, (h + 1) * 1024)
                self.chunks.append((
                    [spl[f"p_d{k}"][ic * 128:(ic + 1) * 128, cs_] for k in range(4)]
                    + [spl[f"c_cd{k}"][ic * 128:(ic + 1) * 128, cs_] for k in range(4)],
                    agd_in[ic * 128:(ic + 1) * 128, cs_]))
        self.n = 0

    def hook(self, k=1):
        P, B = self.P, self.B
        for _ in range(k):
            if self.n >= len(self.chunks):
                return
            srcs, dst = self.chunks[self.n]
            sl = self.n % 2
            self.n += 1
            for j, sap in enumerate(srcs):
                P.dma("gpsimd", self.spst[:, sl * 8 * TC + j * TC:
                                          sl * 8 * TC + (j + 1) * TC],
                      sap, self.s_sp[sl])
            P.wait("gpsimd", self.s_sp[sl], self.s_sp[sl].n)
            pv = lambda i, sl=sl: self.spst[:, sl * 8 * TC + i * TC:
                                            sl * 8 * TC + (i + 1) * TC]
            ta = self.sp16[:, 0:TC]
            tb = self.sp16[:, TC:2 * TC]
            wo = self.wout[:, sl * TC:(sl + 1) * TC]
            P.op("gpsimd", lambda e, t=ta, x=pv(0), y=pv(4): e.tensor_mul(t, x, y))
            P.op("gpsimd", lambda e, t=tb, x=pv(1), y=pv(5): e.tensor_mul(t, x, y))
            P.op("gpsimd", lambda e, a=ta, b=tb: e.tensor_add(a, a, b))
            P.op("gpsimd", lambda e, t=tb, x=pv(2), y=pv(6): e.tensor_mul(t, x, y))
            P.op("gpsimd", lambda e, a=ta, b=tb: e.tensor_add(a, a, b))
            P.op("gpsimd", lambda e, t=tb, x=pv(3), y=pv(7): e.tensor_mul(t, x, y))
            P.wait("gpsimd", self.s_agw[sl], self.s_agw[sl].n)
            P.op("gpsimd", lambda e, w=wo, a=ta, b=tb: e.tensor_add(w, a, b))
            P.dma("gpsimd", dst, wo, self.s_agw[sl])

    def finish(self, s_cc, s_cc2):
        P, B = self.P, self.B
        self.hook(len(self.chunks))
        P.wait("gpsimd", self.s_agw[0], self.s_agw[0].n)
        P.wait("gpsimd", self.s_agw[1], self.s_agw[1].n)
        P.op("gpsimd", lambda e: e.collective_compute(
            "AllGather", ALU.bypass, replica_groups=[list(range(NCORES))],
            ins=[self.aggu_in], outs=[self.aggu_out_ap]), s_cc)
        P.op("gpsimd", lambda e: e.collective_compute(
            "AllGather", ALU.bypass, replica_groups=[list(range(NCORES))],
            ins=[self.agd_in], outs=[self.agd_out_ap]), s_cc2)


def emit_l0_ffn(P, B, nc, xin, cv_in, spl, xout, aggu_in, aggu_out,
                agd_in, agd_out, cc_waits=None):
    es2 = contextlib.ExitStack()
    if True:
        sb = lambda name, shape, dt: es2.enter_context(
            nc.sbuf_tensor("B_" + name, shape, dt))
        init_done = B.init_done

        B.xn = sb("xn", [128, KC * TC], BF16)
        B.inter = sb("inter", [128, ICG * TC], BF16)
        B.acc = sb("acc", [128, KC * TC], F32)
        B.wslab = sb("wslab", [128, 2 * 4096], BF16)
        B.slab_stride = 4096
        B.xs = sb("xs", [128, 4 * 512], F32)
        B.dslab = sb("dslab", [128, 2 * 1024], BF16)
        s_wd = [P.sem("s_wd0"), P.sem("s_wd1")]

        eps_ln = B.cv[0:1, CV_EPS_LN:CV_EPS_LN + 1]
        inv_d = 1.0 / 2048.0

        xst = XStream(P, B, xin)

        def stats_src(k, th):
            ap, slot = xst.get(k, th)
            B.g["_cur_xs_slot"] = slot
            return ap

        def xn_put(k, th):
            return B.xn[:, k * TC + th * 512: k * TC + (th + 1) * 512]

        xn_done = []
        for th in range(2):
            emit_ln_stats(P, B, lambda k, th=th: stats_src(k, th), th, KC,
                          inv_d, B.m_b[:, th * 512:(th + 1) * 512],
                          B.r_b[:, th * 512:(th + 1) * 512], eps_ln,
                          cast_first=True)
            for k in range(KC):
                ap, slot = xst.get(k, th)
                m_ap = B.m_b[:, th * 512:(th + 1) * 512]
                r_ap = B.r_b[:, th * 512:(th + 1) * 512]
                P.op("vector", lambda e, x=ap, m=m_ap: e.tensor_sub(B.t1[:, :], x, m))
                t = P.op("vector", lambda e, o=xn_put(k, th), r=r_ap:
                         e.tensor_mul(o, B.t1[:, :], r), B.s_dve)
                xst.mark(slot, t)
            xn_done.append(B.s_dve.n)

        for sem, tgt in cc_waits:
            P.wait("sync", sem, tgt)

        P.wait("tensor", B.s_dve, xn_done[1])

        def gu_slab_load(g, ic, slot):
            i0 = ic * 128
            gsrc = aggu_out[g * 2 * D: g * 2 * D + D, i0:i0 + 128].rearrange(
                "(k p) j -> p k j", p=128)
            usrc = aggu_out[g * 2 * D + D: (g + 1) * 2 * D, i0:i0 + 128].rearrange(
                "(k p) j -> p k j", p=128)
            P.dma("sync", sb3(B.wslab[:, slot * 4096: slot * 4096 + 2048], KC),
                  gsrc, B.s_w[slot])
            return P.dma("sync", sb3(B.wslab[:, slot * 4096 + 2048:
                                             slot * 4096 + 4096], KC), usrc, B.s_w[slot])

        inter_free = 0
        for g in range(IG):
            for ic in range(ICG):
                slot = ic % 2
                P.wait("sync", B.s_mm, B.g.get(("gu_slab", slot), 0))
                wt = gu_slab_load(g, ic, slot)
                P.wait("tensor", B.s_w[slot], wt)
                if ic == 0:
                    P.wait("tensor", B.s_mm, inter_free)
                for th in range(2):
                    gbk = (ic * 2 + th) % 2
                    pg = bank(B, 4 + gbk)
                    pu = bank(B, 2 + gbk)
                    P.wait("tensor", B.s_act, B.g.get(("gact", gbk), 0))
                    P.wait("tensor", B.s_dve, B.g.get(("uact", gbk), 0))
                    for k in range(KC):
                        lg = B.wslab[:, slot * 4096 + k * 128: slot * 4096 + (k + 1) * 128]
                        P.op("tensor", lambda e, p=pg, l=lg, r=xn_put(k, th),
                             st=(k == 0), sp=(k == KC - 1):
                             e.matmul(p, l, r, start=st, stop=sp),
                             B.s_mm if k == KC - 1 else None)
                    gmm = B.s_mm.n
                    for k in range(KC):
                        lu = B.wslab[:, slot * 4096 + 2048 + k * 128:
                                     slot * 4096 + 2048 + (k + 1) * 128]
                        P.op("tensor", lambda e, p=pu, l=lu, r=xn_put(k, th),
                             st=(k == 0), sp=(k == KC - 1):
                             e.matmul(p, l, r, start=st, stop=sp),
                             B.s_mm if k == KC - 1 else None)
                    umm = B.s_mm.n
                    sg = B.scr_a[:, th * 512:(th + 1) * 512]
                    P.wait("scalar", B.s_mm, gmm)
                    P.wait("scalar", B.s_dve, B.g.get(("sg", th), 0))
                    P.op("scalar", lambda e, d=sg, p=pg:
                         e.activation(d, p, AF.Silu), B.s_act)
                    B.g[("gact", gbk)] = B.s_act.n
                    it = B.inter[:, ic * TC + th * 512: ic * TC + (th + 1) * 512]
                    P.wait("vector", B.s_mm, umm)
                    P.wait("vector", B.s_act, B.g[("gact", gbk)])
                    P.op("vector", lambda e, d=it, s=sg, p=pu:
                         e.tensor_mul(d, s, p), B.s_dve)
                    B.g[("uact", gbk)] = B.s_dve.n
                    B.g[("sg", th)] = B.s_dve.n
                B.g[("gu_slab", slot)] = B.s_mm.n
            inter_done = B.s_dve.n
            P.wait("tensor", B.s_dve, inter_done)
            for o in range(KC):
                dslot = o % 2
                P.wait("sync", B.s_mm, B.g.get(("d_slab", dslot), 0))
                dsrc = agd_out[g * 1024: (g + 1) * 1024, o * 128:(o + 1) * 128]
                dt_ = P.dma("sync",
                            sb3(B.dslab[:, dslot * 1024:(dslot + 1) * 1024], ICG),
                            dsrc.rearrange("(k p) j -> p k j", p=128), s_wd[dslot])
                P.wait("tensor", s_wd[dslot], dt_)
                for th in range(2):
                    dbk = (o * 2 + th) % 2
                    pd = bank(B, 6 + dbk)
                    P.wait("tensor", B.s_dve, B.g.get(("dacc", dbk), 0))
                    for ic in range(ICG):
                        ld = B.dslab[:, dslot * 1024 + ic * 128:
                                     dslot * 1024 + (ic + 1) * 128]
                        rh = B.inter[:, ic * TC + th * 512: ic * TC + (th + 1) * 512]
                        P.op("tensor", lambda e, p=pd, l=ld, r=rh, st=(ic == 0),
                             sp=(ic == ICG - 1): e.matmul(p, l, r, start=st, stop=sp),
                             B.s_mm if ic == ICG - 1 else None)
                    dmm = B.s_mm.n
                    a_ap = B.acc[:, o * TC + th * 512: o * TC + (th + 1) * 512]
                    P.wait("vector", B.s_mm, dmm)
                    if g == 0:
                        P.op("vector", lambda e, a=a_ap, p=pd: e.tensor_copy(a, p),
                             B.s_dve)
                    else:
                        P.op("vector", lambda e, a=a_ap, p=pd: e.tensor_add(a, a, p),
                             B.s_dve)
                    B.g[("dacc", dbk)] = B.s_dve.n
                B.g[("d_slab", dslot)] = B.s_mm.n
            inter_free = B.s_mm.n

        acc_done = B.s_dve.n
        P.wait("vector", B.s_dve, acc_done)
        P.wait("scalar", B.s_dve, acc_done)
        P.wait("tensor", B.s_dve, acc_done)

        def acc_get(k, th):
            return B.acc[:, k * TC + th * 512: k * TC + (th + 1) * 512]

        res_done = 0
        for th in range(2):
            emit_ln_stats(P, B, lambda k, th=th: acc_get(k, th), th, KC,
                          inv_d, B.m_b[:, th * 512:(th + 1) * 512],
                          B.r_b[:, th * 512:(th + 1) * 512], eps_ln,
                          cast_first=True)
            for k in range(KC):
                a_ap = acc_get(k, th)
                m_ap = B.m_b[:, th * 512:(th + 1) * 512]
                r_ap = B.r_b[:, th * 512:(th + 1) * 512]
                ap, slot = xst.get(k, th)
                P.op("vector", lambda e, a=a_ap, m=m_ap: e.tensor_sub(B.t1[:, :], a, m))
                P.op("vector", lambda e, r=r_ap: e.tensor_mul(B.t1[:, :], B.t1[:, :], r))
                t = P.op("vector", lambda e, a=a_ap, x=ap:
                         e.tensor_add(a, x, B.t1[:, :]), B.s_dve)
                xst.mark(slot, t)
            res_done = B.s_dve.n

        P.wait("sync", B.s_dve, res_done)
        P.dma("sync", chunked(xout), sb3(B.acc[:, :], KC), B.s_st)
        P.wait("sync", B.s_st, B.s_st.n)
    es2.close()



NH = 16
QR = 768
KVR = 512
ROPE = 64
VD = 128


def emit_l1_attn(P, B, nc, xin, wq_a, qnw, wq_b, wkv_a, kvnw, wkvb_n, wkvb_v,
                 wo, cos_in, sin_in, cv_in, cv128_in, xout,
                 agk_in, agk_out, agv_in, agv_out):
    es2 = contextlib.ExitStack()
    if True:
        sb = lambda name, shape, dt: es2.enter_context(
            nc.sbuf_tensor("C_" + name, shape, dt))
        init_done = B.init_done

        B.xn = sb("xn", [128, KC * TC], BF16)
        B.qs = sb("qs", [128, KC * TC], BF16)
        B.qa = sb("qa", [128, 6 * TC], BF16)
        B.ckv = sb("ckv", [128, 4 * TC], BF16)
        B.kpe_b = sb("kpe_b", [128, TC], BF16)
        B.knope = sb("knope", [128, 8 * TC], BF16)
        B.vt = sb("vt", [128, 2 * 512], BF16)
        B.cosb = sb("cosb", [32, TC], BF16)
        B.sinb = sb("sinb", [32, TC], BF16)
        B.rp = sb("rp", [32, 3 * TC], BF16)
        B.xw = sb("xw", [32, 2 * TC], BF16)
        B.wslab = sb("wslab", [128, 2 * 4096], BF16)
        B.slab_stride = 4096
        B.kvsl = sb("kvsl", [128, 2 * 4096], BF16)
        B.psb = sb("psb", [128, 2 * 2048], BF16)
        B.xs = sb("xs", [128, 4 * 512], F32)
        B.oscr = sb("oscr", [128, 2 * 512], F32)
        B.qnw_sb = sb("qnw_sb", [128, QR // 128], F32)
        B.kvnw_sb = sb("kvnw_sb", [128, KVR // 128], F32)
        B.cv128 = sb("cv128_sb", [128, NCONST], F32)
        B.den_row = sb("den_row", [1, 1024], F32)
        B.rec_bf = sb("rec_bf", [1, 1024], BF16)

        eps_ln = B.cv[0:1, CV_EPS_LN:CV_EPS_LN + 1]
        eps_rms = B.cv[0:1, CV_EPS_RMS:CV_EPS_RMS + 1]
        inv_d = 1.0 / 2048.0

        P.dma("sync", B.cv[:, :], cv_in, B.s_x)
        P.dma("sync", B.cv128[:, :], cv128_in, B.s_x)
        P.dma("sync", B.qnw_sb[:, :], qnw, B.s_x)
        P.dma("sync", B.kvnw_sb[:, :], kvnw, B.s_x)
        P.dma("gpsimd", B.cosb[:, :], cos_in, B.s_x)
        P.dma("gpsimd", B.sinb[:, :], sin_in, B.s_x)
        consts_tgt = B.s_x.n
        P.wait("scalar", B.s_x, consts_tgt)
        P.wait("vector", B.s_x, consts_tgt)

        xst = XStream(P, B, xin)

        def stats_src(k, th):
            ap, slot = xst.get(k, th)
            B.g["_cur_xs_slot"] = slot
            return ap

        def xn_put(k, th):
            return B.xn[:, k * TC + th * 512: k * TC + (th + 1) * 512]

        xn_done = []
        for th in range(2):
            emit_ln_stats(P, B, lambda k, th=th: stats_src(k, th), th, KC,
                          inv_d, B.m_b[:, th * 512:(th + 1) * 512],
                          B.r_b[:, th * 512:(th + 1) * 512], eps_ln,
                          cast_first=True)
            for k in range(KC):
                ap, slot = xst.get(k, th)
                m_ap = B.m_b[:, th * 512:(th + 1) * 512]
                r_ap = B.r_b[:, th * 512:(th + 1) * 512]
                P.op("vector", lambda e, x=ap, m=m_ap: e.tensor_sub(B.t1[:, :], x, m))
                t = P.op("vector", lambda e, o=xn_put(k, th), r=r_ap:
                         e.tensor_mul(o, B.t1[:, :], r), B.s_dve)
                xst.mark(slot, t)
            xn_done.append(B.s_dve.n)
        P.wait("tensor", B.s_dve, xn_done[1])

        def slab_load(w, og, slot, cols=256, n_k=KC):
            src = w[:, og * cols:(og + 1) * cols].rearrange("(k p) j -> p k j", p=128)
            return P.dma("gpsimd",
                         sb3(B.wslab[:, slot * 4096: slot * 4096 + n_k * cols], n_k),
                         src, B.s_w[slot])

        emit_proj(P, B, 3, lambda og, slot: slab_load(wq_a, og, slot),
                  xn_put, lambda o, th: B.qa[:, o * TC + th * 512: o * TC + (th + 1) * 512],
                  KC, 256, og_cols=256)

        qa_get = lambda k, th: B.qa[:, k * TC + th * 512: k * TC + (th + 1) * 512]
        for th in range(2):
            emit_ln_stats(P, B, lambda k, th=th: qa_get(k, th), th, 6,
                          1.0 / QR, None, B.r_b[:, th * 512:(th + 1) * 512],
                          eps_rms, cast_first=False)
            for k in range(6):
                r_ap = B.r_b[:, th * 512:(th + 1) * 512]
                P.op("vector", lambda e, q=qa_get(k, th), r=r_ap:
                     e.tensor_mul(q, q, r))
                P.op("vector", lambda e, q=qa_get(k, th), k_=k: e.tensor_scalar(
                    q, q, B.qnw_sb[:, k_:k_ + 1], None, ALU.mult), B.s_dve)
        qan_done = B.s_dve.n
        P.wait("tensor", B.s_dve, qan_done)

        emit_proj(P, B, 8, lambda og, slot: slab_load(wq_b, og, slot, n_k=6),
                  qa_get, lambda o, th: B.qs[:, o * TC + th * 512: o * TC + (th + 1) * 512],
                  6, 256, og_cols=256)
        q_evac_done = B.s_act.n

        emit_proj(P, B, 2, lambda og, slot: slab_load(wkv_a, og, slot),
                  xn_put, lambda o, th: B.ckv[:, o * TC + th * 512: o * TC + (th + 1) * 512],
                  KC, 256, og_cols=256)
        P.wait("gpsimd", B.s_mm, B.s_mm.n)
        kpe_slab = P.dma("gpsimd",
                         sb3(B.wslab[:, 0:KC * 64], KC),
                         wkv_a[:, KVR:KVR + ROPE].rearrange("(k p) j -> p k j", p=128),
                         B.s_w[0])
        P.wait("tensor", B.s_w[0], kpe_slab)
        P.wait("tensor", B.s_act, B.s_act.n)
        for th in range(2):
            pm = bank(B, 4 + th, parts=64)
            for k in range(KC):
                lhsT = B.wslab[:, k * 64:(k + 1) * 64]
                P.op("tensor", lambda e, p=pm, l=lhsT, r=xn_put(k, th),
                     st=(k == 0), sp=(k == KC - 1):
                     e.matmul(p, l, r, start=st, stop=sp),
                     B.s_mm if k == KC - 1 else None)
            P.wait("scalar", B.s_mm, B.s_mm.n)
            P.op("scalar", lambda e, th=th, p=pm: e.activation(
                B.kpe_b[0:64, th * 512:(th + 1) * 512], p, AF.Identity), B.s_act)
        kpe_done = B.s_act.n
        B.g[("proj_evac", 0)] = kpe_done
        B.g[("proj_evac", 1)] = kpe_done

        c_get = lambda k, th: B.ckv[:, k * TC + th * 512: k * TC + (th + 1) * 512]
        P.wait("vector", B.s_act, B.s_act.n)
        for th in range(2):
            emit_ln_stats(P, B, lambda k, th=th: c_get(k, th), th, 4,
                          1.0 / KVR, None, B.r_b[:, th * 512:(th + 1) * 512],
                          eps_rms, cast_first=False)
            for k in range(4):
                r_ap = B.r_b[:, th * 512:(th + 1) * 512]
                P.op("vector", lambda e, q=c_get(k, th), r=r_ap:
                     e.tensor_mul(q, q, r))
                P.op("vector", lambda e, q=c_get(k, th), k_=k: e.tensor_scalar(
                    q, q, B.kvnw_sb[:, k_:k_ + 1], None, ALU.mult), B.s_dve)
        cn_done = B.s_dve.n
        P.wait("tensor", B.s_dve, cn_done)

        s_rp = [P.sem("s_rp0"), P.sem("s_rp1")]

        def emit_rope(x1, x2_src, o1, o2_dst, slot):
            xi = B.xw[0:32, slot * TC: (slot + 1) * TC]
            xo = xi
            P.wait("sync", s_rp[slot], B.g.get(("xw_dma", slot), 0))
            P.wait("sync", B.s_dve, B.g.get(("xw", slot), 0))
            it = P.dma("sync", xi, x2_src, s_rp[slot])
            a = B.rp[0:32, 0:TC]
            b = B.rp[0:32, TC:2 * TC]
            c = B.rp[0:32, 2 * TC:3 * TC]
            cos = B.cosb[0:32, :]
            sin = B.sinb[0:32, :]
            P.wait("vector", s_rp[slot], it)
            P.op("vector", lambda e: e.tensor_mul(a, x1, cos))
            P.op("vector", lambda e: e.tensor_mul(b, xi, sin))
            P.op("vector", lambda e: e.tensor_mul(c, x1, sin))
            P.op("vector", lambda e: e.tensor_mul(xi, xi, cos))
            P.op("vector", lambda e: e.tensor_sub(o1, a, b))
            t = P.op("vector", lambda e: e.tensor_add(xo, c, xi), B.s_dve)
            B.g[("xw", slot)] = t
            P.wait("sync", B.s_dve, t)
            B.g[("xw_dma", slot)] = P.dma("sync", o2_dst, xo, s_rp[slot])
            return B.g[("xw_dma", slot)]

        P.wait("vector", B.s_act, q_evac_done)
        P.wait("sync", B.s_act, q_evac_done)
        for h in range(NH):
            base = h * TC
            emit_rope(B.qs[0:32, base:base + TC], B.qs[32:64, base:base + TC],
                      B.qs[0:32, base:base + TC], B.qs[32:64, base:base + TC],
                      h % 2)
        P.wait("vector", B.s_act, kpe_done)
        P.wait("sync", B.s_act, kpe_done)
        emit_rope(B.kpe_b[0:32, :], B.kpe_b[32:64, :],
                  B.kpe_b[0:32, :], B.kpe_b[32:64, :], 0)
        rope_done = B.s_dve.n
        rope_dma = [s_rp[0].n, s_rp[1].n]

        emit_proj(P, B, 4, lambda og, slot: slab_load(wkvb_n, og, slot, n_k=4),
                  c_get, lambda o, th: B.knope[:, o * TC + th * 512: o * TC + (th + 1) * 512],
                  4, 256, og_cols=256)
        knope_done = B.s_act.n

        s_cck = B.s_cc
        s_ccv = P.sem("s_ccv")
        P.wait("sync", B.s_dve, rope_done)
        P.wait("sync", s_rp[0], rope_dma[0])
        P.wait("sync", s_rp[1], rope_dma[1])
        for h in range(NH):
            P.dma("sync", agk_in[h * 128: h * 128 + 64, :], B.kpe_b[0:64, :], B.s_st)
        P.wait("sync", B.s_act, knope_done)
        for j in range(8):
            P.dma("sync", agk_in[j * 256 + 64: j * 256 + 128, :],
                  B.knope[0:64, j * TC:(j + 1) * TC], B.s_st)
            P.dma("sync", agk_in[j * 256 + 192: j * 256 + 256, :],
                  B.knope[64:128, j * TC:(j + 1) * TC], B.s_st)
        kstage_done = B.s_st.n
        P.wait("gpsimd", B.s_st, kstage_done)
        P.op("gpsimd", lambda e: e.collective_compute(
            "AllGather", ALU.bypass, replica_groups=[list(range(NCORES))],
            ins=[agk_in], outs=[agk_out]), s_cck)

        s_vt = [P.sem("s_vt0"), P.sem("s_vt1")]
        P.wait("tensor", B.s_act, knope_done)
        P.wait("gpsimd", B.s_mm, B.s_mm.n)
        for vg in range(4):
            slot = vg % 2
            P.wait("gpsimd", B.s_mm, B.g.get(("vslab", slot), 0))
            wt = P.dma("gpsimd",
                       sb3(B.wslab[:, slot * 4096: slot * 4096 + 4 * 512], 4),
                       wkvb_v[:, vg * 512:(vg + 1) * 512].rearrange(
                           "(k p) j -> p k j", p=128), B.s_w[slot])
            P.wait("tensor", B.s_w[slot], wt)
            for s in range(8):
                vbk = s % 2
                pm = bank(B, 4 + vbk)
                P.wait("tensor", B.s_act, B.g.get(("vprod", vbk), 0))
                for k in range(4):
                    lhsT = B.ckv[:, k * TC + s * 128: k * TC + s * 128 + 128]
                    rhs = B.wslab[:, slot * 4096 + k * 512:(slot * 4096) + (k + 1) * 512]
                    P.op("tensor", lambda e, p=pm, l=lhsT, r=rhs, st=(k == 0),
                         sp=(k == 3): e.matmul(p, l, r, start=st, stop=sp),
                         B.s_mm if k == 3 else None)
                mm_tgt = B.s_mm.n
                vsl = vbk
                P.wait("scalar", B.s_mm, mm_tgt)
                P.wait("scalar", s_vt[vsl], B.g.get(("vt", vsl), 0))
                dst = B.vt[:, vsl * 512:(vsl + 1) * 512]
                P.op("scalar", lambda e, d=dst, p=pm:
                     e.activation(d, p, AF.Identity), B.s_act)
                B.g[("vprod", vbk)] = B.s_act.n
                P.wait("sync", B.s_act, B.s_act.n)
                B.g[("vt", vsl)] = P.dma(
                    "sync", agv_in[s * 128:(s + 1) * 128, vg * 512:(vg + 1) * 512],
                    dst, s_vt[vsl])
            B.g[("vslab", slot)] = B.s_mm.n

        for sl in range(2):
            P.wait("sync", s_vt[sl], s_vt[sl].n)
        P.op("sync", lambda e: e.nop(), B.s_st, inc=1)
        vstage_done = B.s_st.n
        P.wait("gpsimd", B.s_st, vstage_done)
        P.op("gpsimd", lambda e: e.collective_compute(
            "AllGather", ALU.bypass, replica_groups=[list(range(NCORES))],
            ins=[agv_in], outs=[agv_out]), s_ccv)
        P.wait("sync", s_cck, s_cck.n)
        P.wait("sync", s_ccv, s_ccv.n)

        s_kv = [P.sem("s_kv0"), P.sem("s_kv1")]
        P.wait("tensor", B.s_dve, B.s_dve.n)
        P.wait("tensor", B.s_act, B.s_act.n)
        P.wait("tensor", s_rp[0], rope_dma[0])
        P.wait("tensor", s_rp[1], rope_dma[1])
        attnscale = B.cv128[:, CV_ATTNSCALE:CV_ATTNSCALE + 1]
        def emit_pvden(pend):
            pdst, slot_, h01_, st_, sp_, kcg_, psl_, exp_tgt = pend
            if st_:
                P.wait("tensor", B.s_dve, B.g.get(("den", h01_), 0))
                P.wait("tensor", B.s_dve, B.g.get(("pv", h01_), 0))
            P.wait("tensor", B.s_act, exp_tgt)
            for kc4 in range(4):
                kc = kcg_ * 4 + kc4
                sp = sp_ and kc4 == 3
                vtile = B.kvsl[:, slot_ * 4096 + 2048 + kc * 256 +
                               h01_ * 128: slot_ * 4096 + 2048 +
                               kc * 256 + h01_ * 128 + 128]
                prh = pdst[:, kc4 * 512:(kc4 + 1) * 512]
                P.op("tensor", lambda e, p=bank(B, 6 + h01_), l=vtile,
                     r=prh, st=(st_ and kc4 == 0), sp=sp:
                     e.matmul(p, l, r, start=st, stop=sp),
                     B.s_mm if sp else None)
                P.op("tensor", lambda e, p=bank(B, h01_, parts=1),
                     r=prh, st=(st_ and kc4 == 0), sp=sp:
                     e.matmul(p, B.ones_col[:, :], r, start=st, stop=sp),
                     B.s_mm if (sp or kc4 == 3) else None)
            B.g[("psb", psl_)] = B.s_mm.n
            B.g[("kvsl", slot_)] = max(B.g.get(("kvsl", slot_), 0), B.s_mm.n)

        for qh in range(2):
            for hp in range(8):
                pending = None
                uidx = 0
                for r in range(NCORES):
                    slot = (hp * 8 + r) % 2
                    P.wait("sync", B.s_mm, B.g.get(("kvsl", slot), 0))
                    for h01 in range(2):
                        h = hp * 2 + h01
                        P.dma("sync", B.kvsl[:, slot * 4096 + h01 * 1024:
                                             slot * 4096 + (h01 + 1) * 1024],
                              agk_out[r * 2048 + h * 128: r * 2048 + (h + 1) * 128, :],
                              s_kv[slot])
                    kvt = P.dma("sync",
                                sb3(B.kvsl[:, slot * 4096 + 2048: (slot + 1) * 4096], 8),
                                agv_out[r * 1024:(r + 1) * 1024,
                                        hp * 256:(hp + 1) * 256].rearrange(
                                            "(kc p) j -> p kc j", p=128),
                                s_kv[slot])
                    P.wait("tensor", s_kv[slot], kvt)
                    first = (r == 0)
                    last = (r == NCORES - 1)
                    for h01 in range(2):
                        h = hp * 2 + h01
                        for kcg in range(2):
                            psl = uidx % 2
                            uidx += 1
                            P.wait("tensor", B.s_act, B.g.get("exp_done", 0))
                            for kc4 in range(4):
                                kc = kcg * 4 + kc4
                                lhsT = B.kvsl[:, slot * 4096 + h01 * 1024 + kc * 128:
                                              slot * 4096 + h01 * 1024 + (kc + 1) * 128]
                                rhs = B.qs[:, h * TC + qh * 512: h * TC + (qh + 1) * 512]
                                P.op("tensor", lambda e, p=bank(B, 2 + kc4), l=lhsT, r=rhs:
                                     e.matmul(p, l, r, start=True, stop=True),
                                     B.s_mm if kc4 == 3 else None)
                            sc_tgt = B.s_mm.n
                            P.wait("scalar", B.s_mm, sc_tgt)
                            P.wait("scalar", B.s_mm, B.g.get(("psb", psl), 0))
                            pdst = B.psb[:, psl * 2048:(psl + 1) * 2048]
                            P.op("scalar", lambda e, d=pdst: e.activation(
                                d, B.pool[:, 1024:3072], AF.Exp, scale=attnscale),
                                B.s_act)
                            B.g["exp_done"] = B.s_act.n
                            if pending is not None:
                                emit_pvden(pending)
                            pending = (pdst, slot, h01, first and kcg == 0,
                                       last and kcg == 1, kcg, psl, B.s_act.n)
                if pending is not None:
                    emit_pvden(pending)
                    pending = None
                pv_tgt = B.s_mm.n
                P.wait("vector", B.s_mm, pv_tgt)
                P.op("vector", lambda e: e.tensor_copy(B.den_row[0:1, 0:1024],
                                                       B.pool[0:1, 0:1024]), B.s_dve)
                den_copied = B.s_dve.n
                for h01 in range(2):
                    B.g[("den", h01)] = den_copied
                P.op("vector", lambda e: e.reciprocal(B.den_row[0:1, 0:1024],
                                                      B.den_row[0:1, 0:1024]))
                P.op("vector", lambda e: e.tensor_copy(B.rec_bf[0:1, 0:1024],
                                                       B.den_row[0:1, 0:1024]), B.s_dve)
                P.wait("tensor", B.s_dve, B.s_dve.n)
                P.wait("tensor", B.s_act, B.g.get("exp_done", 0))
                for h01 in range(2):
                    P.op("tensor", lambda e, h01=h01: e.matmul(
                        bank(B, 2 + h01), B.ones_row[0:1, 0:128],
                        B.rec_bf[0:1, h01 * 512:(h01 + 1) * 512],
                        start=True, stop=True), B.s_mm)
                rb_tgt = B.s_mm.n
                P.wait("vector", B.s_mm, rb_tgt)
                for h01 in range(2):
                    h = hp * 2 + h01
                    P.op("vector", lambda e, h01=h01: e.tensor_copy(
                        B.m_b[:, h01 * 512:(h01 + 1) * 512], bank(B, 2 + h01)), B.s_dve)
                    dst = B.xn[:, h * TC + qh * 512: h * TC + (qh + 1) * 512]
                    P.op("vector", lambda e, d=dst, h01=h01: e.tensor_mul(
                        d, bank(B, 6 + h01), B.m_b[:, h01 * 512:(h01 + 1) * 512]),
                        B.s_dve)
                    B.g[("pv", h01)] = B.s_dve.n
                B.g["exp_done"] = B.s_act.n
                B.g[("rb_read")] = B.s_dve.n
                P.wait("tensor", B.s_dve, B.g[("rb_read")])
        attn_done = B.s_dve.n

        P.wait("tensor", B.s_dve, attn_done)
        P.wait("gpsimd", B.s_dve, attn_done)
        emit_proj(P, B, 8, lambda og, slot: slab_load(wo, og, slot),
                  lambda k, th: B.xn[:, k * TC + th * 512: k * TC + (th + 1) * 512],
                  lambda o, th: B.qs[:, o * TC + th * 512: o * TC + (th + 1) * 512],
                  KC, 256, og_cols=256)
        y_all = B.s_act.n

        P.wait("vector", B.s_act, y_all)
        P.wait("scalar", B.s_act, y_all)
        P.wait("tensor", B.s_act, y_all)
        y_get = lambda k, th: B.qs[:, k * TC + th * 512: k * TC + (th + 1) * 512]
        s_out = [P.sem("s_out0"), P.sem("s_out1")]
        for th in range(2):
            emit_ln_stats(P, B, lambda k, th=th: y_get(k, th), th, KC,
                          inv_d, B.m_b[:, th * 512:(th + 1) * 512],
                          B.r_b[:, th * 512:(th + 1) * 512], eps_ln,
                          cast_first=False)
            for k in range(KC):
                m_ap = B.m_b[:, th * 512:(th + 1) * 512]
                r_ap = B.r_b[:, th * 512:(th + 1) * 512]
                osl = k % 2
                o_ap = B.oscr[:, osl * 512:(osl + 1) * 512]
                ap, slot = xst.get(k, th)
                P.op("vector", lambda e, y=y_get(k, th), m=m_ap:
                     e.tensor_sub(B.t1[:, :], y, m))
                P.op("vector", lambda e, r=r_ap: e.tensor_mul(B.t1[:, :], B.t1[:, :], r))
                P.wait("vector", s_out[osl], B.g.get(("oscr", osl), 0))
                t = P.op("vector", lambda e, o=o_ap, x=ap:
                         e.tensor_add(o, x, B.t1[:, :]), B.s_dve)
                xst.mark(slot, t)
                P.wait("sync", B.s_dve, t)
                B.g[("oscr", osl)] = P.dma(
                    "sync", xout[k * 128:(k + 1) * 128, th * 512:(th + 1) * 512],
                    o_ap, s_out[osl])
        P.wait("sync", s_out[0], s_out[0].n)
        P.wait("sync", s_out[1], s_out[1].n)
    es2.close()


def prep_l1_attn(inp):
    wq_b = inp["L1_attn_wq_b"]
    perm = np.zeros(D, dtype=np.int64)
    for h in range(NH):
        perm[h * 128: h * 128 + 64] = h * 128 + 64 + np.arange(64)
        perm[h * 128 + 64: h * 128 + 128] = h * 128 + np.arange(64)
    wq_b_p = np.ascontiguousarray(wq_b[:, perm])
    wkv_b = inp["L1_attn_wkv_b"]
    cols_n = np.concatenate([h * 192 + np.arange(64) for h in range(NH)])
    cols_v = np.concatenate([h * 192 + 64 + np.arange(128) for h in range(NH)])
    wkvb_n = np.ascontiguousarray(wkv_b[:, cols_n])
    wkvb_v = np.ascontiguousarray(wkv_b[:, cols_v])
    qnw = np.ascontiguousarray(inp["L1_attn_q_a_norm_w"].reshape(6, 128).T)
    kvnw = np.ascontiguousarray(inp["L1_attn_kv_a_norm_w"].reshape(4, 128).T)
    cv128 = np.repeat(CV_VALS, 128, axis=0)
    inv_freq = (1.0 / 10000.0 ** (np.arange(0, ROPE, 2, dtype=np.float64) / ROPE))
    maps = []
    for c in range(NCORES):
        tpos = np.arange(c * TC, (c + 1) * TC, dtype=np.float64)
        freqs = np.outer(inv_freq, tpos)
        maps.append({
            "wq_a": inp["L1_attn_wq_a"], "qnw": qnw, "wq_b": wq_b_p,
            "wkv_a": inp["L1_attn_wkv_a"], "kvnw": kvnw,
            "wkvb_n": wkvb_n, "wkvb_v": wkvb_v, "wo": inp["L1_attn_wo"],
            "cos_in": np.cos(freqs).astype(np.float32),
            "sin_in": np.sin(freqs).astype(np.float32),
            "cv": CV_VALS, "cv128": cv128,
        })
    return maps



HID = 5632
HC = HID // 128
DG = 4
HCG = HC // DG
D_TRUNC = 4


def emit_l1_ffn(P, B, nc, xin, wqgT, wquT, wqdT, sg_c, su_c, sd_c, cv_in, xout):
    es2 = contextlib.ExitStack()
    if True:
        sb = lambda name, shape, dt: es2.enter_context(
            nc.sbuf_tensor("D_" + name, shape, dt))
        init_done = B.init_done

        B.xn = sb("xn", [128, KC * TC], BF16)
        B.inter = sb("inter", [128, HCG * TC], BF16)
        B.acc = sb("acc", [128, KC * TC], F32)
        B.wslab = sb("wslab", [128, 2 * 4096], BF16)
        B.slab_stride = 4096
        B.dslab = sb("dslab", [128, 2 * HCG * 128], BF16)
        B.xs = sb("xs", [128, 4 * 512], F32)
        B.sgc = sb("sgc", [128, HC], F32)
        B.suc = sb("suc", [128, HC], F32)
        B.sdc = sb("sdc", [128, KC], F32)
        B.m128 = sb("m128", [1, 128], BF16)
        B.sx_bf = sb("sx_bf", [1, TC], BF16)
        B.si_bf = sb("si_bf", [1, TC], BF16)

        eps_ln = B.cv[0:1, CV_EPS_LN:CV_EPS_LN + 1]
        inv_d = 1.0 / 2048.0
        P.dma("sync", B.sgc[:, :], sg_c, B.s_x)
        P.dma("sync", B.suc[:, :], su_c, B.s_x)
        P.dma("sync", B.sdc[:, :], sd_c, B.s_x)
        P.wait("scalar", B.s_x, B.s_x.n)
        P.wait("vector", B.s_x, B.s_x.n)
        P.op("vector", lambda e: e.memset(B.m128[:, :], -128.0), B.s_dve)

        xst = XStream(P, B, xin)

        def stats_src(k, th):
            ap, slot = xst.get(k, th)
            B.g["_cur_xs_slot"] = slot
            return ap

        def xn_put(k, th):
            return B.xn[:, k * TC + th * 512: k * TC + (th + 1) * 512]

        xn_done = []
        for th in range(2):
            emit_ln_stats(P, B, lambda k, th=th: stats_src(k, th), th, KC,
                          inv_d, B.m_b[:, th * 512:(th + 1) * 512],
                          B.r_b[:, th * 512:(th + 1) * 512], eps_ln,
                          cast_first=True)
            for k in range(KC):
                ap, slot = xst.get(k, th)
                m_ap = B.m_b[:, th * 512:(th + 1) * 512]
                r_ap = B.r_b[:, th * 512:(th + 1) * 512]
                P.op("vector", lambda e, x=ap, m=m_ap: e.tensor_sub(B.t1[:, :], x, m))
                t = P.op("vector", lambda e, o=xn_put(k, th), r=r_ap:
                         e.tensor_mul(o, B.t1[:, :], r), B.s_dve)
                xst.mark(slot, t)
            xn_done.append(B.s_dve.n)
        P.wait("tensor", B.s_dve, xn_done[1])

        for th in range(2 if D_TRUNC >= 1 else 0):
            P.wait("tensor", B.s_dve, B.s_dve.n)
            for k in range(KC):
                P.op("tensor", lambda e, r=xn_put(k, th), st=(k == 0), sp=(k == KC - 1):
                     e.matmul(bank(B, 0, parts=1), B.ones_col[:, :], r,
                              start=st, stop=sp),
                     B.s_mm if k == KC - 1 else None)
            P.wait("vector", B.s_mm, B.s_mm.n)
            P.op("vector", lambda e, th=th: e.tensor_copy(
                B.sx_bf[0:1, th * 512:(th + 1) * 512], bank(B, 0, parts=1)), B.s_dve)
        sx_done = B.s_dve.n
        P.wait("tensor", B.s_dve, sx_done)
        if D_TRUNC < 1:
            sx_done = B.s_dve.n

        s_wd = [P.sem("s_wdq0"), P.sem("s_wdq1")]
        inter_free = 0
        for g in range(DG if D_TRUNC >= 4 else (1 if D_TRUNC >= 2 else 0)):
            for ic in range(HCG):
                hc = g * HCG + ic
                slot = ic % 2
                P.wait("gpsimd", B.s_mm, B.g.get(("gu_slab", slot), 0))
                P.dma("gpsimd", sb3(B.wslab[:, slot * 4096: slot * 4096 + 2048], KC),
                      chunked(wqgT[:, hc * 128:(hc + 1) * 128]), B.s_w[slot])
                wt = P.dma("gpsimd",
                           sb3(B.wslab[:, slot * 4096 + 2048: slot * 4096 + 4096], KC),
                           chunked(wquT[:, hc * 128:(hc + 1) * 128]), B.s_w[slot])
                P.wait("tensor", B.s_w[slot], wt)
                if ic == 0:
                    P.wait("tensor", B.s_mm, inter_free)
                for th in range(2):
                    gbk = (ic * 2 + th) % 2
                    pg = bank(B, 4 + gbk)
                    pu = bank(B, 2 + gbk)
                    P.wait("tensor", B.s_act, B.g.get(("gact", gbk), 0))
                    P.wait("tensor", B.s_dve, B.g.get(("uact", gbk), 0))
                    for which, base, pm in (("g", 0, pg), ("u", 2048, pu)):
                        for k in range(KC):
                            lw = B.wslab[:, slot * 4096 + base + k * 128:
                                         slot * 4096 + base + (k + 1) * 128]
                            P.op("tensor", lambda e, p=pm, l=lw, r=xn_put(k, th),
                                 st=(k == 0): e.matmul(p, l, r, start=st, stop=False))
                        P.op("tensor", lambda e, p=pm, th=th: e.matmul(
                            p, B.m128[:, :], B.sx_bf[0:1, th * 512:(th + 1) * 512],
                            start=False, stop=True),
                            B.s_mm)
                    gu_mm = B.s_mm.n
                    sg = B.scr_a[:, th * 512:(th + 1) * 512]
                    P.wait("scalar", B.s_mm, gu_mm)
                    P.wait("scalar", B.s_dve, B.g.get(("sg", th), 0))
                    P.op("scalar", lambda e, d=sg, p=pg, hc=hc: e.activation(
                        d, p, AF.Silu, scale=B.sgc[:, hc:hc + 1]), B.s_act)
                    B.g[("gact", gbk)] = B.s_act.n
                    it = B.inter[:, ic * TC + th * 512: ic * TC + (th + 1) * 512]
                    P.wait("vector", B.s_mm, gu_mm)
                    P.wait("vector", B.s_act, B.g[("gact", gbk)])
                    P.op("vector", lambda e, d=it, s=sg, p=pu, hc=hc:
                         e.scalar_tensor_tensor(d, p, B.suc[:, hc:hc + 1], s,
                                                ALU.mult, ALU.mult), B.s_dve)
                    B.g[("uact", gbk)] = B.s_dve.n
                    B.g[("sg", th)] = B.s_dve.n
                B.g[("gu_slab", slot)] = B.s_mm.n
            inter_done = B.s_dve.n
            P.wait("tensor", B.s_dve, inter_done)
            for th in range(2):
                P.wait("tensor", B.s_dve, B.s_dve.n)
                for ic in range(HCG):
                    it = B.inter[:, ic * TC + th * 512: ic * TC + (th + 1) * 512]
                    P.op("tensor", lambda e, r=it, st=(ic == 0), sp=(ic == HCG - 1):
                         e.matmul(bank(B, 1, parts=1), B.ones_col[:, :], r,
                                  start=st, stop=sp),
                         B.s_mm if ic == HCG - 1 else None)
                P.wait("vector", B.s_mm, B.s_mm.n)
                P.op("vector", lambda e, th=th: e.tensor_copy(
                    B.si_bf[0:1, th * 512:(th + 1) * 512], bank(B, 1, parts=1)),
                    B.s_dve)
            si_done = B.s_dve.n
            P.wait("tensor", B.s_dve, si_done)
            for o in range(KC if D_TRUNC >= 3 else 0):
                dslot = o % 2
                P.wait("gpsimd", B.s_mm, B.g.get(("d_slab", dslot), 0))
                dt_ = P.dma("gpsimd",
                            sb3(B.dslab[:, dslot * HCG * 128:(dslot + 1) * HCG * 128],
                                HCG),
                            chunked(wqdT[g * HCG * 128:(g + 1) * HCG * 128,
                                         o * 128:(o + 1) * 128]), s_wd[dslot])
                P.wait("tensor", s_wd[dslot], dt_)
                for th in range(2):
                    dbk = (o * 2 + th) % 2
                    pd = bank(B, 6 + dbk)
                    P.wait("tensor", B.s_dve, B.g.get(("dacc", dbk), 0))
                    for ic in range(HCG):
                        ld = B.dslab[:, dslot * HCG * 128 + ic * 128:
                                     dslot * HCG * 128 + (ic + 1) * 128]
                        rh = B.inter[:, ic * TC + th * 512: ic * TC + (th + 1) * 512]
                        P.op("tensor", lambda e, p=pd, l=ld, r=rh, st=(ic == 0):
                             e.matmul(p, l, r, start=st, stop=False))
                    P.op("tensor", lambda e, p=pd, th=th: e.matmul(
                        p, B.m128[:, :], B.si_bf[0:1, th * 512:(th + 1) * 512],
                        start=False, stop=True), B.s_mm)
                    dmm = B.s_mm.n
                    a_ap = B.acc[:, o * TC + th * 512: o * TC + (th + 1) * 512]
                    P.wait("vector", B.s_mm, dmm)
                    if g == 0:
                        P.op("vector", lambda e, a=a_ap, p=pd, o=o: e.tensor_scalar(
                            a, p, B.sdc[:, o:o + 1], None, ALU.mult), B.s_dve)
                    else:
                        P.op("vector", lambda e, a=a_ap, p=pd, o=o:
                             e.scalar_tensor_tensor(a, p, B.sdc[:, o:o + 1], a,
                                                    ALU.mult, ALU.add), B.s_dve)
                    B.g[("dacc", dbk)] = B.s_dve.n
                B.g[("d_slab", dslot)] = B.s_mm.n
            inter_free = B.s_mm.n

        acc_done = B.s_dve.n
        P.wait("vector", B.s_dve, acc_done)
        P.wait("scalar", B.s_dve, acc_done)
        P.wait("tensor", B.s_dve, acc_done)

        def acc_get(k, th):
            return B.acc[:, k * TC + th * 512: k * TC + (th + 1) * 512]

        res_done = 0
        for th in range(2):
            emit_ln_stats(P, B, lambda k, th=th: acc_get(k, th), th, KC,
                          inv_d, B.m_b[:, th * 512:(th + 1) * 512],
                          B.r_b[:, th * 512:(th + 1) * 512], eps_ln,
                          cast_first=True)
            for k in range(KC):
                a_ap = acc_get(k, th)
                m_ap = B.m_b[:, th * 512:(th + 1) * 512]
                r_ap = B.r_b[:, th * 512:(th + 1) * 512]
                ap, slot = xst.get(k, th)
                P.op("vector", lambda e, a=a_ap, m=m_ap: e.tensor_sub(B.t1[:, :], a, m))
                P.op("vector", lambda e, r=r_ap: e.tensor_mul(B.t1[:, :], B.t1[:, :], r))
                t = P.op("vector", lambda e, a=a_ap, x=ap:
                         e.tensor_add(a, x, B.t1[:, :]), B.s_dve)
                xst.mark(slot, t)
            res_done = B.s_dve.n

        P.wait("sync", B.s_dve, res_done)
        P.dma("sync", chunked(xout), sb3(B.acc[:, :], KC), B.s_st)
        P.wait("sync", B.s_st, B.s_st.n)
    es2.close()


def _u8(a):
    return np.ascontiguousarray(a.view(np.uint8).reshape(*a.shape, 4)[..., 0])


def prep_l1_ffn(inp):
    wqgT = np.ascontiguousarray(_u8(inp["L1_ffn_wq_gate"]).T)
    wquT = np.ascontiguousarray(_u8(inp["L1_ffn_wq_up"]).T)
    wqdT = np.ascontiguousarray(_u8(inp["L1_ffn_wq_down"]).T)
    sg_c = np.ascontiguousarray(inp["L1_ffn_s_gate"].reshape(HC, 128).T)
    su_c = np.ascontiguousarray(inp["L1_ffn_s_up"].reshape(HC, 128).T)
    sd_c = np.ascontiguousarray(inp["L1_ffn_s_down"].reshape(KC, 128).T)
    shared = {"wqgT": wqgT, "wquT": wquT, "wqdT": wqdT,
              "sg_c": sg_c, "su_c": su_c, "sd_c": sd_c, "cv": CV_VALS}
    return [dict(shared) for _ in range(NCORES)]



def _shard_x(x):
    xs = x.reshape(NCORES, TC, D)
    return [np.ascontiguousarray(xs[c].T) for c in range(NCORES)]


N_CTRL = 130156


def _spline_grid():
    n = INTER * D
    t = np.linspace(0.0, N_CTRL - 1.0, n, dtype=np.float64)
    i = np.clip(np.floor(t).astype(np.int64), 0, N_CTRL - 2)
    f = (t - i).astype(np.float32)
    idx = [np.clip(i + k, 0, N_CTRL - 1).astype(np.int32) for k in (-1, 0, 1, 2)]
    f64 = f.astype(np.float64)
    f2, f3 = f64 * f64, f64 * f64 * f64
    c0 = (0.5 * (-f64 + 2.0 * f2 - f3)).astype(np.float32)
    c1 = (0.5 * (2.0 - 5.0 * f2 + 3.0 * f3)).astype(np.float32)
    c2 = (0.5 * (f64 + 4.0 * f2 - 3.0 * f3)).astype(np.float32)
    c3 = (0.5 * (f3 - f2)).astype(np.float32)
    return idx, (c0, c1, c2, c3)


def _spline_shards(inp):
    idx, cs = _spline_grid()
    cp_g = inp["L0_ffn_gate_cp"]
    cp_u = inp["L0_ffn_up_cp"]
    cp_d = inp["L0_ffn_down_cp"]
    shards = [dict() for _ in range(NCORES)]
    for k in range(4):
        ik = idx[k]
        for name, cp in (("p_g", cp_g), ("p_u", cp_u)):
            G = np.take(cp, ik).reshape(NCORES, TC, D)
            for c in range(NCORES):
                shards[c][f"{name}{k}"] = np.ascontiguousarray(G[c].T)
        C = cs[k].reshape(NCORES, TC, D)
        for c in range(NCORES):
            shards[c][f"c_gu{k}"] = np.ascontiguousarray(C[c].T)
        Gd = np.take(cp_d, ik).reshape(D, INTER)
        Cd = cs[k].reshape(D, INTER)
        for c in range(NCORES):
            shards[c][f"p_d{k}"] = np.ascontiguousarray(Gd[:, c * TC:(c + 1) * TC].T)
            shards[c][f"c_cd{k}"] = np.ascontiguousarray(Cd[:, c * TC:(c + 1) * TC].T)
    return shards


def prep_l0_ffn(inp):
    shards = _spline_shards(inp)
    for m in shards:
        m["cv"] = CV_VALS
    return shards


def prep_l0_attn(inp):
    shared = {
        "wqT": np.ascontiguousarray(inp["L0_attn_Wq"].T),
        "wkT": np.ascontiguousarray(inp["L0_attn_Wk"].T),
        "wvT": np.ascontiguousarray(inp["L0_attn_Wv"].T),
        "woT": np.ascontiguousarray(inp["L0_attn_Wo"].T),
        "bq_c": np.ascontiguousarray(inp["L0_attn_bq"].reshape(KC, 128).T),
        "bo_c": np.ascontiguousarray(inp["L0_attn_bo"].reshape(KC, 128).T),
        "bk_r": inp["L0_attn_bk"].reshape(1, D),
        "bv_r": inp["L0_attn_bv"].reshape(1, D),
        "cv": CV_VALS,
    }
    return [dict(shared) for _ in range(NCORES)]



def phase_barrier(P, B):
    s = B.s_bar
    for eng in P.ENGINES:
        P.op(eng, lambda e: e.nop(), s)
    tgt = s.n
    for eng in P.ENGINES:
        P.wait(eng, s, tgt)
    B.g = {}


def declare_drams(nc):
    dr = {}

    def inp(name, shape, dt=F32):
        dr[name] = nc.dram_tensor(name, shape, dt, kind="ExternalInput").ap()

    def internal(name, shape, dt=F32):
        dr[name] = nc.dram_tensor(name, shape, dt).ap()

    inp("xin", [D, TC])
    inp("cv", [1, NCONST])
    for n in ("wqT", "wkT", "wvT", "woT"):
        inp(n, [D, D])
    inp("bq_c", [128, KC]); inp("bo_c", [128, KC])
    inp("bk_r", [1, D]); inp("bv_r", [1, D])
    internal("kv_cc_in", [128, D]); internal("kv_cc_out", [128, D])
    for w, shape in (("g", [D, TC]), ("u", [D, TC]), ("d", [TC, D])):
        for k in range(4):
            inp(f"p_{w}{k}", shape)
    for w, shape in (("gu", [D, TC]), ("cd", [TC, D])):
        for k in range(4):
            inp(f"c_{w}{k}", shape)
    internal("aggu_in", [2 * D, TC], BF16)
    internal("aggu_out", [NCORES * 2 * D, TC], BF16)
    internal("agd_in", [TC, D], BF16)
    internal("agd_out", [T, D], BF16)
    inp("wq_a", [D, QR]); inp("qnw", [128, QR // 128]); inp("wq_b", [QR, D])
    inp("wkv_a", [D, KVR + ROPE]); inp("kvnw", [128, KVR // 128])
    inp("wkvb_n", [KVR, NH * 64]); inp("wkvb_v", [KVR, NH * VD])
    inp("wo", [D, D]); inp("cos_in", [32, TC]); inp("sin_in", [32, TC])
    inp("cv128", [128, NCONST])
    internal("agk_in", [NH * 128, TC], BF16)
    internal("agk_out", [NCORES * NH * 128, TC], BF16)
    internal("agv_in", [TC, NH * VD], BF16)
    internal("agv_out", [T, NH * VD], BF16)
    inp("wqgT", [D, HID], U8); inp("wquT", [D, HID], U8); inp("wqdT", [HID, D], U8)
    inp("sg_c", [128, HC]); inp("su_c", [128, HC]); inp("sd_c", [128, KC])
    internal("xmid1", [D, TC]); internal("xmid2", [D, TC]); internal("xmid3", [D, TC])
    dr["xout"] = nc.dram_tensor("xout", [D, TC], F32, kind="ExternalOutput").ap()
    return dr


def build_fused():
    nc = bass.Bass("TRN2", target_bir_lowering=False)
    es = contextlib.ExitStack()
    P = Plan(nc, es)
    dr = declare_drams(nc)
    with es:
        class B:
            pass
        make_common(P, nc, es, B)
        B.s_bar = P.sem("s_bar")
        spl = {k: dr[k] for k in dr if k.startswith(("p_g", "p_u", "p_d", "c_gu", "c_cd"))}
        es_sp = contextlib.ExitStack()
        B.spline = Spline(P, B, nc, es_sp, spl, dr["aggu_in"], dr["agd_in"])
        B.sp_hook = B.spline.hook
        emit_l0_attn(P, B, nc, dr["xin"], dr["wqT"], dr["wkT"], dr["wvT"],
                     dr["woT"], dr["bq_c"], dr["bo_c"], dr["bk_r"], dr["bv_r"],
                     dr["cv"], dr["xmid1"], dr["kv_cc_in"], dr["kv_cc_out"])
        B.spline.aggu_out_ap = dr["aggu_out"]
        B.spline.agd_out_ap = dr["agd_out"]
        s_ccB = P.sem("s_ccB")
        s_ccB2 = P.sem("s_ccB2")
        B.spline.finish(s_ccB, s_ccB2)
        es_sp.close()
        phase_barrier(P, B)
        emit_l0_ffn(P, B, nc, dr["xmid1"], dr["cv"], spl, dr["xmid2"],
                    dr["aggu_in"], dr["aggu_out"], dr["agd_in"], dr["agd_out"],
                    cc_waits=[(s_ccB, s_ccB.n), (s_ccB2, s_ccB2.n)])
        phase_barrier(P, B)
        emit_l1_attn(P, B, nc, dr["xmid2"], dr["wq_a"], dr["qnw"], dr["wq_b"],
                     dr["wkv_a"], dr["kvnw"], dr["wkvb_n"], dr["wkvb_v"],
                     dr["wo"], dr["cos_in"], dr["sin_in"], dr["cv"],
                     dr["cv128"], dr["xmid3"],
                     dr["agk_in"], dr["agk_out"], dr["agv_in"], dr["agv_out"])
        phase_barrier(P, B)
        emit_l1_ffn(P, B, nc, dr["xmid3"], dr["wqgT"], dr["wquT"], dr["wqdT"],
                    dr["sg_c"], dr["su_c"], dr["sd_c"], dr["cv"], dr["xout"])
        P.build()
    return nc


def kernel(**inputs):
    x = inputs["L0_attn_x"].astype(np.float32, copy=False)
    xs = _shard_x(x)
    pa = prep_l0_attn(inputs)
    pb = prep_l0_ffn(inputs)
    pc = prep_l1_attn(inputs)
    pd = prep_l1_ffn(inputs)
    in_maps = []
    for c in range(NCORES):
        m = {}
        m.update(pa[c]); m.update(pb[c]); m.update(pc[c]); m.update(pd[c])
        m["xin"] = xs[c]
        in_maps.append(m)
    nc = build_fused()
    res = run_bass_kernel_spmd(nc, in_maps, core_ids=list(range(NCORES)))
    out = np.concatenate([r["xout"].T[None] for r in res.results], axis=1)
    return np.ascontiguousarray(out.reshape(1, T, D))


# version 05
# speedup vs baseline: 1.2737x; 1.0242x over previous
import contextlib
import numpy as np
import ml_dtypes
import concourse.bass as bass
import concourse.mybir as mybir
from concourse.bass_utils import run_bass_kernel_spmd

F32 = mybir.dt.float32
BF16 = mybir.dt.bfloat16
FP16 = mybir.dt.float16
U8 = mybir.dt.uint8
AF = mybir.ActivationFunctionType
ALU = mybir.AluOpType

NCORES = 8
T = 8192
TC = T // NCORES
D = 2048
KC = D // 128
BFD = np.dtype(ml_dtypes.bfloat16)

CV_EPS_LN = 0
CV_NHALF = 1
CV_INV_D = 2
CV_KVSCALE = 3
CV_EPS_RMS = 4
CV_INV_QR = 5
CV_INV_KVR = 6
CV_ATTNSCALE = 7
NCONST = 8
CV_VALS = np.array([[1e-5, -0.5, 1.0 / 2048.0, 1.0 / np.sqrt(128.0),
                     float(np.finfo(np.float32).eps), 1.0 / 768.0,
                     1.0 / 512.0, 128.0 ** -0.5]], dtype=np.float32)


class Sem:
    __slots__ = ("h", "n", "name")

    def __init__(self, h, name):
        self.h = h
        self.n = 0
        self.name = name


class Plan:

    ENGINES = ("sync", "tensor", "vector", "scalar", "gpsimd")

    def __init__(self, nc, es):
        self.nc = nc
        self.es = es
        self.ops = {e: [] for e in self.ENGINES}

    def sem(self, name):
        return Sem(self.es.enter_context(self.nc.semaphore(name)), name)

    def emit(self, eng, fn):
        self.ops[eng].append(fn)

    def wait(self, eng, sem, val=None):
        v = sem.n if val is None else val
        if v > 0:
            self.emit(eng, lambda e, s=sem.h, v=v: e.wait_ge(s, v))

    def dma(self, eng, out, in_, sem=None, **kw):
        tgt = None
        if sem is not None:
            sem.n += 16
            tgt = sem.n
        def fn(e, out=out, in_=in_, sem=sem, kw=kw):
            ins = e.dma_start(out=out, in_=in_, **kw)
            if sem is not None:
                ins.then_inc(sem.h, 16)
        self.emit(eng, fn)
        return tgt

    def op(self, eng, fn, sem=None, inc=1):
        tgt = None
        if sem is not None:
            sem.n += inc
            tgt = sem.n
        def wrapped(e, fn=fn, sem=sem, inc=inc):
            ins = fn(e)
            if sem is not None:
                ins.then_inc(sem.h, inc)
        self.emit(eng, wrapped)
        return tgt

    def build(self):
        with self.nc.Block() as block:
            for eng_name in self.ENGINES:
                ops = self.ops[eng_name]
                if not ops:
                    continue
                def body(e, ops=ops):
                    for fn in ops:
                        fn(e)
                getattr(block, eng_name)(body)



def bank(B, i, parts=128, width=512):
    return B.pool[0:parts, i * 512:i * 512 + width]


def chunked(dram_ap):
    return dram_ap.rearrange("(k p) t -> p k t", p=128)


def sb3(sbuf_ap, n):
    return sbuf_ap.rearrange("p (k t) -> p k t", k=n)


def emit_ln_stats(P, B, src, th, n_chunks, inv_n, out_m, out_r,
                  eps_col, cast_first):
    nc = P.nc
    sA, sT, sV = B.s_act, B.s_mm, B.s_dve
    want_m = out_m is not None
    for k in range(n_chunks):
        slot = k % 2
        a = B.scr_a[:, slot * 512:(slot + 1) * 512]
        b = B.scr_b[:, slot * 512:(slot + 1) * 512]
        guard = B.g.get(("lnmm", slot), 0)
        if cast_first:
            P.wait("vector", sT, guard)
            P.op("vector", lambda e, a=a, s=src(k): e.tensor_copy(a, s), sV)
            cast_tgt = sV.n
            xslot = B.g.pop("_cur_xs_slot", None)
            if xslot is not None:
                B.g[("xs", xslot)] = cast_tgt
            P.wait("scalar", sV, cast_tgt)
            stat_in = a
        else:
            stat_in = src(k)
            P.wait("scalar", sT, guard)
        P.op("scalar", lambda e, b=b, s=stat_in: e.activation(b, s, AF.Square), sA)
        sq_tgt = sA.n
        if cast_first:
            P.wait("tensor", sV, cast_tgt)
        st = (k == 0)
        sp = (k == n_chunks - 1)
        if st:
            P.wait("tensor", sV, B.g.get("ln_stats_free", 0))
        if want_m:
            P.op("tensor", lambda e, s=stat_in, st=st, sp=sp:
                 e.matmul(bank(B, 0, parts=1), B.ones_col[:, :], s, start=st, stop=sp))
        P.wait("tensor", sA, sq_tgt)
        P.op("tensor", lambda e, b=b, st=st, sp=sp:
             e.matmul(bank(B, 1, parts=1), B.ones_col[:, :], b, start=st, stop=sp), sT)
        B.g[("lnmm", slot)] = sT.n
    mm_done = sT.n
    ts = th * 512
    m_row = B.m_row[0:1, ts:ts + 512]
    v_row = B.v_row[0:1, ts:ts + 512]
    r_row = B.r_row[0:1, ts:ts + 512]
    m_bf = B.m_bf[0:1, ts:ts + 512]
    r_bf = B.r_bf[0:1, ts:ts + 512]
    P.wait("vector", sT, mm_done)
    P.wait("vector", sT, B.g.get(("row_free", th), 0))
    if want_m:
        P.op("vector", lambda e, m=m_row: e.tensor_scalar(
            m, bank(B, 0, parts=1), inv_n, None, ALU.mult), sV)
        P.op("vector", lambda e, m=m_row, v=v_row: e.tensor_mul(v, m, m), sV)
        P.op("vector", lambda e, v=v_row: e.scalar_tensor_tensor(
            v, bank(B, 1, parts=1), inv_n, v, ALU.mult, ALU.subtract), sV)
    else:
        P.op("vector", lambda e, v=v_row: e.tensor_scalar(
            v, bank(B, 1, parts=1), inv_n, None, ALU.mult), sV)
    B.g["ln_stats_free"] = sV.n
    P.wait("scalar", sV, sV.n)
    P.op("scalar", lambda e, r=r_row, v=v_row, b=eps_col:
         e.activation(r, v, AF.Ln, bias=b), sA)
    P.op("scalar", lambda e, r=r_row: e.activation(
        r, r, AF.Exp, scale=B.cv[0:1, CV_NHALF:CV_NHALF + 1]), sA)
    P.wait("vector", sA, sA.n)
    if want_m:
        P.op("vector", lambda e, m=m_row, mb=m_bf: e.tensor_copy(mb, m), sV)
    P.op("vector", lambda e, r=r_row, rb=r_bf: e.tensor_copy(rb, r), sV)
    rows_done = sV.n
    P.wait("tensor", sV, rows_done)
    P.wait("tensor", sV, B.g.get("ln_bc_free", 0))
    if want_m:
        P.op("tensor", lambda e, mb=m_bf: e.matmul(
            bank(B, 2), B.ones_row[0:1, 0:128], mb, start=True, stop=True))
    P.op("tensor", lambda e, rb=r_bf: e.matmul(
        bank(B, 3), B.ones_row[0:1, 0:128], rb, start=True, stop=True), sT)
    bc_done = sT.n
    B.g[("row_free", th)] = bc_done
    P.wait("vector", sT, bc_done)
    if want_m:
        P.op("vector", lambda e, o=out_m: e.tensor_copy(o, bank(B, 2)), sV)
    P.op("vector", lambda e, o=out_r: e.tensor_copy(o, bank(B, 3)), sV)
    B.g["ln_bc_free"] = sV.n
    return sV.n


def emit_proj(P, B, n_og, wget, rhs_get, out_put, n_k, kstride, bias_col=None,
              scale_col=None, act=None, corr=None, og_cols=512, rhs_parts=128):
    act = act or AF.Identity
    n_oc = og_cols // 128
    P.wait("gpsimd", B.s_mm, B.s_mm.n)
    tgts = {0: wget(0, 0)}
    if n_og > 1:
        tgts[1] = wget(1, 1)
    for og in range(n_og):
        slot = og % 2
        P.wait("tensor", B.s_w[og % 2], tgts[og])
        for oc in range(n_oc):
            o = og * n_oc + oc
            for th in range(2):
                bk = (oc * 2 + th) % 2
                pm = bank(B, 4 + bk)
                prev = B.g.get(("proj_evac", bk), 0)
                P.wait("tensor", B.s_act, prev)
                mms = []
                ss = B.slab_stride
                for k in range(n_k):
                    lhsT = B.wslab[:, slot * ss + k * kstride + oc * 128:
                                   slot * ss + k * kstride + oc * 128 + 128]
                    mms.append((lhsT, rhs_get(k, th)))
                if corr is not None:
                    mms.extend(corr(o, th))
                for i, (l, r) in enumerate(mms):
                    last = (i == len(mms) - 1)
                    P.op("tensor", lambda e, p=pm, l=l, r=r, st=(i == 0), sp=last:
                         e.matmul(p, l, r, start=st, stop=sp),
                         B.s_mm if last else None)
                mm_tgt = B.s_mm.n
                P.wait("scalar", B.s_mm, mm_tgt)
                kw = {}
                if bias_col is not None:
                    kw["bias"] = bias_col(o)
                if scale_col is not None:
                    kw["scale"] = scale_col(o)
                P.op("scalar", lambda e, d=out_put(o, th), p=pm, kw=kw:
                     e.activation(d, p, act, **kw), B.s_act)
                B.g[("proj_evac", bk)] = B.s_act.n
        if og + 2 < n_og:
            P.wait("gpsimd", B.s_mm, B.s_mm.n)
            tgts[og + 2] = wget(og + 2, slot)
    return B.s_act.n


def emit_res_ln(P, B, ybuf_get, x_get, n_chunks, eps_col, inv_n):
    for th in range(2):
        emit_ln_stats(P, B, lambda k, th=th: ybuf_get(k, th), th, n_chunks,
                      inv_n, B.m_b[:, th * 512:(th + 1) * 512],
                      B.r_b[:, th * 512:(th + 1) * 512], eps_col, cast_first=False)
        for k in range(n_chunks):
            y_ap = ybuf_get(k, th)
            x_ap = x_get(k, th)
            m_ap = B.m_b[:, th * 512:(th + 1) * 512]
            r_ap = B.r_b[:, th * 512:(th + 1) * 512]
            P.op("vector", lambda e, y=y_ap, m=m_ap: e.tensor_sub(B.t1[:, :], y, m))
            P.op("vector", lambda e, r=r_ap: e.tensor_mul(B.t1[:, :], B.t1[:, :], r))
            P.op("vector", lambda e, x=x_ap: e.tensor_add(x, x, B.t1[:, :]), B.s_dve)
    return B.s_dve.n


def emit_inner_ln(P, B, x_get, xn_put, n_chunks, eps_col, inv_n):
    done = []
    for th in range(2):
        emit_ln_stats(P, B, lambda k, th=th: x_get(k, th), th, n_chunks,
                      inv_n, B.m_b[:, th * 512:(th + 1) * 512],
                      B.r_b[:, th * 512:(th + 1) * 512], eps_col, cast_first=True)
        for k in range(n_chunks):
            x_ap = x_get(k, th)
            xn_ap = xn_put(k, th)
            m_ap = B.m_b[:, th * 512:(th + 1) * 512]
            r_ap = B.r_b[:, th * 512:(th + 1) * 512]
            P.op("vector", lambda e, x=x_ap, m=m_ap: e.tensor_sub(B.t1[:, :], x, m))
            P.op("vector", lambda e, o=xn_ap, r=r_ap:
                 e.tensor_mul(o, B.t1[:, :], r), B.s_dve)
        done.append(B.s_dve.n)
    return done


def make_common(P, nc, es, B):
    sb = lambda name, shape, dt: es.enter_context(nc.sbuf_tensor(name, shape, dt))
    B.g = {}
    B.pool = es.enter_context(nc.psum_tensor("pool", [128, 4096], F32))
    B.scr_a = sb("scr_a", [128, 1024], BF16)
    B.scr_b = sb("scr_b", [128, 1024], BF16)
    B.t1 = sb("t1", [128, 512], F32)
    B.m_b = sb("m_b", [128, 1024], F32)
    B.r_b = sb("r_b", [128, 1024], F32)
    B.m_row = sb("m_row", [1, 1024], F32)
    B.v_row = sb("v_row", [1, 1024], F32)
    B.r_row = sb("r_row", [1, 1024], F32)
    B.m_bf = sb("m_bf", [1, 1024], BF16)
    B.r_bf = sb("r_bf", [1, 1024], BF16)
    B.ones_col = sb("ones_col", [128, 1], BF16)
    B.ones_row = sb("ones_row", [1, 512], BF16)
    B.cv = sb("cv_sb", [1, NCONST], F32)
    B.s_x = P.sem("s_x")
    B.s_w = [P.sem("s_w0"), P.sem("s_w1")]
    B.s_dve = P.sem("s_dve")
    B.s_act = P.sem("s_act")
    B.s_mm = P.sem("s_mm")
    B.s_cc = P.sem("s_cc")
    B.s_st = P.sem("s_st")
    P.op("vector", lambda e: e.memset(B.ones_col[:, :], 1.0))
    P.op("vector", lambda e: e.memset(B.ones_row[:, :], 1.0), B.s_dve)
    B.init_done = B.s_dve.n
    return B.init_done



def emit_l0_attn(P, B, nc, xin, wqT, wkT, wvT, woT, bq_c, bo_c, bk_r, bv_r,
                 cv_in, xout, kv_cc_in, kv_cc_out):
    es2 = contextlib.ExitStack()
    if True:
        sb = lambda name, shape, dt: es2.enter_context(
            nc.sbuf_tensor("A_" + name, shape, dt))
        init_done = B.init_done

        B.xn = sb("xn", [128, KC * TC], BF16)
        B.xs = sb("xs", [128, 4 * 512], F32)
        B.oscr = sb("oscr", [128, 2 * 512], F32)
        B.qb = sb("qb", [128, KC * TC], BF16)
        B.wslab = sb("wslab", [128, 2 * 4096], BF16)
        B.slab_stride = 4096
        B.kst = sb("kst", [128, 2 * 128], BF16)
        B.vst = sb("vst", [128, 2 * 128], BF16)
        B.kvacc = sb("kvacc", [128, D], F32)
        B.kvb = sb("kvb", [128, D], BF16)
        B.bqc = sb("bqc", [128, KC], F32)
        B.boc = sb("boc", [128, KC], F32)
        B.bkr = sb("bkr", [1, D], BF16)
        B.bvr = sb("bvr", [1, D], BF16)

        P.dma("sync", B.cv[:, :], cv_in, B.s_x)
        P.dma("sync", B.bqc[:, :], bq_c, B.s_x)
        P.dma("sync", B.boc[:, :], bo_c, B.s_x)
        P.dma("gpsimd", B.bkr[:, :], bk_r, B.s_x)
        P.dma("gpsimd", B.bvr[:, :], bv_r, B.s_x)
        x_loaded = B.s_x.n
        B.sp_hook(6)

        eps_ln = B.cv[0:1, CV_EPS_LN:CV_EPS_LN + 1]
        inv_d = 1.0 / 2048.0

        def slab_load(w, og, slot, cols=256):
            src = chunked(w[:, og * cols:(og + 1) * cols])
            t = P.dma("gpsimd",
                      sb3(B.wslab[:, slot * 4096: slot * 4096 + KC * cols], KC),
                      src, B.s_w[slot])
            B.sp_hook(2)
            return t

        P.wait("vector", B.s_x, x_loaded)
        P.wait("scalar", B.s_x, x_loaded)
        P.wait("tensor", B.s_dve, init_done)

        xst = XStream(P, B, xin)

        def stats_src(k, th):
            ap, slot = xst.get(k, th)
            B.g["_cur_xs_slot"] = slot
            return ap

        def xn_put(k, th):
            return B.xn[:, k * TC + th * 512: k * TC + (th + 1) * 512]

        xn_done = []
        for th in range(2):
            emit_ln_stats(P, B, lambda k, th=th: stats_src(k, th), th, KC,
                          inv_d, B.m_b[:, th * 512:(th + 1) * 512],
                          B.r_b[:, th * 512:(th + 1) * 512], eps_ln,
                          cast_first=True)
            for k in range(KC):
                ap, slot = xst.get(k, th)
                m_ap = B.m_b[:, th * 512:(th + 1) * 512]
                r_ap = B.r_b[:, th * 512:(th + 1) * 512]
                P.op("vector", lambda e, x=ap, m=m_ap: e.tensor_sub(B.t1[:, :], x, m))
                t = P.op("vector", lambda e, o=xn_put(k, th), r=r_ap:
                         e.tensor_mul(o, B.t1[:, :], r), B.s_dve)
                xst.mark(slot, t)
            xn_done.append(B.s_dve.n)

        P.wait("tensor", B.s_dve, xn_done[1])

        def q_out(o, th):
            return B.qb[:, o * TC + th * 512: o * TC + (th + 1) * 512]

        emit_proj(P, B, 8, lambda og, slot: slab_load(wqT, og, slot),
                  lambda k, th: xn_put(k, th), q_out, KC, 256,
                  bias_col=lambda o: B.bqc[:, o:o + 1], og_cols=256)
        q_all = B.s_act.n

        def kv_slab_load(dg, slot):
            P.dma("gpsimd", sb3(B.wslab[:, slot * 4096: slot * 4096 + KC * 128], KC),
                  chunked(wkT[:, dg * 128:(dg + 1) * 128]), B.s_w[slot])
            t = P.dma("gpsimd",
                      sb3(B.wslab[:, slot * 4096 + 2048:
                                  slot * 4096 + 2048 + KC * 128], KC),
                      chunked(wvT[:, dg * 128:(dg + 1) * 128]), B.s_w[slot])
            B.sp_hook(2)
            return t

        P.wait("gpsimd", B.s_act, q_all)
        kvt = {0: kv_slab_load(0, 0), 1: kv_slab_load(1, 1)}
        P.wait("tensor", B.s_act, q_all)
        for dg in range(16):
            slot = dg % 2
            P.wait("tensor", B.s_w[dg % 2], kvt[dg])
            P.wait("tensor", B.s_dve, B.g.get("kva_evac", 0))
            for s in range(8):
                sbk = s % 2
                for which, base, pm, dst, brow, gkey in (
                    ("k", 0, bank(B, 4 + sbk, width=128),
                     B.kst[:, sbk * 128:(sbk + 1) * 128], B.bkr, ("kprod", sbk)),
                    ("v", 2048, bank(B, 2 + sbk, width=128),
                     B.vst[:, sbk * 128:(sbk + 1) * 128], B.bvr, ("vprod", sbk)),
                ):
                    P.wait("tensor", B.s_act, B.g.get(gkey, 0))
                    for k in range(KC):
                        lhsT = B.xn[:, k * TC + s * 128: k * TC + s * 128 + 128]
                        rhs = B.wslab[:, slot * 4096 + base + k * 128:
                                      slot * 4096 + base + k * 128 + 128]
                        P.op("tensor", lambda e, p=pm, l=lhsT, r=rhs, st=(k == 0):
                             e.matmul(p, l, r, start=st, stop=False))
                    P.op("tensor", lambda e, p=pm, b=brow, dg=dg: e.matmul(
                        p, B.ones_row[0:1, 0:128], b[0:1, dg * 128:(dg + 1) * 128],
                        start=False, stop=True), B.s_mm)
                    mm_tgt = B.s_mm.n
                    P.wait("scalar", B.s_mm, mm_tgt)
                    P.op("scalar", lambda e, d=dst, p=pm:
                         e.activation(d, p, AF.Identity), B.s_act)
                    B.g[gkey] = B.s_act.n
                P.wait("tensor", B.s_act, B.g[("vprod", sbk)])
                lhsT = B.kst[:, sbk * 128:(sbk + 1) * 128]
                rhs = B.vst[:, sbk * 128:(sbk + 1) * 128]
                pm = B.pool[:, 6 * 512: 6 * 512 + 128]
                P.op("tensor", lambda e, p=pm, l=lhsT, r=rhs, st=(s == 0),
                     sp=(s == 7): e.matmul(p, l, r, start=st, stop=sp),
                     B.s_mm if s == 7 else None)
            P.wait("vector", B.s_mm, B.s_mm.n)
            P.op("vector", lambda e, dg=dg: e.tensor_copy(
                B.kvacc[:, dg * 128:(dg + 1) * 128], bank(B, 6, width=128)), B.s_dve)
            B.g["kva_evac"] = B.s_dve.n
            if dg + 2 < 16:
                P.wait("gpsimd", B.s_mm, B.s_mm.n)
                kvt[dg + 2] = kv_slab_load(dg + 2, slot)

        P.wait("sync", B.s_dve, B.g["kva_evac"])
        P.dma("sync", kv_cc_in, B.kvacc[:, :], B.s_st)
        P.wait("gpsimd", B.s_st, B.s_st.n)
        P.op("gpsimd", lambda e: e.collective_compute(
            "AllReduce", ALU.add, replica_groups=[list(range(NCORES))],
            ins=[kv_cc_in], outs=[kv_cc_out]), B.s_cc)
        P.wait("sync", B.s_cc, B.s_cc.n)
        P.dma("sync", B.kvacc[:, :], kv_cc_out, B.s_x)
        P.wait("vector", B.s_x, B.s_x.n)
        P.op("vector", lambda e: e.tensor_scalar(
            B.kvb[:, :], B.kvacc[:, :], float(128.0 ** -0.5),
            None, ALU.mult), B.s_dve)
        kvb_done = B.s_dve.n

        P.wait("tensor", B.s_dve, kvb_done)
        for h in range(16):
            for th in range(2):
                bk = (h * 2 + th) % 2
                pm = bank(B, 4 + bk)
                P.wait("tensor", B.s_act, B.g.get(("attn", bk), 0))
                lhsT = B.kvb[:, h * 128:(h + 1) * 128]
                rhs = B.qb[:, h * TC + th * 512: h * TC + (th + 1) * 512]
                P.op("tensor", lambda e, p=pm, l=lhsT, r=rhs:
                     e.matmul(p, l, r, start=True, stop=True), B.s_mm)
                P.wait("scalar", B.s_mm, B.s_mm.n)
                dst = B.qb[:, h * TC + th * 512: h * TC + (th + 1) * 512]
                P.op("scalar", lambda e, d=dst, p=pm:
                     e.activation(d, p, AF.Identity), B.s_act)
                B.g[("attn", bk)] = B.s_act.n
        attn_all = B.s_act.n

        P.wait("gpsimd", B.s_act, attn_all)
        P.wait("tensor", B.s_act, attn_all)

        def y_put(o, th):
            return B.xn[:, o * TC + th * 512: o * TC + (th + 1) * 512]

        emit_proj(P, B, 8, lambda og, slot: slab_load(woT, og, slot),
                  lambda k, th: B.qb[:, k * TC + th * 512: k * TC + (th + 1) * 512],
                  y_put, KC, 256, bias_col=lambda o: B.boc[:, o:o + 1], og_cols=256)
        y_all = B.s_act.n

        P.wait("vector", B.s_act, y_all)
        P.wait("scalar", B.s_act, y_all)
        P.wait("tensor", B.s_act, y_all)
        s_out = [P.sem("s_outA0"), P.sem("s_outA1")]
        for th in range(2):
            emit_ln_stats(P, B, lambda k, th=th: y_put(k, th), th, KC,
                          inv_d, B.m_b[:, th * 512:(th + 1) * 512],
                          B.r_b[:, th * 512:(th + 1) * 512], eps_ln,
                          cast_first=False)
            for k in range(KC):
                m_ap = B.m_b[:, th * 512:(th + 1) * 512]
                r_ap = B.r_b[:, th * 512:(th + 1) * 512]
                osl = k % 2
                o_ap = B.oscr[:, osl * 512:(osl + 1) * 512]
                ap, slot = xst.get(k, th)
                P.op("vector", lambda e, y=y_put(k, th), m=m_ap:
                     e.tensor_sub(B.t1[:, :], y, m))
                P.op("vector", lambda e, r=r_ap: e.tensor_mul(B.t1[:, :], B.t1[:, :], r))
                P.wait("vector", s_out[osl], B.g.get(("oscrA", osl), 0))
                t = P.op("vector", lambda e, o=o_ap, x=ap:
                         e.tensor_add(o, x, B.t1[:, :]), B.s_dve)
                xst.mark(slot, t)
                P.wait("sync", B.s_dve, t)
                B.g[("oscrA", osl)] = P.dma(
                    "sync", xout[k * 128:(k + 1) * 128, th * 512:(th + 1) * 512],
                    o_ap, s_out[osl])
        P.wait("sync", s_out[0], s_out[0].n)
        P.wait("sync", s_out[1], s_out[1].n)
    es2.close()



INTER = 8192
IG = 8
ICG = INTER // 128 // IG


class XStream:

    COUNT = [0]

    def __init__(self, P, B, xin, nslots=4):
        self.P, self.B, self.xin = P, B, xin
        self.nslots = nslots
        self.i = 0
        uid = XStream.COUNT[0]
        XStream.COUNT[0] += 1
        self.sems = [P.sem(f"s_xs{uid}_{j}") for j in range(nslots)]

    def get(self, k, th, waiters=("vector",)):
        P, B = self.P, self.B
        slot = self.i % self.nslots
        self.i += 1
        ap = B.xs[:, slot * 512:(slot + 1) * 512]
        P.wait("sync", B.s_dve, B.g.get(("xs", slot), 0))
        src = self.xin[k * 128:(k + 1) * 128, th * 512:(th + 1) * 512]
        tgt = P.dma("sync", ap, src, self.sems[slot])
        for w in waiters:
            P.wait(w, self.sems[slot], tgt)
        return ap, slot

    def mark(self, slot, tgt):
        self.B.g[("xs", slot)] = tgt




class Spline:

    def __init__(self, P, B, nc, es, spl, aggu_in, agd_in):
        self.P, self.B = P, B
        self.aggu_in, self.agd_in = aggu_in, agd_in
        sb = lambda name, shape, dt: es.enter_context(
            nc.sbuf_tensor("SP_" + name, shape, dt))
        self.spst = sb("spst", [128, 2 * 8 * TC], FP16)
        self.sp16 = sb("sp16", [128, 2 * TC], FP16)
        self.wout = sb("wout", [128, 2 * TC], BF16)
        self.s_sp = [P.sem("s_spl0"), P.sem("s_spl1")]
        self.s_agw = [P.sem("s_agwl0"), P.sem("s_agwl1")]
        self.chunks = []
        for hc in range(KC):
            self.chunks.append((
                [spl[f"p_g{k}"][hc * 128:(hc + 1) * 128, :] for k in range(4)]
                + [spl[f"c_gu{k}"][hc * 128:(hc + 1) * 128, :] for k in range(4)],
                aggu_in[hc * 128:(hc + 1) * 128, :]))
        for hc in range(KC):
            self.chunks.append((
                [spl[f"p_u{k}"][hc * 128:(hc + 1) * 128, :] for k in range(4)]
                + [spl[f"c_gu{k}"][hc * 128:(hc + 1) * 128, :] for k in range(4)],
                aggu_in[D + hc * 128: D + (hc + 1) * 128, :]))
        for ic in range(TC // 128):
            for h in range(2):
                cs_ = slice(h * 1024, (h + 1) * 1024)
                self.chunks.append((
                    [spl[f"p_d{k}"][ic * 128:(ic + 1) * 128, cs_] for k in range(4)]
                    + [spl[f"c_cd{k}"][ic * 128:(ic + 1) * 128, cs_] for k in range(4)],
                    agd_in[ic * 128:(ic + 1) * 128, cs_]))
        self.n = 0
        self.s_cc1 = None
        self.s_cc2 = None

    def hook(self, k=1):
        P, B = self.P, self.B
        for _ in range(k):
            if self.n >= len(self.chunks):
                return
            if self.n == 2 * KC:
                P.wait("gpsimd", self.s_agw[0], self.s_agw[0].n)
                P.wait("gpsimd", self.s_agw[1], self.s_agw[1].n)
                P.op("gpsimd", lambda e: e.collective_compute(
                    "AllGather", ALU.bypass, replica_groups=[list(range(NCORES))],
                    ins=[self.aggu_in], outs=[self.aggu_out_ap]), self.s_cc1)
            srcs, dst = self.chunks[self.n]
            sl = self.n % 2
            self.n += 1
            for j, sap in enumerate(srcs):
                P.dma("gpsimd", self.spst[:, sl * 8 * TC + j * TC:
                                          sl * 8 * TC + (j + 1) * TC],
                      sap, self.s_sp[sl])
            P.wait("gpsimd", self.s_sp[sl], self.s_sp[sl].n)
            pv = lambda i, sl=sl: self.spst[:, sl * 8 * TC + i * TC:
                                            sl * 8 * TC + (i + 1) * TC]
            ta = self.sp16[:, 0:TC]
            tb = self.sp16[:, TC:2 * TC]
            wo = self.wout[:, sl * TC:(sl + 1) * TC]
            P.op("gpsimd", lambda e, t=ta, x=pv(0), y=pv(4): e.tensor_mul(t, x, y))
            P.op("gpsimd", lambda e, t=tb, x=pv(1), y=pv(5): e.tensor_mul(t, x, y))
            P.op("gpsimd", lambda e, a=ta, b=tb: e.tensor_add(a, a, b))
            P.op("gpsimd", lambda e, t=tb, x=pv(2), y=pv(6): e.tensor_mul(t, x, y))
            P.op("gpsimd", lambda e, a=ta, b=tb: e.tensor_add(a, a, b))
            P.op("gpsimd", lambda e, t=tb, x=pv(3), y=pv(7): e.tensor_mul(t, x, y))
            P.wait("gpsimd", self.s_agw[sl], self.s_agw[sl].n)
            P.op("gpsimd", lambda e, w=wo, a=ta, b=tb: e.tensor_add(w, a, b))
            P.dma("gpsimd", dst, wo, self.s_agw[sl])

    def finish(self):
        P, B = self.P, self.B
        self.hook(len(self.chunks))
        P.wait("gpsimd", self.s_agw[0], self.s_agw[0].n)
        P.wait("gpsimd", self.s_agw[1], self.s_agw[1].n)
        P.op("gpsimd", lambda e: e.collective_compute(
            "AllGather", ALU.bypass, replica_groups=[list(range(NCORES))],
            ins=[self.agd_in], outs=[self.agd_out_ap]), self.s_cc2)


def emit_l0_ffn(P, B, nc, xin, cv_in, spl, xout, aggu_in, aggu_out,
                agd_in, agd_out, cc_waits=None, down_wait=None):
    es2 = contextlib.ExitStack()
    if True:
        sb = lambda name, shape, dt: es2.enter_context(
            nc.sbuf_tensor("B_" + name, shape, dt))
        init_done = B.init_done

        B.xn = sb("xn", [128, KC * TC], BF16)
        B.inter = sb("inter", [128, ICG * TC], BF16)
        B.acc = sb("acc", [128, KC * TC], F32)
        B.wslab = sb("wslab", [128, 2 * 4096], BF16)
        B.slab_stride = 4096
        B.xs = sb("xs", [128, 4 * 512], F32)
        B.dslab = sb("dslab", [128, 2 * 1024], BF16)
        s_wd = [P.sem("s_wd0"), P.sem("s_wd1")]

        eps_ln = B.cv[0:1, CV_EPS_LN:CV_EPS_LN + 1]
        inv_d = 1.0 / 2048.0

        xst = XStream(P, B, xin)

        def stats_src(k, th):
            ap, slot = xst.get(k, th)
            B.g["_cur_xs_slot"] = slot
            return ap

        def xn_put(k, th):
            return B.xn[:, k * TC + th * 512: k * TC + (th + 1) * 512]

        xn_done = []
        for th in range(2):
            emit_ln_stats(P, B, lambda k, th=th: stats_src(k, th), th, KC,
                          inv_d, B.m_b[:, th * 512:(th + 1) * 512],
                          B.r_b[:, th * 512:(th + 1) * 512], eps_ln,
                          cast_first=True)
            for k in range(KC):
                ap, slot = xst.get(k, th)
                m_ap = B.m_b[:, th * 512:(th + 1) * 512]
                r_ap = B.r_b[:, th * 512:(th + 1) * 512]
                P.op("vector", lambda e, x=ap, m=m_ap: e.tensor_sub(B.t1[:, :], x, m))
                t = P.op("vector", lambda e, o=xn_put(k, th), r=r_ap:
                         e.tensor_mul(o, B.t1[:, :], r), B.s_dve)
                xst.mark(slot, t)
            xn_done.append(B.s_dve.n)

        for sem, tgt in cc_waits:
            P.wait("sync", sem, tgt)

        P.wait("tensor", B.s_dve, xn_done[1])

        def gu_slab_load(g, ic, slot):
            i0 = ic * 128
            gsrc = aggu_out[g * 2 * D: g * 2 * D + D, i0:i0 + 128].rearrange(
                "(k p) j -> p k j", p=128)
            usrc = aggu_out[g * 2 * D + D: (g + 1) * 2 * D, i0:i0 + 128].rearrange(
                "(k p) j -> p k j", p=128)
            P.dma("sync", sb3(B.wslab[:, slot * 4096: slot * 4096 + 2048], KC),
                  gsrc, B.s_w[slot])
            return P.dma("sync", sb3(B.wslab[:, slot * 4096 + 2048:
                                             slot * 4096 + 4096], KC), usrc, B.s_w[slot])

        inter_free = 0
        for g in range(IG):
            for ic in range(ICG):
                slot = ic % 2
                P.wait("sync", B.s_mm, B.g.get(("gu_slab", slot), 0))
                wt = gu_slab_load(g, ic, slot)
                P.wait("tensor", B.s_w[slot], wt)
                if ic == 0:
                    P.wait("tensor", B.s_mm, inter_free)
                for th in range(2):
                    gbk = (ic * 2 + th) % 2
                    pg = bank(B, 4 + gbk)
                    pu = bank(B, 2 + gbk)
                    P.wait("tensor", B.s_act, B.g.get(("gact", gbk), 0))
                    P.wait("tensor", B.s_dve, B.g.get(("uact", gbk), 0))
                    for k in range(KC):
                        lg = B.wslab[:, slot * 4096 + k * 128: slot * 4096 + (k + 1) * 128]
                        P.op("tensor", lambda e, p=pg, l=lg, r=xn_put(k, th),
                             st=(k == 0), sp=(k == KC - 1):
                             e.matmul(p, l, r, start=st, stop=sp),
                             B.s_mm if k == KC - 1 else None)
                    gmm = B.s_mm.n
                    for k in range(KC):
                        lu = B.wslab[:, slot * 4096 + 2048 + k * 128:
                                     slot * 4096 + 2048 + (k + 1) * 128]
                        P.op("tensor", lambda e, p=pu, l=lu, r=xn_put(k, th),
                             st=(k == 0), sp=(k == KC - 1):
                             e.matmul(p, l, r, start=st, stop=sp),
                             B.s_mm if k == KC - 1 else None)
                    umm = B.s_mm.n
                    sg = B.scr_a[:, th * 512:(th + 1) * 512]
                    P.wait("scalar", B.s_mm, gmm)
                    P.wait("scalar", B.s_dve, B.g.get(("sg", th), 0))
                    P.op("scalar", lambda e, d=sg, p=pg:
                         e.activation(d, p, AF.Silu), B.s_act)
                    B.g[("gact", gbk)] = B.s_act.n
                    it = B.inter[:, ic * TC + th * 512: ic * TC + (th + 1) * 512]
                    P.wait("vector", B.s_mm, umm)
                    P.wait("vector", B.s_act, B.g[("gact", gbk)])
                    P.op("vector", lambda e, d=it, s=sg, p=pu:
                         e.tensor_mul(d, s, p), B.s_dve)
                    B.g[("uact", gbk)] = B.s_dve.n
                    B.g[("sg", th)] = B.s_dve.n
                B.g[("gu_slab", slot)] = B.s_mm.n
            inter_done = B.s_dve.n
            P.wait("tensor", B.s_dve, inter_done)
            if g == 0 and down_wait is not None:
                P.wait("sync", down_wait[0], down_wait[1])
            for o in range(KC):
                dslot = o % 2
                P.wait("sync", B.s_mm, B.g.get(("d_slab", dslot), 0))
                dsrc = agd_out[g * 1024: (g + 1) * 1024, o * 128:(o + 1) * 128]
                dt_ = P.dma("sync",
                            sb3(B.dslab[:, dslot * 1024:(dslot + 1) * 1024], ICG),
                            dsrc.rearrange("(k p) j -> p k j", p=128), s_wd[dslot])
                P.wait("tensor", s_wd[dslot], dt_)
                for th in range(2):
                    dbk = (o * 2 + th) % 2
                    pd = bank(B, 6 + dbk)
                    P.wait("tensor", B.s_dve, B.g.get(("dacc", dbk), 0))
                    for ic in range(ICG):
                        ld = B.dslab[:, dslot * 1024 + ic * 128:
                                     dslot * 1024 + (ic + 1) * 128]
                        rh = B.inter[:, ic * TC + th * 512: ic * TC + (th + 1) * 512]
                        P.op("tensor", lambda e, p=pd, l=ld, r=rh, st=(ic == 0),
                             sp=(ic == ICG - 1): e.matmul(p, l, r, start=st, stop=sp),
                             B.s_mm if ic == ICG - 1 else None)
                    dmm = B.s_mm.n
                    a_ap = B.acc[:, o * TC + th * 512: o * TC + (th + 1) * 512]
                    P.wait("vector", B.s_mm, dmm)
                    if g == 0:
                        P.op("vector", lambda e, a=a_ap, p=pd: e.tensor_copy(a, p),
                             B.s_dve)
                    else:
                        P.op("vector", lambda e, a=a_ap, p=pd: e.tensor_add(a, a, p),
                             B.s_dve)
                    B.g[("dacc", dbk)] = B.s_dve.n
                B.g[("d_slab", dslot)] = B.s_mm.n
            inter_free = B.s_mm.n

        acc_done = B.s_dve.n
        P.wait("vector", B.s_dve, acc_done)
        P.wait("scalar", B.s_dve, acc_done)
        P.wait("tensor", B.s_dve, acc_done)

        def acc_get(k, th):
            return B.acc[:, k * TC + th * 512: k * TC + (th + 1) * 512]

        res_done = 0
        for th in range(2):
            emit_ln_stats(P, B, lambda k, th=th: acc_get(k, th), th, KC,
                          inv_d, B.m_b[:, th * 512:(th + 1) * 512],
                          B.r_b[:, th * 512:(th + 1) * 512], eps_ln,
                          cast_first=True)
            for k in range(KC):
                a_ap = acc_get(k, th)
                m_ap = B.m_b[:, th * 512:(th + 1) * 512]
                r_ap = B.r_b[:, th * 512:(th + 1) * 512]
                ap, slot = xst.get(k, th)
                P.op("vector", lambda e, a=a_ap, m=m_ap: e.tensor_sub(B.t1[:, :], a, m))
                P.op("vector", lambda e, r=r_ap: e.tensor_mul(B.t1[:, :], B.t1[:, :], r))
                t = P.op("vector", lambda e, a=a_ap, x=ap:
                         e.tensor_add(a, x, B.t1[:, :]), B.s_dve)
                xst.mark(slot, t)
            res_done = B.s_dve.n

        P.wait("sync", B.s_dve, res_done)
        P.dma("sync", chunked(xout), sb3(B.acc[:, :], KC), B.s_st)
        P.wait("sync", B.s_st, B.s_st.n)
    es2.close()



NH = 16
QR = 768
KVR = 512
ROPE = 64
VD = 128


def emit_l1_attn(P, B, nc, xin, wq_a, qnw, wq_b, wkv_a, kvnw, wkvb_n, wkvb_v,
                 wo, cos_in, sin_in, cv_in, cv128_in, xout,
                 agk_in, agk_out, agv_in, agv_out):
    es2 = contextlib.ExitStack()
    if True:
        sb = lambda name, shape, dt: es2.enter_context(
            nc.sbuf_tensor("C_" + name, shape, dt))
        init_done = B.init_done

        B.xn = sb("xn", [128, KC * TC], BF16)
        B.qs = sb("qs", [128, KC * TC], BF16)
        B.qa = sb("qa", [128, 6 * TC], BF16)
        B.ckv = sb("ckv", [128, 4 * TC], BF16)
        B.kpe_b = sb("kpe_b", [128, TC], BF16)
        B.knope = sb("knope", [128, 8 * TC], BF16)
        B.vt = sb("vt", [128, 2 * 512], BF16)
        B.cosb = sb("cosb", [32, TC], BF16)
        B.sinb = sb("sinb", [32, TC], BF16)
        B.rp = sb("rp", [32, 3 * TC], BF16)
        B.xw = sb("xw", [32, 2 * TC], BF16)
        B.wslab = sb("wslab", [128, 2 * 4096], BF16)
        B.slab_stride = 4096
        B.kvsl = sb("kvsl", [128, 2 * 4096], BF16)
        B.psb = sb("psb", [128, 2 * 2048], BF16)
        B.xs = sb("xs", [128, 4 * 512], F32)
        B.oscr = sb("oscr", [128, 2 * 512], F32)
        B.qnw_sb = sb("qnw_sb", [128, QR // 128], F32)
        B.kvnw_sb = sb("kvnw_sb", [128, KVR // 128], F32)
        B.cv128 = sb("cv128_sb", [128, NCONST], F32)
        B.den_row = sb("den_row", [1, 1024], F32)
        B.rec_bf = sb("rec_bf", [1, 1024], BF16)

        eps_ln = B.cv[0:1, CV_EPS_LN:CV_EPS_LN + 1]
        eps_rms = B.cv[0:1, CV_EPS_RMS:CV_EPS_RMS + 1]
        inv_d = 1.0 / 2048.0

        P.dma("sync", B.cv[:, :], cv_in, B.s_x)
        P.dma("sync", B.cv128[:, :], cv128_in, B.s_x)
        P.dma("sync", B.qnw_sb[:, :], qnw, B.s_x)
        P.dma("sync", B.kvnw_sb[:, :], kvnw, B.s_x)
        P.dma("gpsimd", B.cosb[:, :], cos_in, B.s_x)
        P.dma("gpsimd", B.sinb[:, :], sin_in, B.s_x)
        consts_tgt = B.s_x.n
        P.wait("scalar", B.s_x, consts_tgt)
        P.wait("vector", B.s_x, consts_tgt)

        xst = XStream(P, B, xin)

        def stats_src(k, th):
            ap, slot = xst.get(k, th)
            B.g["_cur_xs_slot"] = slot
            return ap

        def xn_put(k, th):
            return B.xn[:, k * TC + th * 512: k * TC + (th + 1) * 512]

        xn_done = []
        for th in range(2):
            emit_ln_stats(P, B, lambda k, th=th: stats_src(k, th), th, KC,
                          inv_d, B.m_b[:, th * 512:(th + 1) * 512],
                          B.r_b[:, th * 512:(th + 1) * 512], eps_ln,
                          cast_first=True)
            for k in range(KC):
                ap, slot = xst.get(k, th)
                m_ap = B.m_b[:, th * 512:(th + 1) * 512]
                r_ap = B.r_b[:, th * 512:(th + 1) * 512]
                P.op("vector", lambda e, x=ap, m=m_ap: e.tensor_sub(B.t1[:, :], x, m))
                t = P.op("vector", lambda e, o=xn_put(k, th), r=r_ap:
                         e.tensor_mul(o, B.t1[:, :], r), B.s_dve)
                xst.mark(slot, t)
            xn_done.append(B.s_dve.n)
        P.wait("tensor", B.s_dve, xn_done[1])

        def slab_load(w, og, slot, cols=256, n_k=KC):
            src = w[:, og * cols:(og + 1) * cols].rearrange("(k p) j -> p k j", p=128)
            return P.dma("gpsimd",
                         sb3(B.wslab[:, slot * 4096: slot * 4096 + n_k * cols], n_k),
                         src, B.s_w[slot])

        emit_proj(P, B, 2, lambda og, slot: slab_load(wkv_a, og, slot),
                  xn_put, lambda o, th: B.ckv[:, o * TC + th * 512: o * TC + (th + 1) * 512],
                  KC, 256, og_cols=256)
        P.wait("gpsimd", B.s_mm, B.s_mm.n)
        kpe_slab = P.dma("gpsimd",
                         sb3(B.wslab[:, 0:KC * 64], KC),
                         wkv_a[:, KVR:KVR + ROPE].rearrange("(k p) j -> p k j", p=128),
                         B.s_w[0])
        P.wait("tensor", B.s_w[0], kpe_slab)
        P.wait("tensor", B.s_act, B.s_act.n)
        for th in range(2):
            pm = bank(B, 4 + th, parts=64)
            for k in range(KC):
                lhsT = B.wslab[:, k * 64:(k + 1) * 64]
                P.op("tensor", lambda e, p=pm, l=lhsT, r=xn_put(k, th),
                     st=(k == 0), sp=(k == KC - 1):
                     e.matmul(p, l, r, start=st, stop=sp),
                     B.s_mm if k == KC - 1 else None)
            P.wait("scalar", B.s_mm, B.s_mm.n)
            P.op("scalar", lambda e, th=th, p=pm: e.activation(
                B.kpe_b[0:64, th * 512:(th + 1) * 512], p, AF.Identity), B.s_act)
        kpe_done = B.s_act.n
        B.g[("proj_evac", 0)] = kpe_done
        B.g[("proj_evac", 1)] = kpe_done

        c_get = lambda k, th: B.ckv[:, k * TC + th * 512: k * TC + (th + 1) * 512]
        P.wait("vector", B.s_act, B.s_act.n)
        for th in range(2):
            emit_ln_stats(P, B, lambda k, th=th: c_get(k, th), th, 4,
                          1.0 / KVR, None, B.r_b[:, th * 512:(th + 1) * 512],
                          eps_rms, cast_first=False)
            for k in range(4):
                r_ap = B.r_b[:, th * 512:(th + 1) * 512]
                P.op("vector", lambda e, q=c_get(k, th), r=r_ap:
                     e.tensor_mul(q, q, r))
                P.op("vector", lambda e, q=c_get(k, th), k_=k: e.tensor_scalar(
                    q, q, B.kvnw_sb[:, k_:k_ + 1], None, ALU.mult), B.s_dve)
        cn_done = B.s_dve.n
        P.wait("tensor", B.s_dve, cn_done)

        s_rp = [P.sem("s_rp0"), P.sem("s_rp1")]

        def emit_rope(x1, x2_src, o1, o2_dst, slot):
            xi = B.xw[0:32, slot * TC: (slot + 1) * TC]
            xo = xi
            P.wait("sync", s_rp[slot], B.g.get(("xw_dma", slot), 0))
            P.wait("sync", B.s_dve, B.g.get(("xw", slot), 0))
            it = P.dma("sync", xi, x2_src, s_rp[slot])
            a = B.rp[0:32, 0:TC]
            b = B.rp[0:32, TC:2 * TC]
            c = B.rp[0:32, 2 * TC:3 * TC]
            cos = B.cosb[0:32, :]
            sin = B.sinb[0:32, :]
            P.wait("vector", s_rp[slot], it)
            P.op("vector", lambda e: e.tensor_mul(a, x1, cos))
            P.op("vector", lambda e: e.tensor_mul(b, xi, sin))
            P.op("vector", lambda e: e.tensor_mul(c, x1, sin))
            P.op("vector", lambda e: e.tensor_mul(xi, xi, cos))
            P.op("vector", lambda e: e.tensor_sub(o1, a, b))
            t = P.op("vector", lambda e: e.tensor_add(xo, c, xi), B.s_dve)
            B.g[("xw", slot)] = t
            P.wait("sync", B.s_dve, t)
            B.g[("xw_dma", slot)] = P.dma("sync", o2_dst, xo, s_rp[slot])
            return B.g[("xw_dma", slot)]

        P.wait("vector", B.s_act, kpe_done)
        P.wait("sync", B.s_act, kpe_done)
        emit_rope(B.kpe_b[0:32, :], B.kpe_b[32:64, :],
                  B.kpe_b[0:32, :], B.kpe_b[32:64, :], 0)
        kpe_rope_done = B.s_dve.n
        kpe_rope_dma = [s_rp[0].n, s_rp[1].n]

        emit_proj(P, B, 4, lambda og, slot: slab_load(wkvb_n, og, slot, n_k=4),
                  c_get, lambda o, th: B.knope[:, o * TC + th * 512: o * TC + (th + 1) * 512],
                  4, 256, og_cols=256)
        knope_done = B.s_act.n

        s_cck = B.s_cc
        s_ccv = P.sem("s_ccv")
        P.wait("sync", B.s_dve, kpe_rope_done)
        P.wait("sync", s_rp[0], kpe_rope_dma[0])
        P.wait("sync", s_rp[1], kpe_rope_dma[1])
        for h in range(NH):
            P.dma("sync", agk_in[h * 128: h * 128 + 64, :], B.kpe_b[0:64, :], B.s_st)
        P.wait("sync", B.s_act, knope_done)
        for j in range(8):
            P.dma("sync", agk_in[j * 256 + 64: j * 256 + 128, :],
                  B.knope[0:64, j * TC:(j + 1) * TC], B.s_st)
            P.dma("sync", agk_in[j * 256 + 192: j * 256 + 256, :],
                  B.knope[64:128, j * TC:(j + 1) * TC], B.s_st)
        kstage_done = B.s_st.n
        P.wait("gpsimd", B.s_st, kstage_done)
        P.op("gpsimd", lambda e: e.collective_compute(
            "AllGather", ALU.bypass, replica_groups=[list(range(NCORES))],
            ins=[agk_in], outs=[agk_out]), s_cck)

        s_vt = [P.sem("s_vt0"), P.sem("s_vt1")]
        P.wait("tensor", B.s_act, knope_done)
        P.wait("gpsimd", B.s_mm, B.s_mm.n)
        for vg in range(4):
            slot = vg % 2
            P.wait("gpsimd", B.s_mm, B.g.get(("vslab", slot), 0))
            wt = P.dma("gpsimd",
                       sb3(B.wslab[:, slot * 4096: slot * 4096 + 4 * 512], 4),
                       wkvb_v[:, vg * 512:(vg + 1) * 512].rearrange(
                           "(k p) j -> p k j", p=128), B.s_w[slot])
            P.wait("tensor", B.s_w[slot], wt)
            for s in range(8):
                vbk = s % 2
                pm = bank(B, 4 + vbk)
                P.wait("tensor", B.s_act, B.g.get(("vprod", vbk), 0))
                for k in range(4):
                    lhsT = B.ckv[:, k * TC + s * 128: k * TC + s * 128 + 128]
                    rhs = B.wslab[:, slot * 4096 + k * 512:(slot * 4096) + (k + 1) * 512]
                    P.op("tensor", lambda e, p=pm, l=lhsT, r=rhs, st=(k == 0),
                         sp=(k == 3): e.matmul(p, l, r, start=st, stop=sp),
                         B.s_mm if k == 3 else None)
                mm_tgt = B.s_mm.n
                vsl = vbk
                P.wait("scalar", B.s_mm, mm_tgt)
                P.wait("scalar", s_vt[vsl], B.g.get(("vt", vsl), 0))
                dst = B.vt[:, vsl * 512:(vsl + 1) * 512]
                P.op("scalar", lambda e, d=dst, p=pm:
                     e.activation(d, p, AF.Identity), B.s_act)
                B.g[("vprod", vbk)] = B.s_act.n
                P.wait("sync", B.s_act, B.s_act.n)
                B.g[("vt", vsl)] = P.dma(
                    "sync", agv_in[s * 128:(s + 1) * 128, vg * 512:(vg + 1) * 512],
                    dst, s_vt[vsl])
            B.g[("vslab", slot)] = B.s_mm.n

        for sl in range(2):
            P.wait("sync", s_vt[sl], s_vt[sl].n)
        P.op("sync", lambda e: e.nop(), B.s_st, inc=1)
        vstage_done = B.s_st.n
        P.wait("gpsimd", B.s_st, vstage_done)
        P.op("gpsimd", lambda e: e.collective_compute(
            "AllGather", ALU.bypass, replica_groups=[list(range(NCORES))],
            ins=[agv_in], outs=[agv_out]), s_ccv)
        P.wait("sync", s_cck, s_cck.n)
        P.wait("sync", s_ccv, s_ccv.n)

        emit_proj(P, B, 3, lambda og, slot: slab_load(wq_a, og, slot),
                  xn_put, lambda o, th: B.qa[:, o * TC + th * 512: o * TC + (th + 1) * 512],
                  KC, 256, og_cols=256)

        qa_get = lambda k, th: B.qa[:, k * TC + th * 512: k * TC + (th + 1) * 512]
        for th in range(2):
            emit_ln_stats(P, B, lambda k, th=th: qa_get(k, th), th, 6,
                          1.0 / QR, None, B.r_b[:, th * 512:(th + 1) * 512],
                          eps_rms, cast_first=False)
            for k in range(6):
                r_ap = B.r_b[:, th * 512:(th + 1) * 512]
                P.op("vector", lambda e, q=qa_get(k, th), r=r_ap:
                     e.tensor_mul(q, q, r))
                P.op("vector", lambda e, q=qa_get(k, th), k_=k: e.tensor_scalar(
                    q, q, B.qnw_sb[:, k_:k_ + 1], None, ALU.mult), B.s_dve)
        qan_done = B.s_dve.n
        P.wait("tensor", B.s_dve, qan_done)

        emit_proj(P, B, 8, lambda og, slot: slab_load(wq_b, og, slot, n_k=6),
                  qa_get, lambda o, th: B.qs[:, o * TC + th * 512: o * TC + (th + 1) * 512],
                  6, 256, og_cols=256)
        q_evac_done = B.s_act.n

        P.wait("vector", B.s_act, q_evac_done)
        P.wait("sync", B.s_act, q_evac_done)
        for h in range(NH):
            base = h * TC
            emit_rope(B.qs[0:32, base:base + TC], B.qs[32:64, base:base + TC],
                      B.qs[0:32, base:base + TC], B.qs[32:64, base:base + TC],
                      h % 2)
        q_rope_dma = [s_rp[0].n, s_rp[1].n]

        s_kv = [P.sem("s_kv0"), P.sem("s_kv1")]
        P.wait("tensor", B.s_dve, B.s_dve.n)
        P.wait("tensor", B.s_act, B.s_act.n)
        P.wait("tensor", s_rp[0], q_rope_dma[0])
        P.wait("tensor", s_rp[1], q_rope_dma[1])
        attnscale = B.cv128[:, CV_ATTNSCALE:CV_ATTNSCALE + 1]
        def emit_pvden(pend):
            pdst, slot_, h01_, st_, sp_, kcg_, psl_, exp_tgt = pend
            if st_:
                P.wait("tensor", B.s_dve, B.g.get(("den", h01_), 0))
                P.wait("tensor", B.s_dve, B.g.get(("pv", h01_), 0))
            P.wait("tensor", B.s_act, exp_tgt)
            for kc4 in range(4):
                kc = kcg_ * 4 + kc4
                sp = sp_ and kc4 == 3
                vtile = B.kvsl[:, slot_ * 4096 + 2048 + kc * 256 +
                               h01_ * 128: slot_ * 4096 + 2048 +
                               kc * 256 + h01_ * 128 + 128]
                prh = pdst[:, kc4 * 512:(kc4 + 1) * 512]
                P.op("tensor", lambda e, p=bank(B, 6 + h01_), l=vtile,
                     r=prh, st=(st_ and kc4 == 0), sp=sp:
                     e.matmul(p, l, r, start=st, stop=sp),
                     B.s_mm if sp else None)
                P.op("tensor", lambda e, p=bank(B, h01_, parts=1),
                     r=prh, st=(st_ and kc4 == 0), sp=sp:
                     e.matmul(p, B.ones_col[:, :], r, start=st, stop=sp),
                     B.s_mm if (sp or kc4 == 3) else None)
            B.g[("psb", psl_)] = B.s_mm.n
            B.g[("kvsl", slot_)] = max(B.g.get(("kvsl", slot_), 0), B.s_mm.n)

        for qh in range(2):
            for hp in range(8):
                pending = None
                uidx = 0
                for r in range(NCORES):
                    slot = (hp * 8 + r) % 2
                    P.wait("sync", B.s_mm, B.g.get(("kvsl", slot), 0))
                    for h01 in range(2):
                        h = hp * 2 + h01
                        P.dma("sync", B.kvsl[:, slot * 4096 + h01 * 1024:
                                             slot * 4096 + (h01 + 1) * 1024],
                              agk_out[r * 2048 + h * 128: r * 2048 + (h + 1) * 128, :],
                              s_kv[slot])
                    kvt = P.dma("sync",
                                sb3(B.kvsl[:, slot * 4096 + 2048: (slot + 1) * 4096], 8),
                                agv_out[r * 1024:(r + 1) * 1024,
                                        hp * 256:(hp + 1) * 256].rearrange(
                                            "(kc p) j -> p kc j", p=128),
                                s_kv[slot])
                    P.wait("tensor", s_kv[slot], kvt)
                    first = (r == 0)
                    last = (r == NCORES - 1)
                    for h01 in range(2):
                        h = hp * 2 + h01
                        for kcg in range(2):
                            psl = uidx % 2
                            uidx += 1
                            P.wait("tensor", B.s_act, B.g.get("exp_done", 0))
                            for kc4 in range(4):
                                kc = kcg * 4 + kc4
                                lhsT = B.kvsl[:, slot * 4096 + h01 * 1024 + kc * 128:
                                              slot * 4096 + h01 * 1024 + (kc + 1) * 128]
                                rhs = B.qs[:, h * TC + qh * 512: h * TC + (qh + 1) * 512]
                                P.op("tensor", lambda e, p=bank(B, 2 + kc4), l=lhsT, r=rhs:
                                     e.matmul(p, l, r, start=True, stop=True),
                                     B.s_mm if kc4 == 3 else None)
                            sc_tgt = B.s_mm.n
                            P.wait("scalar", B.s_mm, sc_tgt)
                            P.wait("scalar", B.s_mm, B.g.get(("psb", psl), 0))
                            pdst = B.psb[:, psl * 2048:(psl + 1) * 2048]
                            P.op("scalar", lambda e, d=pdst: e.activation(
                                d, B.pool[:, 1024:3072], AF.Exp, scale=attnscale),
                                B.s_act)
                            B.g["exp_done"] = B.s_act.n
                            if pending is not None:
                                emit_pvden(pending)
                            pending = (pdst, slot, h01, first and kcg == 0,
                                       last and kcg == 1, kcg, psl, B.s_act.n)
                if pending is not None:
                    emit_pvden(pending)
                    pending = None
                pv_tgt = B.s_mm.n
                P.wait("vector", B.s_mm, pv_tgt)
                P.op("vector", lambda e: e.tensor_copy(B.den_row[0:1, 0:1024],
                                                       B.pool[0:1, 0:1024]), B.s_dve)
                den_copied = B.s_dve.n
                for h01 in range(2):
                    B.g[("den", h01)] = den_copied
                P.op("vector", lambda e: e.reciprocal(B.den_row[0:1, 0:1024],
                                                      B.den_row[0:1, 0:1024]))
                P.op("vector", lambda e: e.tensor_copy(B.rec_bf[0:1, 0:1024],
                                                       B.den_row[0:1, 0:1024]), B.s_dve)
                P.wait("tensor", B.s_dve, B.s_dve.n)
                P.wait("tensor", B.s_act, B.g.get("exp_done", 0))
                for h01 in range(2):
                    P.op("tensor", lambda e, h01=h01: e.matmul(
                        bank(B, 2 + h01), B.ones_row[0:1, 0:128],
                        B.rec_bf[0:1, h01 * 512:(h01 + 1) * 512],
                        start=True, stop=True), B.s_mm)
                rb_tgt = B.s_mm.n
                P.wait("vector", B.s_mm, rb_tgt)
                for h01 in range(2):
                    h = hp * 2 + h01
                    P.op("vector", lambda e, h01=h01: e.tensor_copy(
                        B.m_b[:, h01 * 512:(h01 + 1) * 512], bank(B, 2 + h01)), B.s_dve)
                    dst = B.xn[:, h * TC + qh * 512: h * TC + (qh + 1) * 512]
                    P.op("vector", lambda e, d=dst, h01=h01: e.tensor_mul(
                        d, bank(B, 6 + h01), B.m_b[:, h01 * 512:(h01 + 1) * 512]),
                        B.s_dve)
                    B.g[("pv", h01)] = B.s_dve.n
                B.g["exp_done"] = B.s_act.n
                B.g[("rb_read")] = B.s_dve.n
                P.wait("tensor", B.s_dve, B.g[("rb_read")])
        attn_done = B.s_dve.n

        P.wait("tensor", B.s_dve, attn_done)
        P.wait("gpsimd", B.s_dve, attn_done)
        emit_proj(P, B, 8, lambda og, slot: slab_load(wo, og, slot),
                  lambda k, th: B.xn[:, k * TC + th * 512: k * TC + (th + 1) * 512],
                  lambda o, th: B.qs[:, o * TC + th * 512: o * TC + (th + 1) * 512],
                  KC, 256, og_cols=256)
        y_all = B.s_act.n

        P.wait("vector", B.s_act, y_all)
        P.wait("scalar", B.s_act, y_all)
        P.wait("tensor", B.s_act, y_all)
        y_get = lambda k, th: B.qs[:, k * TC + th * 512: k * TC + (th + 1) * 512]
        s_out = [P.sem("s_out0"), P.sem("s_out1")]
        for th in range(2):
            emit_ln_stats(P, B, lambda k, th=th: y_get(k, th), th, KC,
                          inv_d, B.m_b[:, th * 512:(th + 1) * 512],
                          B.r_b[:, th * 512:(th + 1) * 512], eps_ln,
                          cast_first=False)
            for k in range(KC):
                m_ap = B.m_b[:, th * 512:(th + 1) * 512]
                r_ap = B.r_b[:, th * 512:(th + 1) * 512]
                osl = k % 2
                o_ap = B.oscr[:, osl * 512:(osl + 1) * 512]
                ap, slot = xst.get(k, th)
                P.op("vector", lambda e, y=y_get(k, th), m=m_ap:
                     e.tensor_sub(B.t1[:, :], y, m))
                P.op("vector", lambda e, r=r_ap: e.tensor_mul(B.t1[:, :], B.t1[:, :], r))
                P.wait("vector", s_out[osl], B.g.get(("oscr", osl), 0))
                t = P.op("vector", lambda e, o=o_ap, x=ap:
                         e.tensor_add(o, x, B.t1[:, :]), B.s_dve)
                xst.mark(slot, t)
                P.wait("sync", B.s_dve, t)
                B.g[("oscr", osl)] = P.dma(
                    "sync", xout[k * 128:(k + 1) * 128, th * 512:(th + 1) * 512],
                    o_ap, s_out[osl])
        P.wait("sync", s_out[0], s_out[0].n)
        P.wait("sync", s_out[1], s_out[1].n)
    es2.close()


def prep_l1_attn(inp):
    wq_b = inp["L1_attn_wq_b"]
    perm = np.zeros(D, dtype=np.int64)
    for h in range(NH):
        perm[h * 128: h * 128 + 64] = h * 128 + 64 + np.arange(64)
        perm[h * 128 + 64: h * 128 + 128] = h * 128 + np.arange(64)
    wq_b_p = np.ascontiguousarray(wq_b[:, perm])
    wkv_b = inp["L1_attn_wkv_b"]
    cols_n = np.concatenate([h * 192 + np.arange(64) for h in range(NH)])
    cols_v = np.concatenate([h * 192 + 64 + np.arange(128) for h in range(NH)])
    wkvb_n = np.ascontiguousarray(wkv_b[:, cols_n])
    wkvb_v = np.ascontiguousarray(wkv_b[:, cols_v])
    qnw = np.ascontiguousarray(inp["L1_attn_q_a_norm_w"].reshape(6, 128).T)
    kvnw = np.ascontiguousarray(inp["L1_attn_kv_a_norm_w"].reshape(4, 128).T)
    cv128 = np.repeat(CV_VALS, 128, axis=0)
    inv_freq = (1.0 / 10000.0 ** (np.arange(0, ROPE, 2, dtype=np.float64) / ROPE))
    maps = []
    for c in range(NCORES):
        tpos = np.arange(c * TC, (c + 1) * TC, dtype=np.float64)
        freqs = np.outer(inv_freq, tpos)
        maps.append({
            "wq_a": inp["L1_attn_wq_a"], "qnw": qnw, "wq_b": wq_b_p,
            "wkv_a": inp["L1_attn_wkv_a"], "kvnw": kvnw,
            "wkvb_n": wkvb_n, "wkvb_v": wkvb_v, "wo": inp["L1_attn_wo"],
            "cos_in": np.cos(freqs).astype(np.float32),
            "sin_in": np.sin(freqs).astype(np.float32),
            "cv": CV_VALS, "cv128": cv128,
        })
    return maps



HID = 5632
HC = HID // 128
DG = 4
HCG = HC // DG
D_TRUNC = 4


def emit_l1_ffn(P, B, nc, xin, wqgT, wquT, wqdT, sg_c, su_c, sd_c, cv_in, xout):
    es2 = contextlib.ExitStack()
    if True:
        sb = lambda name, shape, dt: es2.enter_context(
            nc.sbuf_tensor("D_" + name, shape, dt))
        init_done = B.init_done

        B.xn = sb("xn", [128, KC * TC], BF16)
        B.inter = sb("inter", [128, HCG * TC], BF16)
        B.acc = sb("acc", [128, KC * TC], F32)
        B.wslab = sb("wslab", [128, 2 * 4096], BF16)
        B.slab_stride = 4096
        B.dslab = sb("dslab", [128, 2 * HCG * 128], BF16)
        B.xs = sb("xs", [128, 4 * 512], F32)
        B.sgc = sb("sgc", [128, HC], F32)
        B.suc = sb("suc", [128, HC], F32)
        B.sdc = sb("sdc", [128, KC], F32)
        B.m128 = sb("m128", [1, 128], BF16)
        B.sx_bf = sb("sx_bf", [1, TC], BF16)
        B.si_bf = sb("si_bf", [1, TC], BF16)

        eps_ln = B.cv[0:1, CV_EPS_LN:CV_EPS_LN + 1]
        inv_d = 1.0 / 2048.0
        P.dma("sync", B.sgc[:, :], sg_c, B.s_x)
        P.dma("sync", B.suc[:, :], su_c, B.s_x)
        P.dma("sync", B.sdc[:, :], sd_c, B.s_x)
        P.wait("scalar", B.s_x, B.s_x.n)
        P.wait("vector", B.s_x, B.s_x.n)
        P.op("vector", lambda e: e.memset(B.m128[:, :], -128.0), B.s_dve)

        xst = XStream(P, B, xin)

        def stats_src(k, th):
            ap, slot = xst.get(k, th)
            B.g["_cur_xs_slot"] = slot
            return ap

        def xn_put(k, th):
            return B.xn[:, k * TC + th * 512: k * TC + (th + 1) * 512]

        xn_done = []
        for th in range(2):
            emit_ln_stats(P, B, lambda k, th=th: stats_src(k, th), th, KC,
                          inv_d, B.m_b[:, th * 512:(th + 1) * 512],
                          B.r_b[:, th * 512:(th + 1) * 512], eps_ln,
                          cast_first=True)
            for k in range(KC):
                ap, slot = xst.get(k, th)
                m_ap = B.m_b[:, th * 512:(th + 1) * 512]
                r_ap = B.r_b[:, th * 512:(th + 1) * 512]
                P.op("vector", lambda e, x=ap, m=m_ap: e.tensor_sub(B.t1[:, :], x, m))
                t = P.op("vector", lambda e, o=xn_put(k, th), r=r_ap:
                         e.tensor_mul(o, B.t1[:, :], r), B.s_dve)
                xst.mark(slot, t)
            xn_done.append(B.s_dve.n)
        P.wait("tensor", B.s_dve, xn_done[1])

        for th in range(2 if D_TRUNC >= 1 else 0):
            P.wait("tensor", B.s_dve, B.s_dve.n)
            for k in range(KC):
                P.op("tensor", lambda e, r=xn_put(k, th), st=(k == 0), sp=(k == KC - 1):
                     e.matmul(bank(B, 0, parts=1), B.ones_col[:, :], r,
                              start=st, stop=sp),
                     B.s_mm if k == KC - 1 else None)
            P.wait("vector", B.s_mm, B.s_mm.n)
            P.op("vector", lambda e, th=th: e.tensor_copy(
                B.sx_bf[0:1, th * 512:(th + 1) * 512], bank(B, 0, parts=1)), B.s_dve)
        sx_done = B.s_dve.n
        P.wait("tensor", B.s_dve, sx_done)
        if D_TRUNC < 1:
            sx_done = B.s_dve.n

        s_wd = [P.sem("s_wdq0"), P.sem("s_wdq1")]
        inter_free = 0
        for g in range(DG if D_TRUNC >= 4 else (1 if D_TRUNC >= 2 else 0)):
            for ic in range(HCG):
                hc = g * HCG + ic
                slot = ic % 2
                P.wait("gpsimd", B.s_mm, B.g.get(("gu_slab", slot), 0))
                P.dma("gpsimd", sb3(B.wslab[:, slot * 4096: slot * 4096 + 2048], KC),
                      chunked(wqgT[:, hc * 128:(hc + 1) * 128]), B.s_w[slot])
                wt = P.dma("gpsimd",
                           sb3(B.wslab[:, slot * 4096 + 2048: slot * 4096 + 4096], KC),
                           chunked(wquT[:, hc * 128:(hc + 1) * 128]), B.s_w[slot])
                P.wait("tensor", B.s_w[slot], wt)
                if ic == 0:
                    P.wait("tensor", B.s_mm, inter_free)
                for th in range(2):
                    gbk = (ic * 2 + th) % 2
                    pg = bank(B, 4 + gbk)
                    pu = bank(B, 2 + gbk)
                    P.wait("tensor", B.s_act, B.g.get(("gact", gbk), 0))
                    P.wait("tensor", B.s_dve, B.g.get(("uact", gbk), 0))
                    for which, base, pm in (("g", 0, pg), ("u", 2048, pu)):
                        for k in range(KC):
                            lw = B.wslab[:, slot * 4096 + base + k * 128:
                                         slot * 4096 + base + (k + 1) * 128]
                            P.op("tensor", lambda e, p=pm, l=lw, r=xn_put(k, th),
                                 st=(k == 0): e.matmul(p, l, r, start=st, stop=False))
                        P.op("tensor", lambda e, p=pm, th=th: e.matmul(
                            p, B.m128[:, :], B.sx_bf[0:1, th * 512:(th + 1) * 512],
                            start=False, stop=True),
                            B.s_mm)
                    gu_mm = B.s_mm.n
                    sg = B.scr_a[:, th * 512:(th + 1) * 512]
                    P.wait("scalar", B.s_mm, gu_mm)
                    P.wait("scalar", B.s_dve, B.g.get(("sg", th), 0))
                    P.op("scalar", lambda e, d=sg, p=pg, hc=hc: e.activation(
                        d, p, AF.Silu, scale=B.sgc[:, hc:hc + 1]), B.s_act)
                    B.g[("gact", gbk)] = B.s_act.n
                    it = B.inter[:, ic * TC + th * 512: ic * TC + (th + 1) * 512]
                    P.wait("vector", B.s_mm, gu_mm)
                    P.wait("vector", B.s_act, B.g[("gact", gbk)])
                    P.op("vector", lambda e, d=it, s=sg, p=pu, hc=hc:
                         e.scalar_tensor_tensor(d, p, B.suc[:, hc:hc + 1], s,
                                                ALU.mult, ALU.mult), B.s_dve)
                    B.g[("uact", gbk)] = B.s_dve.n
                    B.g[("sg", th)] = B.s_dve.n
                B.g[("gu_slab", slot)] = B.s_mm.n
            inter_done = B.s_dve.n
            P.wait("tensor", B.s_dve, inter_done)
            for th in range(2):
                P.wait("tensor", B.s_dve, B.s_dve.n)
                for ic in range(HCG):
                    it = B.inter[:, ic * TC + th * 512: ic * TC + (th + 1) * 512]
                    P.op("tensor", lambda e, r=it, st=(ic == 0), sp=(ic == HCG - 1):
                         e.matmul(bank(B, 1, parts=1), B.ones_col[:, :], r,
                                  start=st, stop=sp),
                         B.s_mm if ic == HCG - 1 else None)
                P.wait("vector", B.s_mm, B.s_mm.n)
                P.op("vector", lambda e, th=th: e.tensor_copy(
                    B.si_bf[0:1, th * 512:(th + 1) * 512], bank(B, 1, parts=1)),
                    B.s_dve)
            si_done = B.s_dve.n
            P.wait("tensor", B.s_dve, si_done)
            for o in range(KC if D_TRUNC >= 3 else 0):
                dslot = o % 2
                P.wait("gpsimd", B.s_mm, B.g.get(("d_slab", dslot), 0))
                dt_ = P.dma("gpsimd",
                            sb3(B.dslab[:, dslot * HCG * 128:(dslot + 1) * HCG * 128],
                                HCG),
                            chunked(wqdT[g * HCG * 128:(g + 1) * HCG * 128,
                                         o * 128:(o + 1) * 128]), s_wd[dslot])
                P.wait("tensor", s_wd[dslot], dt_)
                for th in range(2):
                    dbk = (o * 2 + th) % 2
                    pd = bank(B, 6 + dbk)
                    P.wait("tensor", B.s_dve, B.g.get(("dacc", dbk), 0))
                    for ic in range(HCG):
                        ld = B.dslab[:, dslot * HCG * 128 + ic * 128:
                                     dslot * HCG * 128 + (ic + 1) * 128]
                        rh = B.inter[:, ic * TC + th * 512: ic * TC + (th + 1) * 512]
                        P.op("tensor", lambda e, p=pd, l=ld, r=rh, st=(ic == 0):
                             e.matmul(p, l, r, start=st, stop=False))
                    P.op("tensor", lambda e, p=pd, th=th: e.matmul(
                        p, B.m128[:, :], B.si_bf[0:1, th * 512:(th + 1) * 512],
                        start=False, stop=True), B.s_mm)
                    dmm = B.s_mm.n
                    a_ap = B.acc[:, o * TC + th * 512: o * TC + (th + 1) * 512]
                    P.wait("vector", B.s_mm, dmm)
                    if g == 0:
                        P.op("vector", lambda e, a=a_ap, p=pd, o=o: e.tensor_scalar(
                            a, p, B.sdc[:, o:o + 1], None, ALU.mult), B.s_dve)
                    else:
                        P.op("vector", lambda e, a=a_ap, p=pd, o=o:
                             e.scalar_tensor_tensor(a, p, B.sdc[:, o:o + 1], a,
                                                    ALU.mult, ALU.add), B.s_dve)
                    B.g[("dacc", dbk)] = B.s_dve.n
                B.g[("d_slab", dslot)] = B.s_mm.n
            inter_free = B.s_mm.n

        acc_done = B.s_dve.n
        P.wait("vector", B.s_dve, acc_done)
        P.wait("scalar", B.s_dve, acc_done)
        P.wait("tensor", B.s_dve, acc_done)

        def acc_get(k, th):
            return B.acc[:, k * TC + th * 512: k * TC + (th + 1) * 512]

        res_done = 0
        for th in range(2):
            emit_ln_stats(P, B, lambda k, th=th: acc_get(k, th), th, KC,
                          inv_d, B.m_b[:, th * 512:(th + 1) * 512],
                          B.r_b[:, th * 512:(th + 1) * 512], eps_ln,
                          cast_first=True)
            for k in range(KC):
                a_ap = acc_get(k, th)
                m_ap = B.m_b[:, th * 512:(th + 1) * 512]
                r_ap = B.r_b[:, th * 512:(th + 1) * 512]
                ap, slot = xst.get(k, th)
                P.op("vector", lambda e, a=a_ap, m=m_ap: e.tensor_sub(B.t1[:, :], a, m))
                P.op("vector", lambda e, r=r_ap: e.tensor_mul(B.t1[:, :], B.t1[:, :], r))
                t = P.op("vector", lambda e, a=a_ap, x=ap:
                         e.tensor_add(a, x, B.t1[:, :]), B.s_dve)
                xst.mark(slot, t)
            res_done = B.s_dve.n

        P.wait("sync", B.s_dve, res_done)
        P.dma("sync", chunked(xout), sb3(B.acc[:, :], KC), B.s_st)
        P.wait("sync", B.s_st, B.s_st.n)
    es2.close()


def _u8(a):
    return np.ascontiguousarray(a.view(np.uint8).reshape(*a.shape, 4)[..., 0])


def prep_l1_ffn(inp):
    wqgT = np.ascontiguousarray(_u8(inp["L1_ffn_wq_gate"]).T)
    wquT = np.ascontiguousarray(_u8(inp["L1_ffn_wq_up"]).T)
    wqdT = np.ascontiguousarray(_u8(inp["L1_ffn_wq_down"]).T)
    sg_c = np.ascontiguousarray(inp["L1_ffn_s_gate"].reshape(HC, 128).T)
    su_c = np.ascontiguousarray(inp["L1_ffn_s_up"].reshape(HC, 128).T)
    sd_c = np.ascontiguousarray(inp["L1_ffn_s_down"].reshape(KC, 128).T)
    shared = {"wqgT": wqgT, "wquT": wquT, "wqdT": wqdT,
              "sg_c": sg_c, "su_c": su_c, "sd_c": sd_c, "cv": CV_VALS}
    return [dict(shared) for _ in range(NCORES)]



def _shard_x(x):
    xs = x.reshape(NCORES, TC, D)
    return [np.ascontiguousarray(xs[c].T) for c in range(NCORES)]


N_CTRL = 130156


def _spline_grid():
    n = INTER * D
    t = np.linspace(0.0, N_CTRL - 1.0, n, dtype=np.float64)
    i = np.clip(np.floor(t).astype(np.int64), 0, N_CTRL - 2)
    f = (t - i).astype(np.float32)
    idx = [np.clip(i + k, 0, N_CTRL - 1).astype(np.int32) for k in (-1, 0, 1, 2)]
    f64 = f.astype(np.float64)
    f2, f3 = f64 * f64, f64 * f64 * f64
    c0 = (0.5 * (-f64 + 2.0 * f2 - f3)).astype(np.float32)
    c1 = (0.5 * (2.0 - 5.0 * f2 + 3.0 * f3)).astype(np.float32)
    c2 = (0.5 * (f64 + 4.0 * f2 - 3.0 * f3)).astype(np.float32)
    c3 = (0.5 * (f3 - f2)).astype(np.float32)
    return idx, (c0, c1, c2, c3)


def _spline_shards(inp):
    idx, cs = _spline_grid()
    cp_g = inp["L0_ffn_gate_cp"]
    cp_u = inp["L0_ffn_up_cp"]
    cp_d = inp["L0_ffn_down_cp"]
    shards = [dict() for _ in range(NCORES)]
    for k in range(4):
        ik = idx[k]
        for name, cp in (("p_g", cp_g), ("p_u", cp_u)):
            G = np.take(cp, ik).reshape(NCORES, TC, D)
            for c in range(NCORES):
                shards[c][f"{name}{k}"] = np.ascontiguousarray(G[c].T)
        C = cs[k].reshape(NCORES, TC, D)
        for c in range(NCORES):
            shards[c][f"c_gu{k}"] = np.ascontiguousarray(C[c].T)
        Gd = np.take(cp_d, ik).reshape(D, INTER)
        Cd = cs[k].reshape(D, INTER)
        for c in range(NCORES):
            shards[c][f"p_d{k}"] = np.ascontiguousarray(Gd[:, c * TC:(c + 1) * TC].T)
            shards[c][f"c_cd{k}"] = np.ascontiguousarray(Cd[:, c * TC:(c + 1) * TC].T)
    return shards


def prep_l0_ffn(inp):
    shards = _spline_shards(inp)
    for m in shards:
        m["cv"] = CV_VALS
    return shards


def prep_l0_attn(inp):
    shared = {
        "wqT": np.ascontiguousarray(inp["L0_attn_Wq"].T),
        "wkT": np.ascontiguousarray(inp["L0_attn_Wk"].T),
        "wvT": np.ascontiguousarray(inp["L0_attn_Wv"].T),
        "woT": np.ascontiguousarray(inp["L0_attn_Wo"].T),
        "bq_c": np.ascontiguousarray(inp["L0_attn_bq"].reshape(KC, 128).T),
        "bo_c": np.ascontiguousarray(inp["L0_attn_bo"].reshape(KC, 128).T),
        "bk_r": inp["L0_attn_bk"].reshape(1, D),
        "bv_r": inp["L0_attn_bv"].reshape(1, D),
        "cv": CV_VALS,
    }
    return [dict(shared) for _ in range(NCORES)]



def phase_barrier(P, B):
    s = B.s_bar
    for eng in P.ENGINES:
        P.op(eng, lambda e: e.nop(), s)
    tgt = s.n
    for eng in P.ENGINES:
        P.wait(eng, s, tgt)
    B.g = {}


def declare_drams(nc):
    dr = {}

    def inp(name, shape, dt=F32):
        dr[name] = nc.dram_tensor(name, shape, dt, kind="ExternalInput").ap()

    def internal(name, shape, dt=F32):
        dr[name] = nc.dram_tensor(name, shape, dt).ap()

    inp("xin", [D, TC])
    inp("cv", [1, NCONST])
    for n in ("wqT", "wkT", "wvT", "woT"):
        inp(n, [D, D])
    inp("bq_c", [128, KC]); inp("bo_c", [128, KC])
    inp("bk_r", [1, D]); inp("bv_r", [1, D])
    internal("kv_cc_in", [128, D]); internal("kv_cc_out", [128, D])
    for w, shape in (("g", [D, TC]), ("u", [D, TC]), ("d", [TC, D])):
        for k in range(4):
            inp(f"p_{w}{k}", shape)
    for w, shape in (("gu", [D, TC]), ("cd", [TC, D])):
        for k in range(4):
            inp(f"c_{w}{k}", shape)
    internal("aggu_in", [2 * D, TC], BF16)
    internal("aggu_out", [NCORES * 2 * D, TC], BF16)
    internal("agd_in", [TC, D], BF16)
    internal("agd_out", [T, D], BF16)
    inp("wq_a", [D, QR]); inp("qnw", [128, QR // 128]); inp("wq_b", [QR, D])
    inp("wkv_a", [D, KVR + ROPE]); inp("kvnw", [128, KVR // 128])
    inp("wkvb_n", [KVR, NH * 64]); inp("wkvb_v", [KVR, NH * VD])
    inp("wo", [D, D]); inp("cos_in", [32, TC]); inp("sin_in", [32, TC])
    inp("cv128", [128, NCONST])
    internal("agk_in", [NH * 128, TC], BF16)
    internal("agk_out", [NCORES * NH * 128, TC], BF16)
    internal("agv_in", [TC, NH * VD], BF16)
    internal("agv_out", [T, NH * VD], BF16)
    inp("wqgT", [D, HID], U8); inp("wquT", [D, HID], U8); inp("wqdT", [HID, D], U8)
    inp("sg_c", [128, HC]); inp("su_c", [128, HC]); inp("sd_c", [128, KC])
    internal("xmid1", [D, TC]); internal("xmid2", [D, TC]); internal("xmid3", [D, TC])
    dr["xout"] = nc.dram_tensor("xout", [D, TC], F32, kind="ExternalOutput").ap()
    return dr


def build_fused():
    nc = bass.Bass("TRN2", target_bir_lowering=False)
    es = contextlib.ExitStack()
    P = Plan(nc, es)
    dr = declare_drams(nc)
    with es:
        class B:
            pass
        make_common(P, nc, es, B)
        B.s_bar = P.sem("s_bar")
        spl = {k: dr[k] for k in dr if k.startswith(("p_g", "p_u", "p_d", "c_gu", "c_cd"))}
        es_sp = contextlib.ExitStack()
        B.spline = Spline(P, B, nc, es_sp, spl, dr["aggu_in"], dr["agd_in"])
        B.spline.aggu_out_ap = dr["aggu_out"]
        B.spline.agd_out_ap = dr["agd_out"]
        B.spline.s_cc1 = P.sem("s_ccB")
        B.spline.s_cc2 = P.sem("s_ccB2")
        B.sp_hook = B.spline.hook
        emit_l0_attn(P, B, nc, dr["xin"], dr["wqT"], dr["wkT"], dr["wvT"],
                     dr["woT"], dr["bq_c"], dr["bo_c"], dr["bk_r"], dr["bv_r"],
                     dr["cv"], dr["xmid1"], dr["kv_cc_in"], dr["kv_cc_out"])
        B.spline.finish()
        s_ccB, s_ccB2 = B.spline.s_cc1, B.spline.s_cc2
        es_sp.close()
        phase_barrier(P, B)
        emit_l0_ffn(P, B, nc, dr["xmid1"], dr["cv"], spl, dr["xmid2"],
                    dr["aggu_in"], dr["aggu_out"], dr["agd_in"], dr["agd_out"],
                    cc_waits=[(s_ccB, s_ccB.n)], down_wait=(s_ccB2, s_ccB2.n))
        phase_barrier(P, B)
        emit_l1_attn(P, B, nc, dr["xmid2"], dr["wq_a"], dr["qnw"], dr["wq_b"],
                     dr["wkv_a"], dr["kvnw"], dr["wkvb_n"], dr["wkvb_v"],
                     dr["wo"], dr["cos_in"], dr["sin_in"], dr["cv"],
                     dr["cv128"], dr["xmid3"],
                     dr["agk_in"], dr["agk_out"], dr["agv_in"], dr["agv_out"])
        phase_barrier(P, B)
        emit_l1_ffn(P, B, nc, dr["xmid3"], dr["wqgT"], dr["wquT"], dr["wqdT"],
                    dr["sg_c"], dr["su_c"], dr["sd_c"], dr["cv"], dr["xout"])
        P.build()
    return nc


def kernel(**inputs):
    x = inputs["L0_attn_x"].astype(np.float32, copy=False)
    xs = _shard_x(x)
    pa = prep_l0_attn(inputs)
    pb = prep_l0_ffn(inputs)
    pc = prep_l1_attn(inputs)
    pd = prep_l1_ffn(inputs)
    in_maps = []
    for c in range(NCORES):
        m = {}
        m.update(pa[c]); m.update(pb[c]); m.update(pc[c]); m.update(pd[c])
        m["xin"] = xs[c]
        in_maps.append(m)
    nc = build_fused()
    res = run_bass_kernel_spmd(nc, in_maps, core_ids=list(range(NCORES)))
    out = np.concatenate([r["xout"].T[None] for r in res.results], axis=1)
    return np.ascontiguousarray(out.reshape(1, T, D))
